# version 30
# baseline (speedup 1.0000x reference)
import math

import numpy as np

import concourse.bass as bass
import concourse.mybir as mybir
from concourse.bass_utils import run_bass_kernel_spmd

F32 = mybir.dt.float32
BF16 = mybir.dt.bfloat16
U8 = mybir.dt.uint8
I16 = mybir.dt.int16
I32 = mybir.dt.int32
AF = mybir.ActivationFunctionType
ALU = mybir.AluOpType
PI = math.pi
INV2PI = 1.0 / (2.0 * PI)
TWOPI = 2.0 * PI
TP_HI = float(np.float32(2 * np.pi))
TP_LO = float(np.float32(2 * np.pi - np.float64(np.float32(2 * np.pi))))
HP_HI = float(np.float32(np.pi / 2))
HP_LO = float(np.float32(np.pi / 2 - np.float64(np.float32(np.pi / 2))))

NCORE = 8
S, D = 16384, 1024
TOK = S // NCORE
NG = TOK // 128
TA = 256
NTA = TOK // TA
TF = 512
NTF = TOK // TF
DF = 4096

SBUF_BASE = 16512
SBUF_TOP = 229344
ENGS = ("pe", "act", "dve", "pool", "sp")
DT_SZ = {F32: 4, BF16: 2, U8: 1, I16: 2, I32: 4}


class Prog:

    def __init__(self, nc):
        self.nc = nc
        self.ops = {e: [] for e in ENGS}
        self.esem = {e: nc.alloc_semaphore(f"sem_{e}") for e in ENGS}
        self.ecnt = {e: 0 for e in ENGS}
        self.water = {e: {} for e in ENGS}
        self.bufs = {}
        self.dsem = {}
        self.sems = dict(self.esem)

    def reg(self, h, lo=None, size=None):
        self.bufs[id(h)] = dict(h=h, iv=(lo, lo + size) if lo is not None else None,
                                w={}, r=[])
        return h

    def sballoc(self, name, shape, dtype, off):
        per = int(np.prod(shape[1:])) * DT_SZ[dtype]
        assert SBUF_BASE + off + per <= SBUF_TOP, (name, off, per)
        h = self.nc.alloc_sbuf_tensor_at(name, list(shape), dtype,
                                         offset=SBUF_BASE + off)
        return self.reg(h, off, per)

    def _overlaps(self, b):
        me = self.bufs[id(b)]
        if me["iv"] is None:
            return [me]
        lo, hi = me["iv"]
        return [m for m in self.bufs.values()
                if m["iv"] is not None and m["iv"][0] < hi and lo < m["iv"][1]]

    def _need(self, eng, mark):
        if mark is None:
            return None
        k, v = mark
        if k == eng == "pe":
            return None
        if self.water[eng].get(k, 0) >= v:
            return None
        return mark

    def _collect(self, eng, reads, writes):
        need = {}

        def add(mark):
            m = self._need(eng, mark)
            if m:
                need[m[0]] = max(need.get(m[0], 0), m[1])

        for b in reads:
            if id(b) in self.bufs:
                for m in self._overlaps(b):
                    for kv in m["w"].items():
                        add(kv)
        for b in writes:
            if id(b) in self.bufs:
                for m in self._overlaps(b):
                    for kv in m["w"].items():
                        add(kv)
                    for rr in m["r"]:
                        add(rr)
        return need

    def _emit_waits(self, eng, need):
        for k, v in sorted(need.items(), key=lambda kv: str(kv[0])):
            self.water[eng][k] = v
            sem = self.sems[k]
            self.ops[eng].append(lambda e, s=sem, vv=v: e.wait_ge(s, vv))

    def _mark(self, reads, writes, semkey, val):
        for b in writes:
            if id(b) not in self.bufs:
                self.reg(b)
            m = self.bufs[id(b)]
            m["w"][semkey] = val
            m["r"] = [r for r in m["r"] if r[0] != semkey or r[1] > val]
        for b in reads:
            if id(b) in self.bufs:
                self.bufs[id(b)]["r"].append((semkey, val))

    def emit(self, eng, fns, reads=(), writes=()):
        if not isinstance(fns, (list, tuple)):
            fns = [fns]
        need = self._collect(eng, reads, writes)
        self._emit_waits(eng, need)
        sem = self.esem[eng]
        for f in fns[:-1]:
            self.ops[eng].append(f)
        self.ecnt[eng] += 1
        val = self.ecnt[eng]
        self.ops[eng].append(lambda e, f=fns[-1], s=sem: f(e).then_inc(s, 1))
        self._mark(reads, writes, eng, val)
        return (eng, val)

    def dma(self, queue, dst, src, dst_buf, src_bufs=(), name="d", after=(),
            **kw):
        key = id(dst_buf) if dst_buf is not None else ("anon", len(self.dsem))
        if key not in self.dsem:
            h = self.nc.alloc_semaphore(f"sd_{name}_{len(self.dsem)}")
            self.dsem[key] = [h, 0]
            self.sems[("d", key)] = h
        ent = self.dsem[key]
        writes = [dst_buf] if dst_buf is not None else []
        need = self._collect(queue, list(src_bufs) + list(after), writes)
        m = self._need(queue, (("d", key), ent[1])) if ent[1] > 0 else None
        if m:
            need[m[0]] = max(need.get(m[0], 0), m[1])
        self._emit_waits(queue, need)
        ent[1] += 16
        total, h = ent[1], ent[0]
        self.ops[queue].append(
            lambda e, d=dst, s=src, hh=h, kws=kw: e.dma_start(d, s, **kws).then_inc(hh, 16))
        self._mark(list(src_bufs), writes, ("d", key), total)
        return (("d", key), total)

    def wait_all(self, engs=ENGS):
        marks = [(e, self.ecnt[e]) for e in ENGS if self.ecnt[e] > 0]
        marks += [(("d", k), v[1]) for k, v in self.dsem.items() if v[1] > 0]
        for eng in engs:
            need = {}
            for k, v in marks:
                m = self._need(eng, (k, v))
                if m:
                    need[m[0]] = max(need.get(m[0], 0), m[1])
            self._emit_waits(eng, need)

    def play(self):
        with self.nc.Block() as block:
            block.tensor(lambda e: [f(e) for f in self.ops["pe"]])
            block.scalar(lambda e: [f(e) for f in self.ops["act"]])
            block.vector(lambda e: [f(e) for f in self.ops["dve"]])
            block.gpsimd(lambda e: [f(e) for f in self.ops["pool"]])
            block.sync(lambda e: [f(e) for f in self.ops["sp"]])


def build(debug=False, stop_after=None):
    nc = bass.Bass("TRN2", target_bir_lowering=False)
    P = Prog(nc)

    def din(name, shape, dtype):
        return nc.dram_tensor(name, list(shape), dtype, kind="ExternalInput").ap()

    x_fm = din("x_fm", [D, TOK], F32)
    x_tm = din("x_tm", [TOK, D], F32)
    w_in = {k: din(f"w_{k}", [128, 8192], F32)
            for k in ("k", "kp", "v", "q", "qp", "o")}
    k0t_d = din("k0t", [128, 8 * DF], U8)
    k0b_d = din("k0b", [128, 8 * DF], U8)
    k1_d = din("k1t", [16, 32, 128, 256], U8)
    k2_d = din("k2t", [4, 32, 128, 256], U8)
    brow_in = {k: din(f"b_{k}", [1, D], F32)
               for k in ("v", "kp", "qp")}
    wkT_d = din("wkT", [128, 8192], F32)
    wqT_d = din("wqT", [128, 8192], F32)
    bkc_d = din("bkc", [128, 8], F32)
    bqc_d = din("bqc", [128, 8], F32)
    bo_d = din("b_o", [1, D], F32)
    ps_d = din("ps_r", [1, D], F32)
    lng_d = din("lng_c", [128, 8], F32)
    lnb_d = din("lnb_c", [128, 8], F32)
    s_d = {k: din(f"s{k}_c", [128, 1], F32) for k in (0, 1, 2)}
    u128_d = din("u128_c", [128, 128], BF16)
    ident_d = din("ident_c", [128, 128], F32)

    out_d = nc.dram_tensor("out_fm", [D, TOK], F32, kind="ExternalOutput").ap()
    dbg = {}
    if debug:
        for nm, shp in (("xn", [128, 8 * TOK]), ("ret", [128, NG * D]),
                        ("h1", [128, 32 * TF]), ("h2", [128, 32 * TF])):
            dbg[nm] = nc.dram_tensor("dbg_" + nm, shp, BF16,
                                     kind="ExternalOutput").ap()
        dbg["acc"] = nc.dram_tensor("dbg_acc", [D, TOK], F32,
                                    kind="ExternalOutput").ap()
        for nm in ("br", "bi", "sq", "cosq"):
            dbg[nm] = nc.dram_tensor("dbg_" + nm, [128, 1024], BF16,
                                     kind="ExternalOutput").ap()
        for nm in ("memr", "memi"):
            dbg[nm] = nc.dram_tensor("dbg_" + nm, [128, 1024], BF16,
                                     kind="ExternalOutput").ap()
        dbg["u128"] = nc.dram_tensor("dbg_u128", [128, 128], BF16,
                                     kind="ExternalOutput").ap()

    def dint(name, shape, dtype):
        return nc.dram_tensor(name, list(shape), dtype, kind="Internal").ap()

    k1bf = dint("k1bf", [16, 32, 128, 256], BF16)
    k2bf = dint("k2bf", [4, 32, 128, 256], BF16)
    k0dbf = dint("k0dbf", [16, 8, 128, 256], BF16)
    acc_d = dint("acc_d", [D, TOK], F32)
    k1bf_h = P.reg(object())
    k2bf_h = P.reg(object())
    k0dbf_h = P.reg(object())
    acc_h = P.reg(object())

    O_XN = 0
    O_W = {k: 32768 + i * 16384 for i, k in enumerate(("k", "kp", "v", "q", "qp"))}
    O_WO = 114688
    O_RET = 131072
    O_ROWS = 163840
    O_ROWSF = 176128
    E_END = 180224
    O_ONES = 180224
    O_PSD = 180736
    O_U128 = 184832
    O_IDENT = 185152
    O_COLS = 185664
    O_LNB = 186688
    G0 = 186816
    GTOP = SBUF_TOP - SBUF_BASE
    assert G0 + 24576 <= GTOP, (G0, GTOP)

    sb = P.sballoc
    xn_sb = sb("xn_sb", [128, 8 * TOK], BF16, O_XN)
    wsb = {k: sb(f"w{k}_sb", [128, 8192], BF16, O_W[k]) for k in O_W}
    wsb["o"] = sb("wo_sb", [128, 8192], BF16, O_WO)
    ret_sb = sb("ret_sb", [128, NG * D], BF16, O_RET)
    rows = sb("rows_sb", [128, 6144], BF16, O_ROWS)
    rowsf = sb("rowsf_sb", [128, 1024], F32, O_ROWSF)
    ones = sb("ones_sb", [128, 256], BF16, O_ONES)
    psd = sb("psd_sb", [128, 1024], F32, O_PSD)
    u128 = sb("u128_sb", [128, 128], BF16, O_U128)
    ident = sb("ident_sb", [128, 128], F32, O_IDENT)
    cols = sb("cols_sb", [128, 256], F32, O_COLS)
    lnbc = sb("lnbc_sb", [128, 8], BF16, O_LNB)
    bkcs = sb("bkc_sb", [128, 8], BF16, O_LNB + 32)
    bqcs = sb("bqc_sb", [128, 8], BF16, O_LNB + 64)

    RP = {"bkp": (0, 0), "bqp": (0, 1024), "mh0": (0, 2048),
          "bv": (0, 3072),
          "s1hi": (0, 3072), "s1lo": (0, 3584),
          "s2hi": (0, 4096), "s2lo": (0, 4608),
          "bwo": (0, 5120)}

    def row(nm, n=1024, c0=0):
        p, c = RP[nm]
        return rows[p:p + 1, c + c0:c + c0 + n]

    RF = {"s1": (0, 0, 512), "s2": (0, 512, 512), "scr": (32, 0, 512)}

    def rowf(nm):
        p, c, n = RF[nm]
        return rowsf[p:p + 1, c:c + n]

    CP = dict(sum1=0, sq1=16, mu1=32, rstd1=48, sum2=64, sq2=80, mu2=96,
              rstd2=112, scrA=128, scrB=144, lng=160, singles=176,
              sqa=192, sqb=208)
    SGL = dict(two_s0=176, four_s1=177, four_s2=178, hpi=179)

    def colg(nm, g):
        return cols[:, CP[nm] + g:CP[nm] + g + 1]

    def colr(nm, n=16):
        return cols[:, CP[nm]:CP[nm] + n]

    def sgl(nm):
        return cols[:, SGL[nm]:SGL[nm] + 1]

    PS = [P.reg(nc.alloc_psum_tensor(f"ps{i}", [128, 1024], F32))
          for i in range(4)]

    xn_v = xn_sb[:].rearrange("p (c t) -> p c t", c=8)
    xng = [P.reg(object(), lo=O_XN + g * 256, size=256) for g in range(NG)]
    x_tmf = sb("xtm_sb", [128, NG * D], F32, O_WO)
    xtm_v = x_tmf[:].rearrange("p (g f) -> p g f", g=NG)
    XSLOT = [NG - 1 - g for g in range(NG)]
    xtmr = [P.reg(object(), lo=O_WO + XSLOT[g] * 4096, size=4096)
            for g in range(NG)]
    x_fm_v = x_fm.rearrange("(c p) t -> p c t", p=128)
    acc_v = acc_d.rearrange("(c p) t -> p c t", p=128)
    out_v = out_d.rearrange("(c p) t -> p c t", p=128)
    ret_v = ret_sb[:].rearrange("p (g f) -> p g f", g=NG)
    RSLOT = [NG - 1 - g for g in range(NG)]
    retr = [P.reg(object(), lo=O_RET + sl * 2048, size=2048) for sl in range(NG)]

    g_off = G0

    def ga(name, shape, dtype, off=None):
        nonlocal g_off
        if off is None:
            off = g_off
            per = int(np.prod(shape[1:])) * DT_SZ[dtype]
            g_off += (per + 63) & ~63
        assert off + int(np.prod(shape[1:])) * DT_SZ[dtype] <= GTOP
        return sb(name, shape, dtype, off)

    scs = {k: ga(f"sc{k}", [128, 1], F32) for k in (0, 1, 2)}
    p0r = ga("p0r", [128, 1024], F32)
    st_a = sb("st_a", [128, DF], BF16, O_WO)
    st_b = sb("st_b", [128, DF], BF16, O_WO + 8192)
    st_dd = sb("st_dd", [128, DF], BF16, O_WO + 16384)
    psc_a = sb("psc_a", [128, 2048], BF16, O_WO + 24576)
    psc_f = sb("psc_f", [128, 2048], F32, O_WO + 28672)

    P.dma("sp", u128[:], u128_d, u128, name="u128")
    P.dma("sp", ident[:], ident_d, ident, name="ident")
    P.emit("dve", lambda e: e.memset(ones[:], 1.0), writes=[ones])
    P.emit("dve", lambda e: e.memset(sgl("hpi"), PI / 2.0), writes=[cols])

    P.dma("pool", lnbc[:], lnb_d, lnbc, name="lnb")
    P.dma("pool", bkcs[:], bkc_d, bkcs, name="bkc")
    P.dma("pool", bqcs[:], bqc_d, bqcs, name="bqc")
    stK = sb("stK", [128, 8192], BF16, O_WO)
    for wtd, wpname, dstname in ((wkT_d, "kp", "k"), (wqT_d, "qp", "q")):
        P.dma("pool", wsb[wpname][:], w_in[wpname], wsb[wpname],
              name=f"w{wpname}", max_dma_last_dim=8192)
        P.dma("pool", stK[:], wtd, stK, name="wT", max_dma_last_dim=8192)
        for dc in range(8):
            for h in range(2):
                ps = PS[(dc * 2 + h) % 4]
                fns = [
                    (lambda e, kfc=kfc, dc=dc, h=h, ps=ps, wp=wpname:
                     e.matmul(ps[:, 0:512],
                              lhsT=stK[:, kfc * 1024 + dc * 128:kfc * 1024 + dc * 128 + 128],
                              rhs=wsb[wp][:, kfc * 1024 + h * 512:kfc * 1024 + h * 512 + 512],
                              start=(kfc == 0), stop=(kfc == 7)))
                    for kfc in range(8)]
                P.emit("pe", fns, reads=[stK, wsb[wpname]], writes=[ps])
                P.emit("act", lambda e, dc=dc, h=h, ps=ps, dn=dstname: e.activation(
                    wsb[dn][:, dc * 1024 + h * 512:dc * 1024 + h * 512 + 512],
                    ps[:, 0:512], AF.Identity), reads=[ps], writes=[wsb[dstname]])
    for k in (0, 1, 2):
        P.dma("sp", scs[k][:], s_d[k], scs[k], name=f"s{k}")
    P.dma("sp", colr("lng", 8), lng_d, cols, name="lng")
    P.emit("dve", lambda e: e.tensor_scalar(sgl("two_s0"), scs[0][:], 2.0, None, op0=ALU.mult),
           reads=[scs[0]], writes=[cols])
    P.emit("dve", lambda e: e.tensor_scalar(sgl("four_s1"), scs[1][:], 4.0, None, op0=ALU.mult),
           reads=[scs[1]], writes=[cols])
    P.emit("dve", lambda e: e.tensor_scalar(sgl("four_s2"), scs[2][:], 4.0, None, op0=ALU.mult),
           reads=[scs[2]], writes=[cols])

    P.dma("pool", wsb["v"][:], w_in["v"], wsb["v"], name="wv",
          max_dma_last_dim=8192)
    P.dma("sp", psc_f[0:1, 0:1024], ps_d, psc_f, name="psr")
    P.emit("dve", lambda e: e.tensor_copy(psc_a[0:1, 0:1024], psc_f[0:1, 0:1024]),
           reads=[psc_f], writes=[psc_a])
    P.emit("dve", lambda e: e.tensor_tensor(psc_f[0:1, 1024:2048], psc_f[0:1, 0:1024], psc_a[0:1, 0:1024], op=ALU.subtract),
           reads=[psc_f, psc_a], writes=[psc_f])
    P.emit("dve", lambda e: e.tensor_copy(psc_a[0:1, 1024:2048], psc_f[0:1, 1024:2048]),
           reads=[psc_f], writes=[psc_a])
    P.emit("pe", [
        lambda e: e.matmul(PS[0][:, 0:512], lhsT=ones[0:1, 0:128], rhs=psc_a[0:1, 0:512], start=True, stop=False),
        lambda e: e.matmul(PS[0][:, 0:512], lhsT=ones[0:1, 0:128], rhs=psc_a[0:1, 1024:1536], start=False, stop=True),
        lambda e: e.matmul(PS[0][:, 512:1024], lhsT=ones[0:1, 0:128], rhs=psc_a[0:1, 512:1024], start=True, stop=False),
        lambda e: e.matmul(PS[0][:, 512:1024], lhsT=ones[0:1, 0:128], rhs=psc_a[0:1, 1536:2048], start=False, stop=True),
    ], reads=[ones, psc_a], writes=[PS[0]])
    P.emit("dve", lambda e: e.tensor_scalar(psd[:], PS[0][:], 0.5, None, op0=ALU.mult),
           reads=[PS[0]], writes=[psd])

    def bias_builds():
        P.dma("sp", p0r[0:1, 0:1024], brow_in["v"], p0r, name="brow")
        P.emit("dve", lambda e: e.tensor_copy(row("bv"), p0r[0:1, 0:1024]),
               reads=[p0r], writes=[rows])
        for bnm, bc, wpname, slot in (("kp", bkcs, "kp", "bkp"),
                                      ("qp", bqcs, "qp", "bqp")):
            P.dma("sp", p0r[0:1, 0:1024], brow_in[bnm], p0r, name="brow")
            fns = []
            for h in range(2):
                for kfc in range(8):
                    fns.append(lambda e, kfc=kfc, h=h, bc=bc, wp=wpname: e.matmul(
                        PS[3][0:1, h * 512:h * 512 + 512],
                        lhsT=bc[:, kfc:kfc + 1],
                        rhs=wsb[wp][:, kfc * 1024 + h * 512:kfc * 1024 + h * 512 + 512],
                        start=(kfc == 0), stop=(kfc == 7)))
            P.emit("pe", fns, reads=[bc, wsb[wpname]], writes=[PS[3]])
            P.emit("dve", lambda e, slot=slot: e.tensor_tensor(
                row(slot), PS[3][0:1, 0:1024], p0r[0:1, 0:1024], op=ALU.add),
                reads=[PS[3], p0r], writes=[rows])
        P.emit("dve", lambda e: e.memset(row("mh0", 128), -0.5), writes=[rows])

    a0_scr = [ga("a0s0", [128, 1024], BF16), ga("a0s1", [128, 1024], BF16)]
    a0_xn = ga("a0xn", [128, 1024], F32)
    for g in range(NG):
        P.dma("sp", xtm_v[:, XSLOT[g], :], x_tm[g * 128:(g + 1) * 128, :],
              xtmr[g], name=f"xt{g}", after=(stK,) if g == 0 else ())
    for g in range(NG):
        sc = a0_scr[g % 2]
        P.emit("dve", lambda e, g=g, s=sc: e.tensor_scalar(
            s[:], xtm_v[:, XSLOT[g], :], 1.0, 0.0, op0=ALU.mult, op1=ALU.add,
            accum_out=colg("sum1", g)), reads=[xtmr[g]], writes=[sc, cols])
        P.emit("act", lambda e, g=g, s=sc: e.activation(
            s[:], xtm_v[:, XSLOT[g], :], AF.Square, accum_out=colg("sq1", g)),
            reads=[xtmr[g]], writes=[sc, cols])
    P.emit("dve", lambda e: e.tensor_scalar(colr("mu1"), colr("sum1"), 1.0 / D, None, op0=ALU.mult),
           reads=[cols], writes=[cols])
    P.emit("dve", lambda e: e.tensor_scalar(colr("scrA"), colr("sq1"), 1.0 / D, None, op0=ALU.mult),
           reads=[cols], writes=[cols])
    P.emit("dve", lambda e: e.tensor_tensor(colr("scrB"), colr("mu1"), colr("mu1"), op=ALU.mult),
           reads=[cols], writes=[cols])
    P.emit("dve", lambda e: e.tensor_tensor(colr("scrA"), colr("scrA"), colr("scrB"), op=ALU.subtract),
           reads=[cols], writes=[cols])
    P.emit("dve", lambda e: e.tensor_scalar(colr("scrA"), colr("scrA"), 1.0, 1e-5, op0=ALU.mult, op1=ALU.add),
           reads=[cols], writes=[cols])
    P.emit("act", lambda e: e.activation(colr("scrB"), colr("scrA"), AF.Ln),
           reads=[cols], writes=[cols])
    P.emit("act", lambda e: e.activation(colr("rstd1"), colr("scrB"), AF.Exp, scale=-0.5),
           reads=[cols], writes=[cols])
    for g in range(NG):
        P.emit("dve", lambda e, g=g: e.tensor_scalar(
            a0_xn[:], xtm_v[:, XSLOT[g], :], colg("mu1", g), colg("rstd1", g),
            op0=ALU.subtract, op1=ALU.mult), reads=[xtmr[g], cols], writes=[a0_xn])
        ps = PS[g % 2]
        P.emit("pe", [
            (lambda e, g=g, c=c, ps=ps: e.matmul(
                ps[:, c * 128:(c + 1) * 128],
                lhsT=a0_xn[:, c * 128:(c + 1) * 128], rhs=ident[:],
                is_transpose=True, start=True, stop=True,
                )) for c in range(8)],
            reads=[a0_xn, ident], writes=[ps])
        P.emit("dve", lambda e, g=g, ps=ps: e.tensor_copy(
            xn_v[:, :, g * 128:(g + 1) * 128],
            ps[:].rearrange("p (c t) -> p c t", c=8)),
            reads=[ps], writes=[xng[g]])
        if g == 3:
            bias_builds()
    if debug:
        P.dma("act", dbg["xn"], xn_sb[:], None, src_bufs=[xn_sb], name="dbgxn")

    if stop_after == "a0":
        P.wait_all(); P.play(); return nc
    for kc in range(8):
        P.dma("pool", st_a[:], k0t_d[:, kc * DF:(kc + 1) * DF], st_a,
              name="k0a", max_dma_last_dim=2048)
        P.dma("pool", st_b[:], k0b_d[:, kc * DF:(kc + 1) * DF], st_b,
              name="k0b", max_dma_last_dim=2048)
        P.emit("pool", lambda e: e.tensor_tensor(st_dd[:], st_a[:], st_b[:], op=ALU.subtract),
               reads=[st_a, st_b], writes=[st_dd])
        P.dma("sp", k0dbf[:, kc, :, :].rearrange("g p n -> p g n"),
              st_dd[:].rearrange("p (g n) -> p g n", g=16), k0dbf_h,
              src_bufs=[st_dd], name="k0d")

    if stop_after == "p0":
        P.wait_all(); P.play(); return nc


    P.dma("pool", wsb["o"][:], w_in["o"], wsb["o"], name="wo",
          max_dma_last_dim=8192)
    P.dma("pool", k1bf[0:8], k1_d[0:8], k1bf_h, name="k1bf", max_dma_last_dim=2048)
    P.dma("pool", k1bf[8:16], k1_d[8:16], k1bf_h, name="k1bf", max_dma_last_dim=2048)
    P.dma("pool", k2bf, k2_d, k2bf_h, name="k2bf", max_dma_last_dim=2048)

    ph_ = [ga("ph0", [128, 1024], F32, off=G0),
           ga("ph1", [128, 1024], F32, off=G0 + 4096)]
    wA_ = [ga("wA0", [128, 1024], BF16, off=G0 + 8192),
           ga("wA1", [128, 1024], BF16, off=G0 + 10240)]
    wB_ = [ga("wB0", [128, 1024], BF16, off=G0 + 12288),
           ga("wB1", [128, 1024], BF16, off=G0 + 14336)]
    brb = ga("brb", [128, 1024], BF16, off=G0 + 16384)
    bib = ga("bib", [128, 1024], BF16, off=G0 + 18432)
    cb = ga("cb", [128, 1024], F32, off=G0 + 20480)

    def tmproj_xn(wname, bslot, gg, ps):
        fns = []
        pb, c = RP[bslot]
        for h2 in range(2):
            for kc in range(8):
                fns.append(lambda e, kc=kc, h2=h2, ps=ps, gg=gg, wn=wname:
                           e.matmul(ps[:, h2 * 512:(h2 + 1) * 512],
                                    lhsT=xn_v[:, kc, gg * 128:gg * 128 + 128],
                                    rhs=wsb[wn][:, kc * 1024 + h2 * 512:kc * 1024 + (h2 + 1) * 512],
                                    start=(kc == 0), stop=False))
            fns.append(lambda e, h2=h2, ps=ps, pb=pb, c=c:
                       e.matmul(ps[:, h2 * 512:(h2 + 1) * 512],
                                lhsT=ones[pb:pb + 1, 0:128],
                                rhs=rows[pb:pb + 1, c + h2 * 512:c + (h2 + 1) * 512],
                                start=False, stop=True))
        P.emit("pe", fns, reads=[xn_sb, wsb[wname], rows, ones, xng[gg]],
               writes=[ps])

    def trig_ops(ps_src, par):
        ph, wA, wB = ph_[par], wA_[par], wB_[par]
        return [
            ("act", (lambda e, p=ps_src, ph=ph: e.activation(ph[:], p[:], AF.Tanh)), [ps_src], [ph]),
            ("dve", (lambda e, ph=ph: e.tensor_tensor(ph[:], ph[:], psd[:], op=ALU.mult)), [ph, psd], [ph]),
            ("act", (lambda e, ph=ph, wA=wA: e.activation(wA[:], ph[:], AF.Sin)), [ph], [wA]),
            ("act", (lambda e, ph=ph: e.activation(ph[:], ph[:], AF.Abs)), [ph], [ph]),
            ("dve", (lambda e, ph=ph: e.tensor_scalar(ph[:], ph[:], HP_HI, HP_LO, op0=ALU.subtract, op1=ALU.subtract)), [ph], [ph]),
            ("act", (lambda e, ph=ph, wB=wB: e.activation(wB[:], ph[:], AF.Sin, scale=-1.0)), [ph], [wB]),
            ("act", (lambda e, ph=ph, wA=wA: e.activation(ph[:], wA[:], AF.Square)), [wA], [ph]),
            ("dve", (lambda e, wA=wA, wB=wB: e.scalar_tensor_tensor(wA[:], wA[:], 2.0, wB[:], op0=ALU.mult, op1=ALU.mult)), [wA, wB], [wA]),
            ("dve", (lambda e, ph=ph, wB=wB: e.tensor_scalar(wB[:], ph[:], -2.0, 1.0, op0=ALU.mult, op1=ALU.add)), [ph], [wB]),
        ]

    def a1_A(gg):
        tmproj_xn("v", "bv", gg, PS[0])
        tmproj_xn("k", "bkp", gg, PS[1])
        tmproj_xn("q", "bqp", gg, PS[2])

    def a1_emit(ops):
        for eng, fn, rd, wr in ops:
            P.emit(eng, fn, reads=rd, writes=wr)

    def a1_cumsum(gg, h):
        P.emit("pe", [
            (lambda e, h=h: e.matmul(PS[3][:, 0:512], lhsT=u128[:],
                                     rhs=brb[:, h * 512:(h + 1) * 512],
                                     start=True, stop=True)),
            (lambda e, h=h: e.matmul(PS[3][:, 512:1024], lhsT=u128[:],
                                     rhs=bib[:, h * 512:(h + 1) * 512],
                                     start=True, stop=True)),
        ], reads=[u128, brb, bib], writes=[PS[3]])

    def a1_combine(gg, h):
        P.emit("dve", lambda e, h=h: e.tensor_tensor(
            ph_[0][:, 0:512], PS[3][:, 0:512],
            wB_[1][:, h * 512:(h + 1) * 512], op=ALU.mult),
            reads=[PS[3], wB_[1]], writes=[ph_[0]])
        P.emit("dve", lambda e, h=h: e.tensor_tensor(
            cb[:, 0:512], PS[3][:, 512:1024],
            wA_[1][:, h * 512:(h + 1) * 512], op=ALU.mult),
            reads=[PS[3], wA_[1]], writes=[cb])
        acc_c = "scrA" if h == 0 else "scrB"
        P.emit("dve", lambda e, h=h, gg=gg, ac=acc_c: e.scalar_tensor_tensor(
            ret_v[:, RSLOT[gg], h * 512:(h + 1) * 512], ph_[0][:, 0:512], 1.0,
            cb[:, 0:512], op0=ALU.mult, op1=ALU.add, accum_out=colg(ac, gg)),
            reads=[ph_[0], cb], writes=[retr[RSLOT[gg]], cols])
        sq_c = "sqa" if h == 0 else "sqb"
        P.emit("act", lambda e, h=h, gg=gg, sc=sq_c: e.activation(
            cb[:, 512:1024], ret_v[:, RSLOT[gg], h * 512:(h + 1) * 512], AF.Square,
            accum_out=colg(sc, gg)),
            reads=[retr[RSLOT[gg]]], writes=[cb, cols])

    a1_A(0)
    for gg in range(NG):
        kp_ops = trig_ops(PS[1], 0)
        qp_ops = trig_ops(PS[2], 1)
        for i in range(len(kp_ops)):
            a1_emit([kp_ops[i], qp_ops[i]])
        P.emit("dve", lambda e: e.tensor_tensor(bib[:], PS[0][:], wA_[0][:], op=ALU.mult),
               reads=[PS[0], wA_[0]], writes=[bib])
        P.emit("dve", lambda e: e.tensor_tensor(brb[:], PS[0][:], wB_[0][:], op=ALU.mult),
               reads=[PS[0], wB_[0]], writes=[brb])
        a1_cumsum(gg, 0)
        if gg + 1 < NG:
            tmproj_xn("v", "bv", gg + 1, PS[0])
            tmproj_xn("k", "bkp", gg + 1, PS[1])
        a1_combine(gg, 0)
        a1_cumsum(gg, 1)
        if gg + 1 < NG:
            tmproj_xn("q", "bqp", gg + 1, PS[2])
        a1_combine(gg, 1)
        P.emit("dve", lambda e, gg=gg: e.tensor_tensor(
            colg("sum2", gg), colg("scrA", gg), colg("scrB", gg), op=ALU.add),
            reads=[cols], writes=[cols])
        P.emit("dve", lambda e, gg=gg: e.tensor_tensor(
            colg("sq2", gg), colg("sqa", gg), colg("sqb", gg), op=ALU.add),
            reads=[cols], writes=[cols])
    if debug:
        P.dma("act", dbg["ret"], ret_sb[:], None, src_bufs=[ret_sb], name="dbgret")

    if stop_after == "a1":
        P.wait_all(); P.play(); return nc
    P.emit("dve", lambda e: e.tensor_scalar(colr("mu2"), colr("sum2"), 1.0 / D, None, op0=ALU.mult),
           reads=[cols], writes=[cols])
    P.emit("dve", lambda e: e.tensor_scalar(colr("scrA"), colr("sq2"), 1.0 / (D * D), None, op0=ALU.mult),
           reads=[cols], writes=[cols])
    P.emit("dve", lambda e: e.tensor_tensor(colr("scrB"), colr("mu2"), colr("mu2"), op=ALU.mult),
           reads=[cols], writes=[cols])
    P.emit("dve", lambda e: e.tensor_scalar(colr("scrB"), colr("scrB"), 1.0 / D, None, op0=ALU.mult),
           reads=[cols], writes=[cols])
    P.emit("dve", lambda e: e.tensor_tensor(colr("scrA"), colr("scrA"), colr("scrB"), op=ALU.subtract),
           reads=[cols], writes=[cols])
    P.emit("dve", lambda e: e.tensor_scalar(colr("scrA"), colr("scrA"), 1.0, 1e-5, op0=ALU.mult, op1=ALU.add),
           reads=[cols], writes=[cols])
    P.emit("act", lambda e: e.activation(colr("scrB"), colr("scrA"), AF.Ln),
           reads=[cols], writes=[cols])
    P.emit("act", lambda e: e.activation(colr("scrA"), colr("scrB"), AF.Exp, scale=-0.5),
           reads=[cols], writes=[cols])
    P.emit("dve", lambda e: e.tensor_scalar(colr("rstd2"), colr("scrA"), 1.0 / 32.0, None, op0=ALU.mult),
           reads=[cols], writes=[cols])
    b2scr = sb("b2scr", [128, 1024], F32, O_W["k"])
    P.dma("sp", b2scr[0:1, 0:1024], bo_d, b2scr, name="bo")
    fns = []
    for nch in range(8):
        for kc in range(8):
            fns.append(lambda e, nch=nch, kc=kc: e.matmul(
                PS[1][0:1, nch * 128:(nch + 1) * 128],
                lhsT=lnbc[:, kc:kc + 1],
                rhs=wsb["o"][:, kc * 1024 + nch * 128:kc * 1024 + nch * 128 + 128],
                start=(kc == 0), stop=(kc == 7)))
    P.emit("pe", fns, reads=[wsb["o"], lnbc], writes=[PS[1]])
    P.emit("dve", lambda e: e.tensor_tensor(row("bwo"), PS[1][0:1, 0:1024], b2scr[0:1, 0:1024], op=ALU.add),
           reads=[PS[1], b2scr], writes=[rows])
    for kc in range(8):
        P.emit("dve", lambda e, kc=kc: e.tensor_scalar(
            wsb["o"][:, kc * 1024:(kc + 1) * 1024],
            wsb["o"][:, kc * 1024:(kc + 1) * 1024],
            cols[:, CP["lng"] + kc:CP["lng"] + kc + 1], None, op0=ALU.mult),
            reads=[wsb["o"], cols], writes=[wsb["o"]])

    if stop_after == "b2":
        P.wait_all(); P.play(); return nc
    x0 = ga("x0", [128, 8 * TA], F32, off=G0)
    x1 = ga("x1", [128, 8 * TA], F32, off=G0 + 8192)
    xh_tm = ga("xh_tm", [128, 1024], F32, off=G0 + 16384)
    xh_fm = ga("xh_fm", [128, 8 * TA], BF16, off=G0 + 20480)

    def a2b_xstage(ti):
        xb = (x0, x1)[ti % 2]
        t0 = ti * TA
        xbv = xb[:].rearrange("p (c t) -> p c t", c=8)
        P.dma("sp", xbv, x_fm_v[:, :, t0:t0 + TA], xb, name=f"xa{ti % 2}")
        for gl in range(TA // 128):
            gg = ti * (TA // 128) + gl
            P.emit("dve", lambda e, gg=gg: e.tensor_scalar(
                xh_tm[:], ret_v[:, RSLOT[gg], :], colg("mu2", gg), colg("rstd2", gg),
                op0=ALU.subtract, op1=ALU.mult),
                reads=[retr[RSLOT[gg]], cols], writes=[xh_tm])
            ps = PS[2 + gl % 2]
            P.emit("pe", [
                (lambda e, c=c, ps=ps: e.matmul(
                    ps[:, c * 128:(c + 1) * 128], lhsT=xh_tm[:, c * 128:(c + 1) * 128],
                    rhs=ident[:], is_transpose=True, start=True, stop=True))
                for c in range(8)],
                reads=[xh_tm, ident], writes=[ps])
            P.emit("dve", lambda e, gl=gl, ps=ps: e.tensor_copy(
                xh_fm[:].rearrange("p (c t) -> p c t", c=8)[:, :, gl * 128:(gl + 1) * 128],
                ps[:].rearrange("p (c t) -> p c t", c=8)),
                reads=[ps], writes=[xh_fm])

    def a2b_mms(ti):
        for half in range(2):
            ps = PS[half]
            fns = []
            for nco in range(4):
                nch = half * 4 + nco
                for kc in range(8):
                    fns.append(lambda e, nch=nch, nco=nco, kc=kc, ps=ps:
                               e.matmul(ps[:, nco * TA:(nco + 1) * TA],
                                        lhsT=wsb["o"][:, kc * 1024 + nch * 128:kc * 1024 + nch * 128 + 128],
                                        rhs=xh_fm[:, kc * TA:(kc + 1) * TA],
                                        start=(kc == 0), stop=False))
                pb, c = RP["bwo"]
                fns.append(lambda e, nch=nch, nco=nco, ps=ps, pb=pb, c=c:
                           e.matmul(ps[:, nco * TA:(nco + 1) * TA],
                                    lhsT=rows[pb:pb + 1, c + nch * 128:c + nch * 128 + 128],
                                    rhs=ones[pb:pb + 1, 0:TA],
                                    start=False, stop=True))
            P.emit("pe", fns, reads=[wsb["o"], xh_fm, rows, ones], writes=[ps])

    def a2b_acc(ti):
        xb = (x0, x1)[ti % 2]
        t0 = ti * TA
        for half in range(2):
            ps = PS[half]
            P.emit("dve", lambda e, half=half, xb=xb, t0=t0: e.tensor_tensor(
                xb[:, half * 4 * TA:(half + 1) * 4 * TA],
                xb[:, half * 4 * TA:(half + 1) * 4 * TA],
                xn_sb[:].rearrange("p (c t) -> p c t", c=8)[:, half * 4:half * 4 + 4, t0:t0 + TA],
                op=ALU.add), reads=[xb, xn_sb], writes=[xb])
            P.emit("dve", lambda e, half=half, ps=ps, xb=xb: e.tensor_tensor(
                xb[:, half * 4 * TA:(half + 1) * 4 * TA],
                xb[:, half * 4 * TA:(half + 1) * 4 * TA],
                ps[:], op=ALU.add), reads=[xb, ps], writes=[xb])
        P.dma("act", acc_v[:, :, t0:t0 + TA],
              xb[:].rearrange("p (c t) -> p c t", c=8), acc_h,
              src_bufs=[xb], name="accst")


    a2b_xstage(0)
    for ti in range(NTA):
        a2b_mms(ti)
        if ti + 1 < NTA:
            a2b_xstage(ti + 1)
        a2b_acc(ti)

    h1 = sb("h1_sb", [128, 32 * TF], BF16, 32768)
    h2 = sb("h2_sb", [128, 32 * TF], BF16, 65536)
    kst = [sb("kst0", [128, 8192], BF16, 98304),
           sb("kst1", [128, 8192], BF16, 114688),
           sb("kst2", [128, 8192], BF16, G0)]
    accin = sb("accin", [128, 8 * TF], F32, 131072)
    outb = sb("outb", [128, 8 * TF], F32, 147456)

    def srow_build(ps, src_h, hi_nm, lo_nm, f_nm):
        P.emit("pe", [
            (lambda e, kc=kc, ps=ps: e.matmul(
                ps[0:1, 0:TF], lhsT=ones[:, 0:1],
                rhs=src_h[:, kc * TF:(kc + 1) * TF],
                start=(kc == 0), stop=(kc == 31)))
            for kc in range(32)], reads=[ones, src_h], writes=[ps])
        P.emit("dve", lambda e, ps=ps: e.tensor_copy(rowf(f_nm), ps[0:1, 0:TF]),
               reads=[ps], writes=[rowsf])
        P.emit("dve", lambda e: e.tensor_copy(row(hi_nm, TF), rowf(f_nm)),
               reads=[rowsf], writes=[rows])
        P.emit("dve", lambda e: e.tensor_tensor(rowf("scr"), rowf(f_nm), row(hi_nm, TF), op=ALU.subtract),
               reads=[rowsf, rows], writes=[rowsf])
        P.emit("dve", lambda e: e.tensor_copy(row(lo_nm, TF), rowf("scr")),
               reads=[rowsf], writes=[rows])

    for tj in range(NTF):
        t0 = tj * TF
        abv = accin[:].rearrange("p (c t) -> p c t", c=8)
        P.dma("sp", abv, acc_v[:, :, t0:t0 + TF], accin, src_bufs=[acc_h],
              name="accin")
        for g in range(16):
            kb = kst[g % 3]
            P.dma("sp", kb[:, 0:2048].rearrange("p (k n) -> p k n", k=8),
                  k0dbf[g].rearrange("k p n -> p k n"), kb,
                  src_bufs=[k0dbf_h], name=f"k0s{g % 2}")
            ps = PS[g % 4]
            fns = []
            for nco in range(2):
                for kc in range(8):
                    fns.append(lambda e, nco=nco, kc=kc, ps=ps, kb=kb, t0=t0:
                               e.matmul(ps[:, nco * TF:(nco + 1) * TF],
                                        lhsT=kb[:, kc * 256 + nco * 128:kc * 256 + nco * 128 + 128],
                                        rhs=xn_v[:, kc, t0:t0 + TF],
                                        start=(kc == 0), stop=(kc == 7)))
            P.emit("pe", fns, reads=[kb, xn_sb], writes=[ps])
            P.emit("act", lambda e, g=g, ps=ps: e.activation(
                h1[:, g * 1024:(g + 1) * 1024], ps[:], AF.Relu, scale=sgl("two_s0")),
                reads=[ps, cols], writes=[h1])
        srow_build(PS[0], h1, "s1hi", "s1lo", "s1")
        for g in range(16):
            kb = kst[g % 3]
            P.dma("sp", kb[:].rearrange("p (k n) -> p k n", k=32),
                  k1bf[g].rearrange("k p n -> p k n"), kb,
                  src_bufs=[k1bf_h], name=f"k1s{g % 3}")
            ps = PS[g % 4]
            fns = []
            for nco in range(2):
                for kc in range(32):
                    fns.append(lambda e, nco=nco, kc=kc, ps=ps, kb=kb:
                               e.matmul(ps[:, nco * TF:(nco + 1) * TF],
                                        lhsT=kb[:, kc * 256 + nco * 128:kc * 256 + nco * 128 + 128],
                                        rhs=h1[:, kc * TF:(kc + 1) * TF],
                                        start=(kc == 0), stop=False))
            P.emit("pe", fns, reads=[kb, h1], writes=[ps])
            fns = []
            for nco in range(2):
                fns.append(lambda e, nco=nco, ps=ps: e.matmul(
                    ps[:, nco * TF:(nco + 1) * TF], lhsT=row("mh0", 128),
                    rhs=row("s1hi", TF), start=False, stop=False))
                fns.append(lambda e, nco=nco, ps=ps: e.matmul(
                    ps[:, nco * TF:(nco + 1) * TF], lhsT=row("mh0", 128),
                    rhs=row("s1lo", TF), start=False, stop=True))
            P.emit("pe", fns, reads=[rows], writes=[ps])
            P.emit("act", lambda e, g=g, ps=ps: e.activation(
                h2[:, g * 1024:(g + 1) * 1024], ps[:], AF.Relu, scale=sgl("four_s1")),
                reads=[ps, cols], writes=[h2])
        srow_build(PS[0], h2, "s2hi", "s2lo", "s2")
        for g in range(4):
            kb = kst[g % 3]
            P.dma("sp", kb[:].rearrange("p (k n) -> p k n", k=32),
                  k2bf[g].rearrange("k p n -> p k n"), kb,
                  src_bufs=[k2bf_h], name=f"k2s{g % 3}")
            ps = PS[g % 4]
            fns = []
            for nco in range(2):
                for kc in range(32):
                    fns.append(lambda e, nco=nco, kc=kc, ps=ps, kb=kb:
                               e.matmul(ps[:, nco * TF:(nco + 1) * TF],
                                        lhsT=kb[:, kc * 256 + nco * 128:kc * 256 + nco * 128 + 128],
                                        rhs=h2[:, kc * TF:(kc + 1) * TF],
                                        start=(kc == 0), stop=False))
            P.emit("pe", fns, reads=[kb, h2], writes=[ps])
            fns = []
            for nco in range(2):
                fns.append(lambda e, nco=nco, ps=ps: e.matmul(
                    ps[:, nco * TF:(nco + 1) * TF], lhsT=row("mh0", 128),
                    rhs=row("s2hi", TF), start=False, stop=False))
                fns.append(lambda e, nco=nco, ps=ps: e.matmul(
                    ps[:, nco * TF:(nco + 1) * TF], lhsT=row("mh0", 128),
                    rhs=row("s2lo", TF), start=False, stop=True))
            P.emit("pe", fns, reads=[rows], writes=[ps])
            P.emit("act", lambda e, g=g, ps=ps: e.activation(
                outb[:, g * 1024:(g + 1) * 1024], ps[:], AF.Identity,
                scale=sgl("four_s2")), reads=[ps, cols], writes=[outb])
        P.emit("dve", lambda e: e.tensor_tensor(outb[:], outb[:], accin[:], op=ALU.add),
               reads=[outb, accin], writes=[outb])
        P.dma("act", out_v[:, :, t0:t0 + TF],
              outb[:].rearrange("p (c t) -> p c t", c=8), None,
              src_bufs=[outb], name="outst")
        if debug and tj == 0:
            P.dma("act", dbg["h1"], h1[:], None, src_bufs=[h1], name="dbgh1")
            P.dma("act", dbg["h2"], h2[:], None, src_bufs=[h2], name="dbgh2")

    P.wait_all()
    P.play()
    return nc


def _prep_shared(inputs):
    f = np.ascontiguousarray
    d = {}
    for k, nm in (("k", "Wk"), ("v", "Wv"), ("q", "Wq"), ("kp", "Wkp"),
                  ("qp", "Wqp"), ("o", "Wo")):
        w = np.asarray(inputs[f"L0_attn_{nm}"], dtype=np.float32)
        d[f"w_{k}"] = f(w.reshape(8, 128, 8, 128).transpose(1, 0, 2, 3)
                        .reshape(128, 8192))
    for k, nm in (("v", "bv"), ("kp", "bkp"), ("qp", "bqp"), ("o", "bo")):
        d[f"b_{k}"] = f(np.asarray(inputs[f"L0_attn_{nm}"],
                                   dtype=np.float32).reshape(1, D))
    for nm, key in (("Wk", "wkT"), ("Wq", "wqT")):
        w = np.asarray(inputs[f"L0_attn_{nm}"], dtype=np.float32).T
        d[key] = f(w.reshape(8, 128, 8, 128).transpose(1, 0, 2, 3)
                   .reshape(128, 8192))
    d["bkc"] = f(np.asarray(inputs["L0_attn_bk"], dtype=np.float32)
                 .reshape(8, 128).T)
    d["bqc"] = f(np.asarray(inputs["L0_attn_bq"], dtype=np.float32)
                 .reshape(8, 128).T)
    d["ps_r"] = f(np.asarray(inputs["L0_attn_phase_scale"],
                             dtype=np.float32).reshape(1, D))
    d["lng_c"] = f(np.asarray(inputs["L0_attn_ln_g"],
                              dtype=np.float32).reshape(8, 128).T)
    d["lnb_c"] = f(np.asarray(inputs["L0_attn_ln_b"],
                              dtype=np.float32).reshape(8, 128).T)
    k0 = np.asarray(inputs["L0_ffn_k0"]).view(np.uint8)
    d["k0t"] = f(k0[:D].reshape(8, 128, DF).transpose(1, 0, 2).reshape(128, 8 * DF))
    d["k0b"] = f(k0[D:].reshape(8, 128, DF).transpose(1, 0, 2).reshape(128, 8 * DF))
    k1 = np.asarray(inputs["L0_ffn_k1"]).view(np.uint8)
    d["k1t"] = f(k1.reshape(32, 128, 16, 256).transpose(2, 0, 1, 3))
    k2 = np.asarray(inputs["L0_ffn_k2"]).view(np.uint8)
    d["k2t"] = f(k2.reshape(32, 128, 4, 256).transpose(2, 0, 1, 3))
    for i in (0, 1, 2):
        d[f"s{i}_c"] = np.full((128, 1), np.asarray(inputs[f"L0_ffn_s{i}"],
                                                    dtype=np.float32))
    import ml_dtypes
    p = np.arange(128)[:, None]; q = np.arange(128)[None, :]
    d["u128_c"] = (((p <= q) & ((p < 64) == (q < 64)))
                   .astype(ml_dtypes.bfloat16))
    d["ident_c"] = np.eye(128, dtype=np.float32)
    return d


def kernel(**inputs):
    nc = build(debug=False)
    shared = _prep_shared(inputs)
    x = np.asarray(inputs["L0_attn_x"], dtype=np.float32).reshape(S, D)
    in_maps = []
    for c in range(NCORE):
        xs = x[c * TOK:(c + 1) * TOK]
        m = dict(shared)
        m["x_tm"] = np.ascontiguousarray(xs)
        m["x_fm"] = np.ascontiguousarray(xs.T)
        in_maps.append(m)
    res = run_bass_kernel_spmd(nc, in_maps, core_ids=list(range(NCORE)))
    out = np.concatenate([r["out_fm"].T for r in res.results], axis=0)
    return out.reshape(1, S, D).astype(np.float32)


# version 31
# speedup vs baseline: 1.0209x; 1.0209x over previous
import math

import numpy as np

import concourse.bass as bass
import concourse.mybir as mybir
from concourse.bass_utils import run_bass_kernel_spmd

F32 = mybir.dt.float32
BF16 = mybir.dt.bfloat16
U8 = mybir.dt.uint8
I16 = mybir.dt.int16
I32 = mybir.dt.int32
AF = mybir.ActivationFunctionType
ALU = mybir.AluOpType
PI = math.pi
INV2PI = 1.0 / (2.0 * PI)
TWOPI = 2.0 * PI
TP_HI = float(np.float32(2 * np.pi))
TP_LO = float(np.float32(2 * np.pi - np.float64(np.float32(2 * np.pi))))
HP_HI = float(np.float32(np.pi / 2))
HP_LO = float(np.float32(np.pi / 2 - np.float64(np.float32(np.pi / 2))))

NCORE = 8
S, D = 16384, 1024
TOK = S // NCORE
NG = TOK // 128
TA = 256
NTA = TOK // TA
TF = 512
NTF = TOK // TF
DF = 4096

SBUF_BASE = 16512
SBUF_TOP = 229344
ENGS = ("pe", "act", "dve", "pool", "sp")
DT_SZ = {F32: 4, BF16: 2, U8: 1, I16: 2, I32: 4}


class Prog:

    def __init__(self, nc):
        self.nc = nc
        self.ops = {e: [] for e in ENGS}
        self.esem = {e: nc.alloc_semaphore(f"sem_{e}") for e in ENGS}
        self.ecnt = {e: 0 for e in ENGS}
        self.water = {e: {} for e in ENGS}
        self.bufs = {}
        self.dsem = {}
        self.sems = dict(self.esem)

    def reg(self, h, lo=None, size=None):
        self.bufs[id(h)] = dict(h=h, iv=(lo, lo + size) if lo is not None else None,
                                w={}, r=[])
        return h

    def sballoc(self, name, shape, dtype, off):
        per = int(np.prod(shape[1:])) * DT_SZ[dtype]
        assert SBUF_BASE + off + per <= SBUF_TOP, (name, off, per)
        h = self.nc.alloc_sbuf_tensor_at(name, list(shape), dtype,
                                         offset=SBUF_BASE + off)
        return self.reg(h, off, per)

    def _overlaps(self, b):
        me = self.bufs[id(b)]
        if me["iv"] is None:
            return [me]
        lo, hi = me["iv"]
        return [m for m in self.bufs.values()
                if m["iv"] is not None and m["iv"][0] < hi and lo < m["iv"][1]]

    def _need(self, eng, mark):
        if mark is None:
            return None
        k, v = mark
        if k == eng == "pe":
            return None
        if self.water[eng].get(k, 0) >= v:
            return None
        return mark

    def _collect(self, eng, reads, writes):
        need = {}

        def add(mark):
            m = self._need(eng, mark)
            if m:
                need[m[0]] = max(need.get(m[0], 0), m[1])

        for b in reads:
            if id(b) in self.bufs:
                for m in self._overlaps(b):
                    for kv in m["w"].items():
                        add(kv)
        for b in writes:
            if id(b) in self.bufs:
                for m in self._overlaps(b):
                    for kv in m["w"].items():
                        add(kv)
                    for rr in m["r"]:
                        add(rr)
        return need

    def _emit_waits(self, eng, need):
        for k, v in sorted(need.items(), key=lambda kv: str(kv[0])):
            self.water[eng][k] = v
            sem = self.sems[k]
            self.ops[eng].append(lambda e, s=sem, vv=v: e.wait_ge(s, vv))

    def _mark(self, reads, writes, semkey, val):
        for b in writes:
            if id(b) not in self.bufs:
                self.reg(b)
            m = self.bufs[id(b)]
            m["w"][semkey] = val
            m["r"] = [r for r in m["r"] if r[0] != semkey or r[1] > val]
        for b in reads:
            if id(b) in self.bufs:
                self.bufs[id(b)]["r"].append((semkey, val))

    def emit(self, eng, fns, reads=(), writes=()):
        if not isinstance(fns, (list, tuple)):
            fns = [fns]
        need = self._collect(eng, reads, writes)
        self._emit_waits(eng, need)
        sem = self.esem[eng]
        for f in fns[:-1]:
            self.ops[eng].append(f)
        self.ecnt[eng] += 1
        val = self.ecnt[eng]
        self.ops[eng].append(lambda e, f=fns[-1], s=sem: f(e).then_inc(s, 1))
        self._mark(reads, writes, eng, val)
        return (eng, val)

    def dma(self, queue, dst, src, dst_buf, src_bufs=(), name="d", after=(),
            **kw):
        key = id(dst_buf) if dst_buf is not None else ("anon", len(self.dsem))
        if key not in self.dsem:
            h = self.nc.alloc_semaphore(f"sd_{name}_{len(self.dsem)}")
            self.dsem[key] = [h, 0]
            self.sems[("d", key)] = h
        ent = self.dsem[key]
        writes = [dst_buf] if dst_buf is not None else []
        need = self._collect(queue, list(src_bufs) + list(after), writes)
        m = self._need(queue, (("d", key), ent[1])) if ent[1] > 0 else None
        if m:
            need[m[0]] = max(need.get(m[0], 0), m[1])
        self._emit_waits(queue, need)
        ent[1] += 16
        total, h = ent[1], ent[0]
        self.ops[queue].append(
            lambda e, d=dst, s=src, hh=h, kws=kw: e.dma_start(d, s, **kws).then_inc(hh, 16))
        self._mark(list(src_bufs), writes, ("d", key), total)
        return (("d", key), total)

    def wait_all(self, engs=ENGS):
        marks = [(e, self.ecnt[e]) for e in ENGS if self.ecnt[e] > 0]
        marks += [(("d", k), v[1]) for k, v in self.dsem.items() if v[1] > 0]
        for eng in engs:
            need = {}
            for k, v in marks:
                m = self._need(eng, (k, v))
                if m:
                    need[m[0]] = max(need.get(m[0], 0), m[1])
            self._emit_waits(eng, need)

    def play(self):
        with self.nc.Block() as block:
            block.tensor(lambda e: [f(e) for f in self.ops["pe"]])
            block.scalar(lambda e: [f(e) for f in self.ops["act"]])
            block.vector(lambda e: [f(e) for f in self.ops["dve"]])
            block.gpsimd(lambda e: [f(e) for f in self.ops["pool"]])
            block.sync(lambda e: [f(e) for f in self.ops["sp"]])


def build(debug=False, stop_after=None):
    nc = bass.Bass("TRN2", target_bir_lowering=False)
    P = Prog(nc)

    def din(name, shape, dtype):
        return nc.dram_tensor(name, list(shape), dtype, kind="ExternalInput").ap()

    x_fm = din("x_fm", [D, TOK], F32)
    x_tm = din("x_tm", [TOK, D], F32)
    w_in = {k: din(f"w_{k}", [128, 8192], F32)
            for k in ("k", "kp", "v", "q", "qp", "o")}
    k0t_d = din("k0t", [128, 8 * DF], U8)
    k0b_d = din("k0b", [128, 8 * DF], U8)
    k1_d = din("k1t", [16, 32, 128, 256], U8)
    k2_d = din("k2t", [4, 32, 128, 256], U8)
    brow_in = {k: din(f"b_{k}", [1, D], F32)
               for k in ("v", "kp", "qp")}
    wkT_d = din("wkT", [128, 8192], F32)
    wqT_d = din("wqT", [128, 8192], F32)
    bkc_d = din("bkc", [128, 8], F32)
    bqc_d = din("bqc", [128, 8], F32)
    bo_d = din("b_o", [1, D], F32)
    ps_d = din("ps_r", [1, D], F32)
    lng_d = din("lng_c", [128, 8], F32)
    lnb_d = din("lnb_c", [128, 8], F32)
    s_d = {k: din(f"s{k}_c", [128, 1], F32) for k in (0, 1, 2)}
    u128_d = din("u128_c", [128, 128], BF16)
    ident_d = din("ident_c", [128, 128], F32)

    out_d = nc.dram_tensor("out_fm", [D, TOK], F32, kind="ExternalOutput").ap()
    dbg = {}
    if debug:
        for nm, shp in (("xn", [128, 8 * TOK]), ("ret", [128, NG * D]),
                        ("h1", [128, 32 * TF]), ("h2", [128, 32 * TF])):
            dbg[nm] = nc.dram_tensor("dbg_" + nm, shp, BF16,
                                     kind="ExternalOutput").ap()
        dbg["acc"] = nc.dram_tensor("dbg_acc", [D, TOK], F32,
                                    kind="ExternalOutput").ap()
        for nm in ("br", "bi", "sq", "cosq"):
            dbg[nm] = nc.dram_tensor("dbg_" + nm, [128, 1024], BF16,
                                     kind="ExternalOutput").ap()
        for nm in ("memr", "memi"):
            dbg[nm] = nc.dram_tensor("dbg_" + nm, [128, 1024], BF16,
                                     kind="ExternalOutput").ap()
        dbg["u128"] = nc.dram_tensor("dbg_u128", [128, 128], BF16,
                                     kind="ExternalOutput").ap()

    def dint(name, shape, dtype):
        return nc.dram_tensor(name, list(shape), dtype, kind="Internal").ap()

    k1bf = dint("k1bf", [16, 32, 128, 256], BF16)
    k2bf = dint("k2bf", [4, 32, 128, 256], BF16)
    k0dbf = dint("k0dbf", [16, 8, 128, 256], BF16)
    acc_d = dint("acc_d", [D, TOK], F32)
    k1bf_h = P.reg(object())
    k2bf_h = P.reg(object())
    k0dbf_h = P.reg(object())
    acc_h = P.reg(object())

    O_XN = 0
    O_W = {k: 32768 + i * 16384 for i, k in enumerate(("k", "kp", "v", "q", "qp"))}
    O_WO = 114688
    O_RET = 131072
    O_ROWS = 163840
    O_ROWSF = 176128
    E_END = 180224
    O_ONES = 180224
    O_PSD = 180736
    O_U128 = 184832
    O_IDENT = 185152
    O_COLS = 185664
    O_LNB = 186688
    G0 = 186816
    GTOP = SBUF_TOP - SBUF_BASE
    assert G0 + 24576 <= GTOP, (G0, GTOP)

    sb = P.sballoc
    xn_sb = sb("xn_sb", [128, 8 * TOK], BF16, O_XN)
    wsb = {k: sb(f"w{k}_sb", [128, 8192], BF16, O_W[k]) for k in O_W}
    wsb["o"] = sb("wo_sb", [128, 8192], BF16, O_WO)
    ret_sb = sb("ret_sb", [128, NG * D], BF16, O_RET)
    rows = sb("rows_sb", [128, 6144], BF16, O_ROWS)
    rowsf = sb("rowsf_sb", [128, 1024], F32, O_ROWSF)
    ones = sb("ones_sb", [128, 256], BF16, O_ONES)
    psd = sb("psd_sb", [128, 1024], F32, O_PSD)
    u128 = sb("u128_sb", [128, 128], BF16, O_U128)
    ident = sb("ident_sb", [128, 128], F32, O_IDENT)
    cols = sb("cols_sb", [128, 256], F32, O_COLS)
    lnbc = sb("lnbc_sb", [128, 8], BF16, O_LNB)
    bkcs = sb("bkc_sb", [128, 8], BF16, O_LNB + 32)
    bqcs = sb("bqc_sb", [128, 8], BF16, O_LNB + 64)

    RP = {"bkp": (0, 0), "bqp": (0, 1024), "mh0": (0, 2048),
          "bv": (0, 3072),
          "s1hi": (0, 3072), "s1lo": (0, 3584),
          "s2hi": (0, 4096), "s2lo": (0, 4608),
          "bwo": (0, 5120)}

    def row(nm, n=1024, c0=0):
        p, c = RP[nm]
        return rows[p:p + 1, c + c0:c + c0 + n]

    RF = {"s1": (0, 0, 512), "s2": (0, 512, 512), "scr": (32, 0, 512)}

    def rowf(nm):
        p, c, n = RF[nm]
        return rowsf[p:p + 1, c:c + n]

    CP = dict(sum1=0, sq1=16, mu1=32, rstd1=48, sum2=64, sq2=80, mu2=96,
              rstd2=112, scrA=128, scrB=144, lng=160, singles=176,
              sqa=192, sqb=208)
    SGL = dict(two_s0=176, four_s1=177, four_s2=178, hpi=179)

    def colg(nm, g):
        return cols[:, CP[nm] + g:CP[nm] + g + 1]

    def colr(nm, n=16):
        return cols[:, CP[nm]:CP[nm] + n]

    def sgl(nm):
        return cols[:, SGL[nm]:SGL[nm] + 1]

    PS = [P.reg(nc.alloc_psum_tensor(f"ps{i}", [128, 1024], F32))
          for i in range(4)]

    xn_v = xn_sb[:].rearrange("p (c t) -> p c t", c=8)
    xng = [P.reg(object(), lo=O_XN + g * 256, size=256) for g in range(NG)]
    x_tmf = sb("xtm_sb", [128, NG * D], F32, O_WO)
    xtm_v = x_tmf[:].rearrange("p (g f) -> p g f", g=NG)
    XSLOT = [NG - 1 - g for g in range(NG)]
    xtmr = [P.reg(object(), lo=O_WO + XSLOT[g] * 4096, size=4096)
            for g in range(NG)]
    x_fm_v = x_fm.rearrange("(c p) t -> p c t", p=128)
    acc_v = acc_d.rearrange("(c p) t -> p c t", p=128)
    out_v = out_d.rearrange("(c p) t -> p c t", p=128)
    ret_v = ret_sb[:].rearrange("p (g f) -> p g f", g=NG)
    RSLOT = [NG - 1 - g for g in range(NG)]
    retr = [P.reg(object(), lo=O_RET + sl * 2048, size=2048) for sl in range(NG)]

    g_off = G0

    def ga(name, shape, dtype, off=None):
        nonlocal g_off
        if off is None:
            off = g_off
            per = int(np.prod(shape[1:])) * DT_SZ[dtype]
            g_off += (per + 63) & ~63
        assert off + int(np.prod(shape[1:])) * DT_SZ[dtype] <= GTOP
        return sb(name, shape, dtype, off)

    scs = {k: ga(f"sc{k}", [128, 1], F32) for k in (0, 1, 2)}
    p0r = ga("p0r", [128, 1024], F32)
    st_a = sb("st_a", [128, DF], BF16, O_WO)
    st_b = sb("st_b", [128, DF], BF16, O_WO + 8192)
    st_dd = sb("st_dd", [128, DF], BF16, O_WO + 16384)
    psc_a = sb("psc_a", [128, 2048], BF16, O_WO + 24576)
    psc_f = sb("psc_f", [128, 2048], F32, O_WO + 28672)

    P.dma("sp", u128[:], u128_d, u128, name="u128")
    P.dma("sp", ident[:], ident_d, ident, name="ident")
    P.emit("dve", lambda e: e.memset(ones[:], 1.0), writes=[ones])
    P.emit("dve", lambda e: e.memset(sgl("hpi"), PI / 2.0), writes=[cols])

    P.dma("pool", lnbc[:], lnb_d, lnbc, name="lnb")
    P.dma("pool", bkcs[:], bkc_d, bkcs, name="bkc")
    P.dma("pool", bqcs[:], bqc_d, bqcs, name="bqc")
    stK = sb("stK", [128, 8192], BF16, O_WO)
    for wtd, wpname, dstname in ((wkT_d, "kp", "k"), (wqT_d, "qp", "q")):
        P.dma("pool", wsb[wpname][:], w_in[wpname], wsb[wpname],
              name=f"w{wpname}", max_dma_last_dim=8192)
        P.dma("pool", stK[:], wtd, stK, name="wT", max_dma_last_dim=8192)
        for dc in range(8):
            for h in range(2):
                ps = PS[(dc * 2 + h) % 4]
                fns = [
                    (lambda e, kfc=kfc, dc=dc, h=h, ps=ps, wp=wpname:
                     e.matmul(ps[:, 0:512],
                              lhsT=stK[:, kfc * 1024 + dc * 128:kfc * 1024 + dc * 128 + 128],
                              rhs=wsb[wp][:, kfc * 1024 + h * 512:kfc * 1024 + h * 512 + 512],
                              start=(kfc == 0), stop=(kfc == 7)))
                    for kfc in range(8)]
                P.emit("pe", fns, reads=[stK, wsb[wpname]], writes=[ps])
                P.emit("act", lambda e, dc=dc, h=h, ps=ps, dn=dstname: e.activation(
                    wsb[dn][:, dc * 1024 + h * 512:dc * 1024 + h * 512 + 512],
                    ps[:, 0:512], AF.Identity), reads=[ps], writes=[wsb[dstname]])
    for k in (0, 1, 2):
        P.dma("sp", scs[k][:], s_d[k], scs[k], name=f"s{k}")
    P.dma("sp", colr("lng", 8), lng_d, cols, name="lng")
    P.emit("dve", lambda e: e.tensor_scalar(sgl("two_s0"), scs[0][:], 2.0, None, op0=ALU.mult),
           reads=[scs[0]], writes=[cols])
    P.emit("dve", lambda e: e.tensor_scalar(sgl("four_s1"), scs[1][:], 4.0, None, op0=ALU.mult),
           reads=[scs[1]], writes=[cols])
    P.emit("dve", lambda e: e.tensor_scalar(sgl("four_s2"), scs[2][:], 4.0, None, op0=ALU.mult),
           reads=[scs[2]], writes=[cols])

    P.dma("pool", wsb["v"][:], w_in["v"], wsb["v"], name="wv",
          max_dma_last_dim=8192)
    P.dma("sp", psc_f[0:1, 0:1024], ps_d, psc_f, name="psr")
    P.emit("dve", lambda e: e.tensor_copy(psc_a[0:1, 0:1024], psc_f[0:1, 0:1024]),
           reads=[psc_f], writes=[psc_a])
    P.emit("dve", lambda e: e.tensor_tensor(psc_f[0:1, 1024:2048], psc_f[0:1, 0:1024], psc_a[0:1, 0:1024], op=ALU.subtract),
           reads=[psc_f, psc_a], writes=[psc_f])
    P.emit("dve", lambda e: e.tensor_copy(psc_a[0:1, 1024:2048], psc_f[0:1, 1024:2048]),
           reads=[psc_f], writes=[psc_a])
    P.emit("pe", [
        lambda e: e.matmul(PS[0][:, 0:512], lhsT=ones[0:1, 0:128], rhs=psc_a[0:1, 0:512], start=True, stop=False),
        lambda e: e.matmul(PS[0][:, 0:512], lhsT=ones[0:1, 0:128], rhs=psc_a[0:1, 1024:1536], start=False, stop=True),
        lambda e: e.matmul(PS[0][:, 512:1024], lhsT=ones[0:1, 0:128], rhs=psc_a[0:1, 512:1024], start=True, stop=False),
        lambda e: e.matmul(PS[0][:, 512:1024], lhsT=ones[0:1, 0:128], rhs=psc_a[0:1, 1536:2048], start=False, stop=True),
    ], reads=[ones, psc_a], writes=[PS[0]])
    P.emit("dve", lambda e: e.tensor_scalar(psd[:], PS[0][:], 0.5, None, op0=ALU.mult),
           reads=[PS[0]], writes=[psd])

    def bias_builds():
        P.dma("sp", p0r[0:1, 0:1024], brow_in["v"], p0r, name="brow")
        P.emit("dve", lambda e: e.tensor_copy(row("bv"), p0r[0:1, 0:1024]),
               reads=[p0r], writes=[rows])
        for bnm, bc, wpname, slot in (("kp", bkcs, "kp", "bkp"),
                                      ("qp", bqcs, "qp", "bqp")):
            P.dma("sp", p0r[0:1, 0:1024], brow_in[bnm], p0r, name="brow")
            fns = []
            for h in range(2):
                for kfc in range(8):
                    fns.append(lambda e, kfc=kfc, h=h, bc=bc, wp=wpname: e.matmul(
                        PS[3][0:1, h * 512:h * 512 + 512],
                        lhsT=bc[:, kfc:kfc + 1],
                        rhs=wsb[wp][:, kfc * 1024 + h * 512:kfc * 1024 + h * 512 + 512],
                        start=(kfc == 0), stop=(kfc == 7)))
            P.emit("pe", fns, reads=[bc, wsb[wpname]], writes=[PS[3]])
            P.emit("dve", lambda e, slot=slot: e.tensor_tensor(
                row(slot), PS[3][0:1, 0:1024], p0r[0:1, 0:1024], op=ALU.add),
                reads=[PS[3], p0r], writes=[rows])
        P.emit("dve", lambda e: e.memset(row("mh0", 128), -0.5), writes=[rows])

    a0_scr = [ga("a0s0", [128, 1024], BF16), ga("a0s1", [128, 1024], BF16)]
    a0_xn = ga("a0xn", [128, 1024], F32)
    for g in range(NG):
        P.dma("sp", xtm_v[:, XSLOT[g], :], x_tm[g * 128:(g + 1) * 128, :],
              xtmr[g], name=f"xt{g}", after=(stK,) if g == 0 else ())
    for g in range(NG):
        sc = a0_scr[g % 2]
        P.emit("dve", lambda e, g=g, s=sc: e.tensor_scalar(
            s[:], xtm_v[:, XSLOT[g], :], 1.0, 0.0, op0=ALU.mult, op1=ALU.add,
            accum_out=colg("sum1", g)), reads=[xtmr[g]], writes=[sc, cols])
        P.emit("act", lambda e, g=g, s=sc: e.activation(
            s[:], xtm_v[:, XSLOT[g], :], AF.Square, accum_out=colg("sq1", g)),
            reads=[xtmr[g]], writes=[sc, cols])
    P.emit("dve", lambda e: e.tensor_scalar(colr("mu1"), colr("sum1"), 1.0 / D, None, op0=ALU.mult),
           reads=[cols], writes=[cols])
    P.emit("dve", lambda e: e.tensor_scalar(colr("scrA"), colr("sq1"), 1.0 / D, None, op0=ALU.mult),
           reads=[cols], writes=[cols])
    P.emit("dve", lambda e: e.tensor_tensor(colr("scrB"), colr("mu1"), colr("mu1"), op=ALU.mult),
           reads=[cols], writes=[cols])
    P.emit("dve", lambda e: e.tensor_tensor(colr("scrA"), colr("scrA"), colr("scrB"), op=ALU.subtract),
           reads=[cols], writes=[cols])
    P.emit("dve", lambda e: e.tensor_scalar(colr("scrA"), colr("scrA"), 1.0, 1e-5, op0=ALU.mult, op1=ALU.add),
           reads=[cols], writes=[cols])
    P.emit("act", lambda e: e.activation(colr("scrB"), colr("scrA"), AF.Ln),
           reads=[cols], writes=[cols])
    P.emit("act", lambda e: e.activation(colr("rstd1"), colr("scrB"), AF.Exp, scale=-0.5),
           reads=[cols], writes=[cols])
    for g in range(NG):
        P.emit("dve", lambda e, g=g: e.tensor_scalar(
            a0_xn[:], xtm_v[:, XSLOT[g], :], colg("mu1", g), colg("rstd1", g),
            op0=ALU.subtract, op1=ALU.mult), reads=[xtmr[g], cols], writes=[a0_xn])
        ps = PS[g % 2]
        P.emit("pe", [
            (lambda e, g=g, c=c, ps=ps: e.matmul(
                ps[:, c * 128:(c + 1) * 128],
                lhsT=a0_xn[:, c * 128:(c + 1) * 128], rhs=ident[:],
                is_transpose=True, start=True, stop=True,
                )) for c in range(8)],
            reads=[a0_xn, ident], writes=[ps])
        P.emit("dve", lambda e, g=g, ps=ps: e.tensor_copy(
            xn_v[:, :, g * 128:(g + 1) * 128],
            ps[:].rearrange("p (c t) -> p c t", c=8)),
            reads=[ps], writes=[xng[g]])
        if g == 3:
            bias_builds()
    if debug:
        P.dma("act", dbg["xn"], xn_sb[:], None, src_bufs=[xn_sb], name="dbgxn")

    if stop_after == "a0":
        P.wait_all(); P.play(); return nc
    for kc in range(8):
        P.dma("pool", st_a[:], k0t_d[:, kc * DF:(kc + 1) * DF], st_a,
              name="k0a", max_dma_last_dim=2048)
        P.dma("pool", st_b[:], k0b_d[:, kc * DF:(kc + 1) * DF], st_b,
              name="k0b", max_dma_last_dim=2048)
        P.emit("pool", lambda e: e.tensor_tensor(st_dd[:], st_a[:], st_b[:], op=ALU.subtract),
               reads=[st_a, st_b], writes=[st_dd])
        P.dma("sp", k0dbf[:, kc, :, :].rearrange("g p n -> p g n"),
              st_dd[:].rearrange("p (g n) -> p g n", g=16), k0dbf_h,
              src_bufs=[st_dd], name="k0d")

    if stop_after == "p0":
        P.wait_all(); P.play(); return nc


    P.dma("pool", wsb["o"][:], w_in["o"], wsb["o"], name="wo",
          max_dma_last_dim=8192)
    P.dma("pool", k1bf[0:8], k1_d[0:8], k1bf_h, name="k1bf", max_dma_last_dim=2048)
    P.dma("pool", k1bf[8:16], k1_d[8:16], k1bf_h, name="k1bf", max_dma_last_dim=2048)
    P.dma("pool", k2bf, k2_d, k2bf_h, name="k2bf", max_dma_last_dim=2048)

    ph_ = [ga("ph0", [128, 1024], F32, off=G0),
           ga("ph1", [128, 1024], F32, off=G0 + 4096)]
    wA_ = [ga("wA0", [128, 1024], BF16, off=G0 + 8192),
           ga("wA1", [128, 1024], BF16, off=G0 + 10240)]
    wB_ = [ga("wB0", [128, 1024], BF16, off=G0 + 12288),
           ga("wB1", [128, 1024], BF16, off=G0 + 14336)]
    brb = ga("brb", [128, 1024], BF16, off=G0 + 16384)
    bib = ga("bib", [128, 1024], BF16, off=G0 + 18432)
    cb = ga("cb", [128, 1024], F32, off=G0 + 20480)

    def tmproj_xn(wname, bslot, gg, ps):
        fns = []
        pb, c = RP[bslot]
        for h2 in range(2):
            for kc in range(8):
                fns.append(lambda e, kc=kc, h2=h2, ps=ps, gg=gg, wn=wname:
                           e.matmul(ps[:, h2 * 512:(h2 + 1) * 512],
                                    lhsT=xn_v[:, kc, gg * 128:gg * 128 + 128],
                                    rhs=wsb[wn][:, kc * 1024 + h2 * 512:kc * 1024 + (h2 + 1) * 512],
                                    start=(kc == 0), stop=False))
            fns.append(lambda e, h2=h2, ps=ps, pb=pb, c=c:
                       e.matmul(ps[:, h2 * 512:(h2 + 1) * 512],
                                lhsT=ones[pb:pb + 1, 0:128],
                                rhs=rows[pb:pb + 1, c + h2 * 512:c + (h2 + 1) * 512],
                                start=False, stop=True))
        P.emit("pe", fns, reads=[xn_sb, wsb[wname], rows, ones, xng[gg]],
               writes=[ps])

    def trig_ops(ps_src, par):
        ph, wA, wB = ph_[par], wA_[par], wB_[par]
        return [
            ("act", (lambda e, p=ps_src, ph=ph: e.activation(ph[:], p[:], AF.Tanh)), [ps_src], [ph]),
            ("dve", (lambda e, ph=ph: e.tensor_tensor(ph[:], ph[:], psd[:], op=ALU.mult)), [ph, psd], [ph]),
            ("act", (lambda e, ph=ph, wA=wA: e.activation(wA[:], ph[:], AF.Sin)), [ph], [wA]),
            ("act", (lambda e, ph=ph: e.activation(ph[:], ph[:], AF.Abs)), [ph], [ph]),
            ("dve", (lambda e, ph=ph: e.tensor_scalar(ph[:], ph[:], HP_HI, HP_LO, op0=ALU.subtract, op1=ALU.subtract)), [ph], [ph]),
            ("act", (lambda e, ph=ph, wB=wB: e.activation(wB[:], ph[:], AF.Sin, scale=-1.0)), [ph], [wB]),
            ("act", (lambda e, ph=ph, wA=wA: e.activation(ph[:], wA[:], AF.Square)), [wA], [ph]),
            ("dve", (lambda e, wA=wA, wB=wB: e.scalar_tensor_tensor(wA[:], wA[:], 2.0, wB[:], op0=ALU.mult, op1=ALU.mult)), [wA, wB], [wA]),
            ("dve", (lambda e, ph=ph, wB=wB: e.tensor_scalar(wB[:], ph[:], -2.0, 1.0, op0=ALU.mult, op1=ALU.add)), [ph], [wB]),
        ]

    def a1_A(gg):
        tmproj_xn("v", "bv", gg, PS[0])
        tmproj_xn("k", "bkp", gg, PS[1])
        tmproj_xn("q", "bqp", gg, PS[2])

    def a1_emit(ops):
        for eng, fn, rd, wr in ops:
            P.emit(eng, fn, reads=rd, writes=wr)

    def a1_cumsum(gg, h):
        P.emit("pe", [
            (lambda e, h=h: e.matmul(PS[3][:, 0:512], lhsT=u128[:],
                                     rhs=brb[:, h * 512:(h + 1) * 512],
                                     start=True, stop=True)),
            (lambda e, h=h: e.matmul(PS[3][:, 512:1024], lhsT=u128[:],
                                     rhs=bib[:, h * 512:(h + 1) * 512],
                                     start=True, stop=True)),
        ], reads=[u128, brb, bib], writes=[PS[3]])

    def a1_combine(gg, h):
        P.emit("dve", lambda e, h=h: e.tensor_tensor(
            ph_[0][:, 0:512], PS[3][:, 0:512],
            wB_[1][:, h * 512:(h + 1) * 512], op=ALU.mult),
            reads=[PS[3], wB_[1]], writes=[ph_[0]])
        P.emit("dve", lambda e, h=h: e.tensor_tensor(
            cb[:, 0:512], PS[3][:, 512:1024],
            wA_[1][:, h * 512:(h + 1) * 512], op=ALU.mult),
            reads=[PS[3], wA_[1]], writes=[cb])
        acc_c = "scrA" if h == 0 else "scrB"
        P.emit("dve", lambda e, h=h, gg=gg, ac=acc_c: e.scalar_tensor_tensor(
            ret_v[:, RSLOT[gg], h * 512:(h + 1) * 512], ph_[0][:, 0:512], 1.0,
            cb[:, 0:512], op0=ALU.mult, op1=ALU.add, accum_out=colg(ac, gg)),
            reads=[ph_[0], cb], writes=[retr[RSLOT[gg]], cols])
        sq_c = "sqa" if h == 0 else "sqb"
        P.emit("act", lambda e, h=h, gg=gg, sc=sq_c: e.activation(
            cb[:, 512:1024], ret_v[:, RSLOT[gg], h * 512:(h + 1) * 512], AF.Square,
            accum_out=colg(sc, gg)),
            reads=[retr[RSLOT[gg]]], writes=[cb, cols])

    a1_A(0)
    for gg in range(NG):
        a1_emit(trig_ops(PS[1], 0))
        P.emit("dve", lambda e: e.tensor_tensor(bib[:], PS[0][:], wA_[0][:], op=ALU.mult),
               reads=[PS[0], wA_[0]], writes=[bib])
        P.emit("dve", lambda e: e.tensor_tensor(brb[:], PS[0][:], wB_[0][:], op=ALU.mult),
               reads=[PS[0], wB_[0]], writes=[brb])
        a1_cumsum(gg, 0)
        if gg + 1 < NG:
            tmproj_xn("v", "bv", gg + 1, PS[0])
            tmproj_xn("k", "bkp", gg + 1, PS[1])
        a1_emit(trig_ops(PS[2], 1))
        a1_combine(gg, 0)
        a1_cumsum(gg, 1)
        if gg + 1 < NG:
            tmproj_xn("q", "bqp", gg + 1, PS[2])
        a1_combine(gg, 1)
        P.emit("dve", lambda e, gg=gg: e.tensor_tensor(
            colg("sum2", gg), colg("scrA", gg), colg("scrB", gg), op=ALU.add),
            reads=[cols], writes=[cols])
        P.emit("dve", lambda e, gg=gg: e.tensor_tensor(
            colg("sq2", gg), colg("sqa", gg), colg("sqb", gg), op=ALU.add),
            reads=[cols], writes=[cols])
    if debug:
        P.dma("act", dbg["ret"], ret_sb[:], None, src_bufs=[ret_sb], name="dbgret")

    if stop_after == "a1":
        P.wait_all(); P.play(); return nc
    P.emit("dve", lambda e: e.tensor_scalar(colr("mu2"), colr("sum2"), 1.0 / D, None, op0=ALU.mult),
           reads=[cols], writes=[cols])
    P.emit("dve", lambda e: e.tensor_scalar(colr("scrA"), colr("sq2"), 1.0 / (D * D), None, op0=ALU.mult),
           reads=[cols], writes=[cols])
    P.emit("dve", lambda e: e.tensor_tensor(colr("scrB"), colr("mu2"), colr("mu2"), op=ALU.mult),
           reads=[cols], writes=[cols])
    P.emit("dve", lambda e: e.tensor_scalar(colr("scrB"), colr("scrB"), 1.0 / D, None, op0=ALU.mult),
           reads=[cols], writes=[cols])
    P.emit("dve", lambda e: e.tensor_tensor(colr("scrA"), colr("scrA"), colr("scrB"), op=ALU.subtract),
           reads=[cols], writes=[cols])
    P.emit("dve", lambda e: e.tensor_scalar(colr("scrA"), colr("scrA"), 1.0, 1e-5, op0=ALU.mult, op1=ALU.add),
           reads=[cols], writes=[cols])
    P.emit("act", lambda e: e.activation(colr("scrB"), colr("scrA"), AF.Ln),
           reads=[cols], writes=[cols])
    P.emit("act", lambda e: e.activation(colr("scrA"), colr("scrB"), AF.Exp, scale=-0.5),
           reads=[cols], writes=[cols])
    P.emit("dve", lambda e: e.tensor_scalar(colr("rstd2"), colr("scrA"), 1.0 / 32.0, None, op0=ALU.mult),
           reads=[cols], writes=[cols])
    b2scr = sb("b2scr", [128, 1024], F32, O_W["k"])
    P.dma("sp", b2scr[0:1, 0:1024], bo_d, b2scr, name="bo")
    fns = []
    for nch in range(8):
        for kc in range(8):
            fns.append(lambda e, nch=nch, kc=kc: e.matmul(
                PS[1][0:1, nch * 128:(nch + 1) * 128],
                lhsT=lnbc[:, kc:kc + 1],
                rhs=wsb["o"][:, kc * 1024 + nch * 128:kc * 1024 + nch * 128 + 128],
                start=(kc == 0), stop=(kc == 7)))
    P.emit("pe", fns, reads=[wsb["o"], lnbc], writes=[PS[1]])
    P.emit("dve", lambda e: e.tensor_tensor(row("bwo"), PS[1][0:1, 0:1024], b2scr[0:1, 0:1024], op=ALU.add),
           reads=[PS[1], b2scr], writes=[rows])
    for kc in range(8):
        P.emit("dve", lambda e, kc=kc: e.tensor_scalar(
            wsb["o"][:, kc * 1024:(kc + 1) * 1024],
            wsb["o"][:, kc * 1024:(kc + 1) * 1024],
            cols[:, CP["lng"] + kc:CP["lng"] + kc + 1], None, op0=ALU.mult),
            reads=[wsb["o"], cols], writes=[wsb["o"]])

    if stop_after == "b2":
        P.wait_all(); P.play(); return nc
    x0 = ga("x0", [128, 8 * TA], F32, off=G0)
    x1 = ga("x1", [128, 8 * TA], F32, off=G0 + 8192)
    xh_tm = ga("xh_tm", [128, 1024], F32, off=G0 + 16384)
    xh_fm = ga("xh_fm", [128, 8 * TA], BF16, off=G0 + 20480)

    def a2b_xstage(ti):
        xb = (x0, x1)[ti % 2]
        t0 = ti * TA
        xbv = xb[:].rearrange("p (c t) -> p c t", c=8)
        P.dma("sp", xbv, x_fm_v[:, :, t0:t0 + TA], xb, name=f"xa{ti % 2}")
        for gl in range(TA // 128):
            gg = ti * (TA // 128) + gl
            P.emit("dve", lambda e, gg=gg: e.tensor_scalar(
                xh_tm[:], ret_v[:, RSLOT[gg], :], colg("mu2", gg), colg("rstd2", gg),
                op0=ALU.subtract, op1=ALU.mult),
                reads=[retr[RSLOT[gg]], cols], writes=[xh_tm])
            ps = PS[2 + gl % 2]
            P.emit("pe", [
                (lambda e, c=c, ps=ps: e.matmul(
                    ps[:, c * 128:(c + 1) * 128], lhsT=xh_tm[:, c * 128:(c + 1) * 128],
                    rhs=ident[:], is_transpose=True, start=True, stop=True))
                for c in range(8)],
                reads=[xh_tm, ident], writes=[ps])
            P.emit("dve", lambda e, gl=gl, ps=ps: e.tensor_copy(
                xh_fm[:].rearrange("p (c t) -> p c t", c=8)[:, :, gl * 128:(gl + 1) * 128],
                ps[:].rearrange("p (c t) -> p c t", c=8)),
                reads=[ps], writes=[xh_fm])

    def a2b_mms(ti):
        for half in range(2):
            ps = PS[half]
            fns = []
            for nco in range(4):
                nch = half * 4 + nco
                for kc in range(8):
                    fns.append(lambda e, nch=nch, nco=nco, kc=kc, ps=ps:
                               e.matmul(ps[:, nco * TA:(nco + 1) * TA],
                                        lhsT=wsb["o"][:, kc * 1024 + nch * 128:kc * 1024 + nch * 128 + 128],
                                        rhs=xh_fm[:, kc * TA:(kc + 1) * TA],
                                        start=(kc == 0), stop=False))
                pb, c = RP["bwo"]
                fns.append(lambda e, nch=nch, nco=nco, ps=ps, pb=pb, c=c:
                           e.matmul(ps[:, nco * TA:(nco + 1) * TA],
                                    lhsT=rows[pb:pb + 1, c + nch * 128:c + nch * 128 + 128],
                                    rhs=ones[pb:pb + 1, 0:TA],
                                    start=False, stop=True))
            P.emit("pe", fns, reads=[wsb["o"], xh_fm, rows, ones], writes=[ps])

    def a2b_acc(ti):
        xb = (x0, x1)[ti % 2]
        t0 = ti * TA
        for half in range(2):
            ps = PS[half]
            P.emit("dve", lambda e, half=half, xb=xb, t0=t0: e.tensor_tensor(
                xb[:, half * 4 * TA:(half + 1) * 4 * TA],
                xb[:, half * 4 * TA:(half + 1) * 4 * TA],
                xn_sb[:].rearrange("p (c t) -> p c t", c=8)[:, half * 4:half * 4 + 4, t0:t0 + TA],
                op=ALU.add), reads=[xb, xn_sb], writes=[xb])
            P.emit("dve", lambda e, half=half, ps=ps, xb=xb: e.tensor_tensor(
                xb[:, half * 4 * TA:(half + 1) * 4 * TA],
                xb[:, half * 4 * TA:(half + 1) * 4 * TA],
                ps[:], op=ALU.add), reads=[xb, ps], writes=[xb])
        P.dma("act", acc_v[:, :, t0:t0 + TA],
              xb[:].rearrange("p (c t) -> p c t", c=8), acc_h,
              src_bufs=[xb], name="accst")


    a2b_xstage(0)
    for ti in range(NTA):
        a2b_mms(ti)
        if ti + 1 < NTA:
            a2b_xstage(ti + 1)
        a2b_acc(ti)

    h1 = sb("h1_sb", [128, 32 * TF], BF16, 32768)
    h2 = sb("h2_sb", [128, 32 * TF], BF16, 65536)
    kst = [sb("kst0", [128, 8192], BF16, 98304),
           sb("kst1", [128, 8192], BF16, 114688),
           sb("kst2", [128, 8192], BF16, G0)]
    accin = sb("accin", [128, 8 * TF], F32, 131072)
    outb = sb("outb", [128, 8 * TF], F32, 147456)

    def srow_build(ps, src_h, hi_nm, lo_nm, f_nm):
        P.emit("pe", [
            (lambda e, kc=kc, ps=ps: e.matmul(
                ps[0:1, 0:TF], lhsT=ones[:, 0:1],
                rhs=src_h[:, kc * TF:(kc + 1) * TF],
                start=(kc == 0), stop=(kc == 31)))
            for kc in range(32)], reads=[ones, src_h], writes=[ps])
        P.emit("dve", lambda e, ps=ps: e.tensor_copy(rowf(f_nm), ps[0:1, 0:TF]),
               reads=[ps], writes=[rowsf])
        P.emit("dve", lambda e: e.tensor_copy(row(hi_nm, TF), rowf(f_nm)),
               reads=[rowsf], writes=[rows])
        P.emit("dve", lambda e: e.tensor_tensor(rowf("scr"), rowf(f_nm), row(hi_nm, TF), op=ALU.subtract),
               reads=[rowsf, rows], writes=[rowsf])
        P.emit("dve", lambda e: e.tensor_copy(row(lo_nm, TF), rowf("scr")),
               reads=[rowsf], writes=[rows])

    for tj in range(NTF):
        t0 = tj * TF
        abv = accin[:].rearrange("p (c t) -> p c t", c=8)
        P.dma("pool", abv, acc_v[:, :, t0:t0 + TF], accin, src_bufs=[acc_h],
              name="accin")
        for g in range(16):
            kb = kst[g % 3]
            P.dma("sp", kb[:, 0:2048].rearrange("p (k n) -> p k n", k=8),
                  k0dbf[g].rearrange("k p n -> p k n"), kb,
                  src_bufs=[k0dbf_h], name=f"k0s{g % 2}")
            ps = PS[g % 4]
            fns = []
            for nco in range(2):
                for kc in range(8):
                    fns.append(lambda e, nco=nco, kc=kc, ps=ps, kb=kb, t0=t0:
                               e.matmul(ps[:, nco * TF:(nco + 1) * TF],
                                        lhsT=kb[:, kc * 256 + nco * 128:kc * 256 + nco * 128 + 128],
                                        rhs=xn_v[:, kc, t0:t0 + TF],
                                        start=(kc == 0), stop=(kc == 7)))
            P.emit("pe", fns, reads=[kb, xn_sb], writes=[ps])
            P.emit("act", lambda e, g=g, ps=ps: e.activation(
                h1[:, g * 1024:(g + 1) * 1024], ps[:], AF.Relu, scale=sgl("two_s0")),
                reads=[ps, cols], writes=[h1])
        srow_build(PS[0], h1, "s1hi", "s1lo", "s1")
        for g in range(16):
            kb = kst[g % 3]
            P.dma("sp", kb[:].rearrange("p (k n) -> p k n", k=32),
                  k1bf[g].rearrange("k p n -> p k n"), kb,
                  src_bufs=[k1bf_h], name=f"k1s{g % 3}")
            ps = PS[g % 4]
            fns = []
            for nco in range(2):
                for kc in range(32):
                    fns.append(lambda e, nco=nco, kc=kc, ps=ps, kb=kb:
                               e.matmul(ps[:, nco * TF:(nco + 1) * TF],
                                        lhsT=kb[:, kc * 256 + nco * 128:kc * 256 + nco * 128 + 128],
                                        rhs=h1[:, kc * TF:(kc + 1) * TF],
                                        start=(kc == 0), stop=False))
            P.emit("pe", fns, reads=[kb, h1], writes=[ps])
            fns = []
            for nco in range(2):
                fns.append(lambda e, nco=nco, ps=ps: e.matmul(
                    ps[:, nco * TF:(nco + 1) * TF], lhsT=row("mh0", 128),
                    rhs=row("s1hi", TF), start=False, stop=False))
                fns.append(lambda e, nco=nco, ps=ps: e.matmul(
                    ps[:, nco * TF:(nco + 1) * TF], lhsT=row("mh0", 128),
                    rhs=row("s1lo", TF), start=False, stop=True))
            P.emit("pe", fns, reads=[rows], writes=[ps])
            P.emit("act", lambda e, g=g, ps=ps: e.activation(
                h2[:, g * 1024:(g + 1) * 1024], ps[:], AF.Relu, scale=sgl("four_s1")),
                reads=[ps, cols], writes=[h2])
        srow_build(PS[0], h2, "s2hi", "s2lo", "s2")
        for g in range(4):
            kb = kst[1 + g % 2]
            P.dma("sp", kb[:].rearrange("p (k n) -> p k n", k=32),
                  k2bf[g].rearrange("k p n -> p k n"), kb,
                  src_bufs=[k2bf_h], name=f"k2s{g % 3}")
            ps = PS[g % 4]
            fns = []
            for nco in range(2):
                for kc in range(32):
                    fns.append(lambda e, nco=nco, kc=kc, ps=ps, kb=kb:
                               e.matmul(ps[:, nco * TF:(nco + 1) * TF],
                                        lhsT=kb[:, kc * 256 + nco * 128:kc * 256 + nco * 128 + 128],
                                        rhs=h2[:, kc * TF:(kc + 1) * TF],
                                        start=(kc == 0), stop=False))
            P.emit("pe", fns, reads=[kb, h2], writes=[ps])
            fns = []
            for nco in range(2):
                fns.append(lambda e, nco=nco, ps=ps: e.matmul(
                    ps[:, nco * TF:(nco + 1) * TF], lhsT=row("mh0", 128),
                    rhs=row("s2hi", TF), start=False, stop=False))
                fns.append(lambda e, nco=nco, ps=ps: e.matmul(
                    ps[:, nco * TF:(nco + 1) * TF], lhsT=row("mh0", 128),
                    rhs=row("s2lo", TF), start=False, stop=True))
            P.emit("pe", fns, reads=[rows], writes=[ps])
            P.emit("act", lambda e, g=g, ps=ps: e.activation(
                outb[:, g * 1024:(g + 1) * 1024], ps[:], AF.Identity,
                scale=sgl("four_s2")), reads=[ps, cols], writes=[outb])
        P.emit("dve", lambda e: e.tensor_tensor(outb[:], outb[:], accin[:], op=ALU.add),
               reads=[outb, accin], writes=[outb])
        P.dma("act", out_v[:, :, t0:t0 + TF],
              outb[:].rearrange("p (c t) -> p c t", c=8), None,
              src_bufs=[outb], name="outst")
        if debug and tj == 0:
            P.dma("act", dbg["h1"], h1[:], None, src_bufs=[h1], name="dbgh1")
            P.dma("act", dbg["h2"], h2[:], None, src_bufs=[h2], name="dbgh2")

    P.wait_all()
    P.play()
    return nc


def _prep_shared(inputs):
    f = np.ascontiguousarray
    d = {}
    for k, nm in (("k", "Wk"), ("v", "Wv"), ("q", "Wq"), ("kp", "Wkp"),
                  ("qp", "Wqp"), ("o", "Wo")):
        w = np.asarray(inputs[f"L0_attn_{nm}"], dtype=np.float32)
        d[f"w_{k}"] = f(w.reshape(8, 128, 8, 128).transpose(1, 0, 2, 3)
                        .reshape(128, 8192))
    for k, nm in (("v", "bv"), ("kp", "bkp"), ("qp", "bqp"), ("o", "bo")):
        d[f"b_{k}"] = f(np.asarray(inputs[f"L0_attn_{nm}"],
                                   dtype=np.float32).reshape(1, D))
    for nm, key in (("Wk", "wkT"), ("Wq", "wqT")):
        w = np.asarray(inputs[f"L0_attn_{nm}"], dtype=np.float32).T
        d[key] = f(w.reshape(8, 128, 8, 128).transpose(1, 0, 2, 3)
                   .reshape(128, 8192))
    d["bkc"] = f(np.asarray(inputs["L0_attn_bk"], dtype=np.float32)
                 .reshape(8, 128).T)
    d["bqc"] = f(np.asarray(inputs["L0_attn_bq"], dtype=np.float32)
                 .reshape(8, 128).T)
    d["ps_r"] = f(np.asarray(inputs["L0_attn_phase_scale"],
                             dtype=np.float32).reshape(1, D))
    d["lng_c"] = f(np.asarray(inputs["L0_attn_ln_g"],
                              dtype=np.float32).reshape(8, 128).T)
    d["lnb_c"] = f(np.asarray(inputs["L0_attn_ln_b"],
                              dtype=np.float32).reshape(8, 128).T)
    k0 = np.asarray(inputs["L0_ffn_k0"]).view(np.uint8)
    d["k0t"] = f(k0[:D].reshape(8, 128, DF).transpose(1, 0, 2).reshape(128, 8 * DF))
    d["k0b"] = f(k0[D:].reshape(8, 128, DF).transpose(1, 0, 2).reshape(128, 8 * DF))
    k1 = np.asarray(inputs["L0_ffn_k1"]).view(np.uint8)
    d["k1t"] = f(k1.reshape(32, 128, 16, 256).transpose(2, 0, 1, 3))
    k2 = np.asarray(inputs["L0_ffn_k2"]).view(np.uint8)
    d["k2t"] = f(k2.reshape(32, 128, 4, 256).transpose(2, 0, 1, 3))
    for i in (0, 1, 2):
        d[f"s{i}_c"] = np.full((128, 1), np.asarray(inputs[f"L0_ffn_s{i}"],
                                                    dtype=np.float32))
    import ml_dtypes
    p = np.arange(128)[:, None]; q = np.arange(128)[None, :]
    d["u128_c"] = (((p <= q) & ((p < 64) == (q < 64)))
                   .astype(ml_dtypes.bfloat16))
    d["ident_c"] = np.eye(128, dtype=np.float32)
    return d


def kernel(**inputs):
    nc = build(debug=False)
    shared = _prep_shared(inputs)
    x = np.asarray(inputs["L0_attn_x"], dtype=np.float32).reshape(S, D)
    in_maps = []
    for c in range(NCORE):
        xs = x[c * TOK:(c + 1) * TOK]
        m = dict(shared)
        m["x_tm"] = np.ascontiguousarray(xs)
        m["x_fm"] = np.ascontiguousarray(xs.T)
        in_maps.append(m)
    res = run_bass_kernel_spmd(nc, in_maps, core_ids=list(range(NCORE)))
    out = np.concatenate([r["out_fm"].T for r in res.results], axis=0)
    return out.reshape(1, S, D).astype(np.float32)


# version 32
# speedup vs baseline: 1.0268x; 1.0058x over previous
import math

import numpy as np

import concourse.bass as bass
import concourse.mybir as mybir
from concourse.bass_utils import run_bass_kernel_spmd

F32 = mybir.dt.float32
BF16 = mybir.dt.bfloat16
U8 = mybir.dt.uint8
I16 = mybir.dt.int16
I32 = mybir.dt.int32
AF = mybir.ActivationFunctionType
ALU = mybir.AluOpType
PI = math.pi
INV2PI = 1.0 / (2.0 * PI)
TWOPI = 2.0 * PI
TP_HI = float(np.float32(2 * np.pi))
TP_LO = float(np.float32(2 * np.pi - np.float64(np.float32(2 * np.pi))))
HP_HI = float(np.float32(np.pi / 2))
HP_LO = float(np.float32(np.pi / 2 - np.float64(np.float32(np.pi / 2))))

NCORE = 8
S, D = 16384, 1024
TOK = S // NCORE
NG = TOK // 128
TA = 256
NTA = TOK // TA
TF = 512
NTF = TOK // TF
DF = 4096

SBUF_BASE = 16512
SBUF_TOP = 229344
ENGS = ("pe", "act", "dve", "pool", "sp")
DT_SZ = {F32: 4, BF16: 2, U8: 1, I16: 2, I32: 4}


class Prog:

    def __init__(self, nc):
        self.nc = nc
        self.ops = {e: [] for e in ENGS}
        self.esem = {e: nc.alloc_semaphore(f"sem_{e}") for e in ENGS}
        self.ecnt = {e: 0 for e in ENGS}
        self.water = {e: {} for e in ENGS}
        self.bufs = {}
        self.dsem = {}
        self.sems = dict(self.esem)

    def reg(self, h, lo=None, size=None):
        self.bufs[id(h)] = dict(h=h, iv=(lo, lo + size) if lo is not None else None,
                                w={}, r=[])
        return h

    def sballoc(self, name, shape, dtype, off):
        per = int(np.prod(shape[1:])) * DT_SZ[dtype]
        assert SBUF_BASE + off + per <= SBUF_TOP, (name, off, per)
        h = self.nc.alloc_sbuf_tensor_at(name, list(shape), dtype,
                                         offset=SBUF_BASE + off)
        return self.reg(h, off, per)

    def _overlaps(self, b):
        me = self.bufs[id(b)]
        if me["iv"] is None:
            return [me]
        lo, hi = me["iv"]
        return [m for m in self.bufs.values()
                if m["iv"] is not None and m["iv"][0] < hi and lo < m["iv"][1]]

    def _need(self, eng, mark):
        if mark is None:
            return None
        k, v = mark
        if k == eng == "pe":
            return None
        if self.water[eng].get(k, 0) >= v:
            return None
        return mark

    def _collect(self, eng, reads, writes):
        need = {}

        def add(mark):
            m = self._need(eng, mark)
            if m:
                need[m[0]] = max(need.get(m[0], 0), m[1])

        for b in reads:
            if id(b) in self.bufs:
                for m in self._overlaps(b):
                    for kv in m["w"].items():
                        add(kv)
        for b in writes:
            if id(b) in self.bufs:
                for m in self._overlaps(b):
                    for kv in m["w"].items():
                        add(kv)
                    for rr in m["r"]:
                        add(rr)
        return need

    def _emit_waits(self, eng, need):
        for k, v in sorted(need.items(), key=lambda kv: str(kv[0])):
            self.water[eng][k] = v
            sem = self.sems[k]
            self.ops[eng].append(lambda e, s=sem, vv=v: e.wait_ge(s, vv))

    def _mark(self, reads, writes, semkey, val):
        for b in writes:
            if id(b) not in self.bufs:
                self.reg(b)
            m = self.bufs[id(b)]
            m["w"][semkey] = val
            m["r"] = [r for r in m["r"] if r[0] != semkey or r[1] > val]
        for b in reads:
            if id(b) in self.bufs:
                self.bufs[id(b)]["r"].append((semkey, val))

    def emit(self, eng, fns, reads=(), writes=()):
        if not isinstance(fns, (list, tuple)):
            fns = [fns]
        need = self._collect(eng, reads, writes)
        self._emit_waits(eng, need)
        sem = self.esem[eng]
        for f in fns[:-1]:
            self.ops[eng].append(f)
        self.ecnt[eng] += 1
        val = self.ecnt[eng]
        self.ops[eng].append(lambda e, f=fns[-1], s=sem: f(e).then_inc(s, 1))
        self._mark(reads, writes, eng, val)
        return (eng, val)

    def dma(self, queue, dst, src, dst_buf, src_bufs=(), name="d", after=(),
            **kw):
        key = id(dst_buf) if dst_buf is not None else ("anon", len(self.dsem))
        if key not in self.dsem:
            h = self.nc.alloc_semaphore(f"sd_{name}_{len(self.dsem)}")
            self.dsem[key] = [h, 0]
            self.sems[("d", key)] = h
        ent = self.dsem[key]
        writes = [dst_buf] if dst_buf is not None else []
        need = self._collect(queue, list(src_bufs) + list(after), writes)
        m = self._need(queue, (("d", key), ent[1])) if ent[1] > 0 else None
        if m:
            need[m[0]] = max(need.get(m[0], 0), m[1])
        self._emit_waits(queue, need)
        ent[1] += 16
        total, h = ent[1], ent[0]
        self.ops[queue].append(
            lambda e, d=dst, s=src, hh=h, kws=kw: e.dma_start(d, s, **kws).then_inc(hh, 16))
        self._mark(list(src_bufs), writes, ("d", key), total)
        return (("d", key), total)

    def wait_all(self, engs=ENGS):
        marks = [(e, self.ecnt[e]) for e in ENGS if self.ecnt[e] > 0]
        marks += [(("d", k), v[1]) for k, v in self.dsem.items() if v[1] > 0]
        for eng in engs:
            need = {}
            for k, v in marks:
                m = self._need(eng, (k, v))
                if m:
                    need[m[0]] = max(need.get(m[0], 0), m[1])
            self._emit_waits(eng, need)

    def play(self):
        with self.nc.Block() as block:
            block.tensor(lambda e: [f(e) for f in self.ops["pe"]])
            block.scalar(lambda e: [f(e) for f in self.ops["act"]])
            block.vector(lambda e: [f(e) for f in self.ops["dve"]])
            block.gpsimd(lambda e: [f(e) for f in self.ops["pool"]])
            block.sync(lambda e: [f(e) for f in self.ops["sp"]])


def build(debug=False, stop_after=None):
    nc = bass.Bass("TRN2", target_bir_lowering=False)
    P = Prog(nc)

    def din(name, shape, dtype):
        return nc.dram_tensor(name, list(shape), dtype, kind="ExternalInput").ap()

    x_fm = din("x_fm", [D, TOK], F32)
    x_tm = din("x_tm", [TOK, D], F32)
    w_in = {k: din(f"w_{k}", [128, 8192], F32)
            for k in ("k", "kp", "v", "q", "qp", "o")}
    k0t_d = din("k0t", [128, 8 * DF], U8)
    k0b_d = din("k0b", [128, 8 * DF], U8)
    k1_d = din("k1t", [16, 32, 128, 256], U8)
    k2_d = din("k2t", [4, 32, 128, 256], U8)
    brow_in = {k: din(f"b_{k}", [1, D], F32)
               for k in ("v", "kp", "qp")}
    wkT_d = din("wkT", [128, 8192], F32)
    wqT_d = din("wqT", [128, 8192], F32)
    bkc_d = din("bkc", [128, 8], F32)
    bqc_d = din("bqc", [128, 8], F32)
    bo_d = din("b_o", [1, D], F32)
    ps_d = din("ps_r", [1, D], F32)
    lng_d = din("lng_c", [128, 8], F32)
    lnb_d = din("lnb_c", [128, 8], F32)
    s_d = {k: din(f"s{k}_c", [128, 1], F32) for k in (0, 1, 2)}
    u128_d = din("u128_c", [128, 128], BF16)
    ident_d = din("ident_c", [128, 128], F32)

    out_d = nc.dram_tensor("out_fm", [D, TOK], F32, kind="ExternalOutput").ap()
    dbg = {}
    if debug:
        for nm, shp in (("xn", [128, 8 * TOK]), ("ret", [128, NG * D]),
                        ("h1", [128, 32 * TF]), ("h2", [128, 32 * TF])):
            dbg[nm] = nc.dram_tensor("dbg_" + nm, shp, BF16,
                                     kind="ExternalOutput").ap()
        dbg["acc"] = nc.dram_tensor("dbg_acc", [D, TOK], F32,
                                    kind="ExternalOutput").ap()
        for nm in ("br", "bi", "sq", "cosq"):
            dbg[nm] = nc.dram_tensor("dbg_" + nm, [128, 1024], BF16,
                                     kind="ExternalOutput").ap()
        for nm in ("memr", "memi"):
            dbg[nm] = nc.dram_tensor("dbg_" + nm, [128, 1024], BF16,
                                     kind="ExternalOutput").ap()
        dbg["u128"] = nc.dram_tensor("dbg_u128", [128, 128], BF16,
                                     kind="ExternalOutput").ap()

    def dint(name, shape, dtype):
        return nc.dram_tensor(name, list(shape), dtype, kind="Internal").ap()

    k1bf = dint("k1bf", [16, 32, 128, 256], BF16)
    k2bf = dint("k2bf", [4, 32, 128, 256], BF16)
    k0dbf = dint("k0dbf", [16, 8, 128, 256], BF16)
    acc_d = dint("acc_d", [D, TOK], F32)
    k1bf_h = P.reg(object())
    k2bf_h = P.reg(object())
    k0dbf_h = P.reg(object())
    acc_h = P.reg(object())

    O_XN = 0
    O_W = {k: 32768 + i * 16384 for i, k in enumerate(("k", "kp", "v", "q", "qp"))}
    O_WO = 114688
    O_RET = 131072
    O_ROWS = 163840
    O_ROWSF = 176128
    E_END = 180224
    O_ONES = 180224
    O_PSD = 180736
    O_U128 = 184832
    O_IDENT = 185152
    O_COLS = 185664
    O_LNB = 186688
    G0 = 186816
    GTOP = SBUF_TOP - SBUF_BASE
    assert G0 + 24576 <= GTOP, (G0, GTOP)

    sb = P.sballoc
    xn_sb = sb("xn_sb", [128, 8 * TOK], BF16, O_XN)
    wsb = {k: sb(f"w{k}_sb", [128, 8192], BF16, O_W[k]) for k in O_W}
    wsb["o"] = sb("wo_sb", [128, 8192], BF16, O_WO)
    ret_sb = sb("ret_sb", [128, NG * D], BF16, O_RET)
    rows = sb("rows_sb", [128, 6144], BF16, O_ROWS)
    rowsf = sb("rowsf_sb", [128, 1024], F32, O_ROWSF)
    ones = sb("ones_sb", [128, 256], BF16, O_ONES)
    psd = sb("psd_sb", [128, 1024], F32, O_PSD)
    u128 = sb("u128_sb", [128, 128], BF16, O_U128)
    ident = sb("ident_sb", [128, 128], F32, O_IDENT)
    cols = sb("cols_sb", [128, 256], F32, O_COLS)
    lnbc = sb("lnbc_sb", [128, 8], BF16, O_LNB)
    bkcs = sb("bkc_sb", [128, 8], BF16, O_LNB + 32)
    bqcs = sb("bqc_sb", [128, 8], BF16, O_LNB + 64)

    RP = {"bkp": (0, 0), "bqp": (0, 1024), "mh0": (0, 2048),
          "bv": (0, 3072),
          "s1hi": (0, 3072), "s1lo": (0, 3584),
          "s2hi": (0, 4096), "s2lo": (0, 4608),
          "bwo": (0, 5120)}

    def row(nm, n=1024, c0=0):
        p, c = RP[nm]
        return rows[p:p + 1, c + c0:c + c0 + n]

    RF = {"s1": (0, 0, 512), "s2": (0, 512, 512), "scr": (32, 0, 512)}

    def rowf(nm):
        p, c, n = RF[nm]
        return rowsf[p:p + 1, c:c + n]

    CP = dict(sum1=0, sq1=16, mu1=32, rstd1=48, sum2=64, sq2=80, mu2=96,
              rstd2=112, scrA=128, scrB=144, lng=160, singles=176,
              sqa=192, sqb=208)
    SGL = dict(two_s0=176, four_s1=177, four_s2=178, hpi=179)

    def colg(nm, g):
        return cols[:, CP[nm] + g:CP[nm] + g + 1]

    def colr(nm, n=16):
        return cols[:, CP[nm]:CP[nm] + n]

    def sgl(nm):
        return cols[:, SGL[nm]:SGL[nm] + 1]

    PS = [P.reg(nc.alloc_psum_tensor(f"ps{i}", [128, 1024], F32))
          for i in range(4)]

    xn_v = xn_sb[:].rearrange("p (c t) -> p c t", c=8)
    xng = [P.reg(object(), lo=O_XN + g * 256, size=256) for g in range(NG)]
    x_tmf = sb("xtm_sb", [128, NG * D], F32, O_WO)
    xtm_v = x_tmf[:].rearrange("p (g f) -> p g f", g=NG)
    XSLOT = [NG - 1 - g for g in range(NG)]
    xtmr = [P.reg(object(), lo=O_WO + XSLOT[g] * 4096, size=4096)
            for g in range(NG)]
    x_fm_v = x_fm.rearrange("(c p) t -> p c t", p=128)
    acc_v = acc_d.rearrange("(c p) t -> p c t", p=128)
    out_v = out_d.rearrange("(c p) t -> p c t", p=128)
    ret_v = ret_sb[:].rearrange("p (g f) -> p g f", g=NG)
    RSLOT = [NG - 1 - g for g in range(NG)]
    retr = [P.reg(object(), lo=O_RET + sl * 2048, size=2048) for sl in range(NG)]

    g_off = G0

    def ga(name, shape, dtype, off=None):
        nonlocal g_off
        if off is None:
            off = g_off
            per = int(np.prod(shape[1:])) * DT_SZ[dtype]
            g_off += (per + 63) & ~63
        assert off + int(np.prod(shape[1:])) * DT_SZ[dtype] <= GTOP
        return sb(name, shape, dtype, off)

    scs = {k: ga(f"sc{k}", [128, 1], F32) for k in (0, 1, 2)}
    p0r = ga("p0r", [128, 1024], F32)
    st_a = sb("st_a", [128, DF], BF16, O_WO)
    st_b = sb("st_b", [128, DF], BF16, O_WO + 8192)
    st_dd = sb("st_dd", [128, DF], BF16, O_WO + 16384)
    psc_a = sb("psc_a", [128, 2048], BF16, O_WO + 24576)
    psc_f = sb("psc_f", [128, 2048], F32, O_WO + 28672)

    P.dma("sp", u128[:], u128_d, u128, name="u128")
    P.dma("sp", ident[:], ident_d, ident, name="ident")
    P.emit("dve", lambda e: e.memset(ones[:], 1.0), writes=[ones])
    P.emit("dve", lambda e: e.memset(sgl("hpi"), PI / 2.0), writes=[cols])

    P.dma("pool", lnbc[:], lnb_d, lnbc, name="lnb")
    P.dma("pool", bkcs[:], bkc_d, bkcs, name="bkc")
    P.dma("pool", bqcs[:], bqc_d, bqcs, name="bqc")
    stK = sb("stK", [128, 8192], BF16, O_WO)
    for wtd, wpname, dstname in ((wkT_d, "kp", "k"), (wqT_d, "qp", "q")):
        P.dma("pool", wsb[wpname][:], w_in[wpname], wsb[wpname],
              name=f"w{wpname}", max_dma_last_dim=8192)
        P.dma("pool", stK[:], wtd, stK, name="wT", max_dma_last_dim=8192)
        for dc in range(8):
            for h in range(2):
                ps = PS[(dc * 2 + h) % 4]
                fns = [
                    (lambda e, kfc=kfc, dc=dc, h=h, ps=ps, wp=wpname:
                     e.matmul(ps[:, 0:512],
                              lhsT=stK[:, kfc * 1024 + dc * 128:kfc * 1024 + dc * 128 + 128],
                              rhs=wsb[wp][:, kfc * 1024 + h * 512:kfc * 1024 + h * 512 + 512],
                              start=(kfc == 0), stop=(kfc == 7)))
                    for kfc in range(8)]
                P.emit("pe", fns, reads=[stK, wsb[wpname]], writes=[ps])
                P.emit("act", lambda e, dc=dc, h=h, ps=ps, dn=dstname: e.activation(
                    wsb[dn][:, dc * 1024 + h * 512:dc * 1024 + h * 512 + 512],
                    ps[:, 0:512], AF.Identity), reads=[ps], writes=[wsb[dstname]])
    for k in (0, 1, 2):
        P.dma("sp", scs[k][:], s_d[k], scs[k], name=f"s{k}")
    P.dma("sp", colr("lng", 8), lng_d, cols, name="lng")
    P.emit("dve", lambda e: e.tensor_scalar(sgl("two_s0"), scs[0][:], 2.0, None, op0=ALU.mult),
           reads=[scs[0]], writes=[cols])
    P.emit("dve", lambda e: e.tensor_scalar(sgl("four_s1"), scs[1][:], 4.0, None, op0=ALU.mult),
           reads=[scs[1]], writes=[cols])
    P.emit("dve", lambda e: e.tensor_scalar(sgl("four_s2"), scs[2][:], 4.0, None, op0=ALU.mult),
           reads=[scs[2]], writes=[cols])

    P.dma("pool", wsb["v"][:], w_in["v"], wsb["v"], name="wv",
          max_dma_last_dim=8192)
    P.dma("sp", psc_f[0:1, 0:1024], ps_d, psc_f, name="psr")
    P.emit("dve", lambda e: e.tensor_copy(psc_a[0:1, 0:1024], psc_f[0:1, 0:1024]),
           reads=[psc_f], writes=[psc_a])
    P.emit("dve", lambda e: e.tensor_tensor(psc_f[0:1, 1024:2048], psc_f[0:1, 0:1024], psc_a[0:1, 0:1024], op=ALU.subtract),
           reads=[psc_f, psc_a], writes=[psc_f])
    P.emit("dve", lambda e: e.tensor_copy(psc_a[0:1, 1024:2048], psc_f[0:1, 1024:2048]),
           reads=[psc_f], writes=[psc_a])
    P.emit("pe", [
        lambda e: e.matmul(PS[0][:, 0:512], lhsT=ones[0:1, 0:128], rhs=psc_a[0:1, 0:512], start=True, stop=False),
        lambda e: e.matmul(PS[0][:, 0:512], lhsT=ones[0:1, 0:128], rhs=psc_a[0:1, 1024:1536], start=False, stop=True),
        lambda e: e.matmul(PS[0][:, 512:1024], lhsT=ones[0:1, 0:128], rhs=psc_a[0:1, 512:1024], start=True, stop=False),
        lambda e: e.matmul(PS[0][:, 512:1024], lhsT=ones[0:1, 0:128], rhs=psc_a[0:1, 1536:2048], start=False, stop=True),
    ], reads=[ones, psc_a], writes=[PS[0]])
    P.emit("dve", lambda e: e.tensor_scalar(psd[:], PS[0][:], 0.5, None, op0=ALU.mult),
           reads=[PS[0]], writes=[psd])

    def bias_builds():
        P.dma("sp", p0r[0:1, 0:1024], brow_in["v"], p0r, name="brow")
        P.emit("dve", lambda e: e.tensor_copy(row("bv"), p0r[0:1, 0:1024]),
               reads=[p0r], writes=[rows])
        for bnm, bc, wpname, slot in (("kp", bkcs, "kp", "bkp"),
                                      ("qp", bqcs, "qp", "bqp")):
            P.dma("sp", p0r[0:1, 0:1024], brow_in[bnm], p0r, name="brow")
            fns = []
            for h in range(2):
                for kfc in range(8):
                    fns.append(lambda e, kfc=kfc, h=h, bc=bc, wp=wpname: e.matmul(
                        PS[3][0:1, h * 512:h * 512 + 512],
                        lhsT=bc[:, kfc:kfc + 1],
                        rhs=wsb[wp][:, kfc * 1024 + h * 512:kfc * 1024 + h * 512 + 512],
                        start=(kfc == 0), stop=(kfc == 7)))
            P.emit("pe", fns, reads=[bc, wsb[wpname]], writes=[PS[3]])
            P.emit("dve", lambda e, slot=slot: e.tensor_tensor(
                row(slot), PS[3][0:1, 0:1024], p0r[0:1, 0:1024], op=ALU.add),
                reads=[PS[3], p0r], writes=[rows])
        P.emit("dve", lambda e: e.memset(row("mh0", 128), -0.5), writes=[rows])

    a0_scr = [ga("a0s0", [128, 1024], BF16), ga("a0s1", [128, 1024], BF16)]
    a0_xn = ga("a0xn", [128, 1024], F32)
    for g in range(NG):
        P.dma("sp", xtm_v[:, XSLOT[g], :], x_tm[g * 128:(g + 1) * 128, :],
              xtmr[g], name=f"xt{g}", after=(stK,) if g == 0 else ())
    for g in range(NG):
        sc = a0_scr[g % 2]
        P.emit("dve", lambda e, g=g, s=sc: e.tensor_scalar(
            s[:], xtm_v[:, XSLOT[g], :], 1.0, 0.0, op0=ALU.mult, op1=ALU.add,
            accum_out=colg("sum1", g)), reads=[xtmr[g]], writes=[sc, cols])
        P.emit("dve", lambda e, g=g, s=sc: e.scalar_tensor_tensor(
            s[:], xtm_v[:, XSLOT[g], :], 1.0, xtm_v[:, XSLOT[g], :],
            op0=ALU.mult, op1=ALU.mult, accum_out=colg("sq1", g)),
            reads=[xtmr[g]], writes=[sc, cols])
    P.emit("dve", lambda e: e.tensor_scalar(colr("mu1"), colr("sum1"), 1.0 / D, None, op0=ALU.mult),
           reads=[cols], writes=[cols])
    P.emit("dve", lambda e: e.tensor_scalar(colr("scrA"), colr("sq1"), 1.0 / D, None, op0=ALU.mult),
           reads=[cols], writes=[cols])
    P.emit("dve", lambda e: e.tensor_tensor(colr("scrB"), colr("mu1"), colr("mu1"), op=ALU.mult),
           reads=[cols], writes=[cols])
    P.emit("dve", lambda e: e.tensor_tensor(colr("scrA"), colr("scrA"), colr("scrB"), op=ALU.subtract),
           reads=[cols], writes=[cols])
    P.emit("dve", lambda e: e.tensor_scalar(colr("scrA"), colr("scrA"), 1.0, 1e-5, op0=ALU.mult, op1=ALU.add),
           reads=[cols], writes=[cols])
    P.emit("act", lambda e: e.activation(colr("scrB"), colr("scrA"), AF.Ln),
           reads=[cols], writes=[cols])
    P.emit("act", lambda e: e.activation(colr("rstd1"), colr("scrB"), AF.Exp, scale=-0.5),
           reads=[cols], writes=[cols])
    for g in range(NG):
        P.emit("dve", lambda e, g=g: e.tensor_scalar(
            a0_xn[:], xtm_v[:, XSLOT[g], :], colg("mu1", g), colg("rstd1", g),
            op0=ALU.subtract, op1=ALU.mult), reads=[xtmr[g], cols], writes=[a0_xn])
        ps = PS[g % 2]
        P.emit("pe", [
            (lambda e, g=g, c=c, ps=ps: e.matmul(
                ps[:, c * 128:(c + 1) * 128],
                lhsT=a0_xn[:, c * 128:(c + 1) * 128], rhs=ident[:],
                is_transpose=True, start=True, stop=True,
                )) for c in range(8)],
            reads=[a0_xn, ident], writes=[ps])
        P.emit("dve", lambda e, g=g, ps=ps: e.tensor_copy(
            xn_v[:, :, g * 128:(g + 1) * 128],
            ps[:].rearrange("p (c t) -> p c t", c=8)),
            reads=[ps], writes=[xng[g]])
        if g == 3:
            bias_builds()
    if debug:
        P.dma("act", dbg["xn"], xn_sb[:], None, src_bufs=[xn_sb], name="dbgxn")

    if stop_after == "a0":
        P.wait_all(); P.play(); return nc
    for kc in range(8):
        P.dma("pool", st_a[:], k0t_d[:, kc * DF:(kc + 1) * DF], st_a,
              name="k0a", max_dma_last_dim=2048)
        P.dma("pool", st_b[:], k0b_d[:, kc * DF:(kc + 1) * DF], st_b,
              name="k0b", max_dma_last_dim=2048)
        P.emit("pool", lambda e: e.tensor_tensor(st_dd[:], st_a[:], st_b[:], op=ALU.subtract),
               reads=[st_a, st_b], writes=[st_dd])
        P.dma("sp", k0dbf[:, kc, :, :].rearrange("g p n -> p g n"),
              st_dd[:].rearrange("p (g n) -> p g n", g=16), k0dbf_h,
              src_bufs=[st_dd], name="k0d")

    if stop_after == "p0":
        P.wait_all(); P.play(); return nc


    P.dma("pool", wsb["o"][:], w_in["o"], wsb["o"], name="wo",
          max_dma_last_dim=8192)
    P.dma("pool", k1bf[0:8], k1_d[0:8], k1bf_h, name="k1bf", max_dma_last_dim=2048)
    P.dma("pool", k1bf[8:16], k1_d[8:16], k1bf_h, name="k1bf", max_dma_last_dim=2048)
    P.dma("pool", k2bf, k2_d, k2bf_h, name="k2bf", max_dma_last_dim=2048)

    ph_ = [ga("ph0", [128, 1024], F32, off=G0),
           ga("ph1", [128, 1024], F32, off=G0 + 4096)]
    wA_ = [ga("wA0", [128, 1024], BF16, off=G0 + 8192),
           ga("wA1", [128, 1024], BF16, off=G0 + 10240)]
    wB_ = [ga("wB0", [128, 1024], BF16, off=G0 + 12288),
           ga("wB1", [128, 1024], BF16, off=G0 + 14336)]
    brb = ga("brb", [128, 1024], BF16, off=G0 + 16384)
    bib = ga("bib", [128, 1024], BF16, off=G0 + 18432)
    cb = ga("cb", [128, 1024], F32, off=G0 + 20480)

    def tmproj_xn(wname, bslot, gg, ps):
        fns = []
        pb, c = RP[bslot]
        for h2 in range(2):
            for kc in range(8):
                fns.append(lambda e, kc=kc, h2=h2, ps=ps, gg=gg, wn=wname:
                           e.matmul(ps[:, h2 * 512:(h2 + 1) * 512],
                                    lhsT=xn_v[:, kc, gg * 128:gg * 128 + 128],
                                    rhs=wsb[wn][:, kc * 1024 + h2 * 512:kc * 1024 + (h2 + 1) * 512],
                                    start=(kc == 0), stop=False))
            fns.append(lambda e, h2=h2, ps=ps, pb=pb, c=c:
                       e.matmul(ps[:, h2 * 512:(h2 + 1) * 512],
                                lhsT=ones[pb:pb + 1, 0:128],
                                rhs=rows[pb:pb + 1, c + h2 * 512:c + (h2 + 1) * 512],
                                start=False, stop=True))
        P.emit("pe", fns, reads=[xn_sb, wsb[wname], rows, ones, xng[gg]],
               writes=[ps])

    def trig_ops(ps_src, par):
        ph, wA, wB = ph_[par], wA_[par], wB_[par]
        return [
            ("act", (lambda e, p=ps_src, ph=ph: e.activation(ph[:], p[:], AF.Tanh)), [ps_src], [ph]),
            ("dve", (lambda e, ph=ph: e.tensor_tensor(ph[:], ph[:], psd[:], op=ALU.mult)), [ph, psd], [ph]),
            ("act", (lambda e, ph=ph, wA=wA: e.activation(wA[:], ph[:], AF.Sin)), [ph], [wA]),
            ("act", (lambda e, ph=ph: e.activation(ph[:], ph[:], AF.Abs)), [ph], [ph]),
            ("dve", (lambda e, ph=ph: e.tensor_scalar(ph[:], ph[:], HP_HI, HP_LO, op0=ALU.subtract, op1=ALU.subtract)), [ph], [ph]),
            ("act", (lambda e, ph=ph, wB=wB: e.activation(wB[:], ph[:], AF.Sin, scale=-1.0)), [ph], [wB]),
            ("act", (lambda e, ph=ph, wA=wA: e.activation(ph[:], wA[:], AF.Square)), [wA], [ph]),
            ("dve", (lambda e, wA=wA, wB=wB: e.scalar_tensor_tensor(wA[:], wA[:], 2.0, wB[:], op0=ALU.mult, op1=ALU.mult)), [wA, wB], [wA]),
            ("dve", (lambda e, ph=ph, wB=wB: e.tensor_scalar(wB[:], ph[:], -2.0, 1.0, op0=ALU.mult, op1=ALU.add)), [ph], [wB]),
        ]

    def a1_A(gg):
        tmproj_xn("v", "bv", gg, PS[0])
        tmproj_xn("k", "bkp", gg, PS[1])
        tmproj_xn("q", "bqp", gg, PS[2])

    def a1_emit(ops):
        for eng, fn, rd, wr in ops:
            P.emit(eng, fn, reads=rd, writes=wr)

    def a1_cumsum(gg, h):
        P.emit("pe", [
            (lambda e, h=h: e.matmul(PS[3][:, 0:512], lhsT=u128[:],
                                     rhs=brb[:, h * 512:(h + 1) * 512],
                                     start=True, stop=True)),
            (lambda e, h=h: e.matmul(PS[3][:, 512:1024], lhsT=u128[:],
                                     rhs=bib[:, h * 512:(h + 1) * 512],
                                     start=True, stop=True)),
        ], reads=[u128, brb, bib], writes=[PS[3]])

    def a1_combine(gg, h):
        P.emit("dve", lambda e, h=h: e.tensor_tensor(
            ph_[0][:, 0:512], PS[3][:, 0:512],
            wB_[1][:, h * 512:(h + 1) * 512], op=ALU.mult),
            reads=[PS[3], wB_[1]], writes=[ph_[0]])
        P.emit("dve", lambda e, h=h: e.tensor_tensor(
            cb[:, 0:512], PS[3][:, 512:1024],
            wA_[1][:, h * 512:(h + 1) * 512], op=ALU.mult),
            reads=[PS[3], wA_[1]], writes=[cb])
        acc_c = "scrA" if h == 0 else "scrB"
        P.emit("dve", lambda e, h=h, gg=gg, ac=acc_c: e.scalar_tensor_tensor(
            ret_v[:, RSLOT[gg], h * 512:(h + 1) * 512], ph_[0][:, 0:512], 1.0,
            cb[:, 0:512], op0=ALU.mult, op1=ALU.add, accum_out=colg(ac, gg)),
            reads=[ph_[0], cb], writes=[retr[RSLOT[gg]], cols])
        sq_c = "sqa" if h == 0 else "sqb"
        P.emit("act", lambda e, h=h, gg=gg, sc=sq_c: e.activation(
            cb[:, 512:1024], ret_v[:, RSLOT[gg], h * 512:(h + 1) * 512], AF.Square,
            accum_out=colg(sc, gg)),
            reads=[retr[RSLOT[gg]]], writes=[cb, cols])

    a1_A(0)
    for gg in range(NG):
        a1_emit(trig_ops(PS[1], 0))
        P.emit("dve", lambda e: e.tensor_tensor(bib[:], PS[0][:], wA_[0][:], op=ALU.mult),
               reads=[PS[0], wA_[0]], writes=[bib])
        P.emit("dve", lambda e: e.tensor_tensor(brb[:], PS[0][:], wB_[0][:], op=ALU.mult),
               reads=[PS[0], wB_[0]], writes=[brb])
        a1_cumsum(gg, 0)
        if gg + 1 < NG:
            tmproj_xn("v", "bv", gg + 1, PS[0])
            tmproj_xn("k", "bkp", gg + 1, PS[1])
        a1_emit(trig_ops(PS[2], 1))
        a1_combine(gg, 0)
        a1_cumsum(gg, 1)
        if gg + 1 < NG:
            tmproj_xn("q", "bqp", gg + 1, PS[2])
        a1_combine(gg, 1)
        P.emit("dve", lambda e, gg=gg: e.tensor_tensor(
            colg("sum2", gg), colg("scrA", gg), colg("scrB", gg), op=ALU.add),
            reads=[cols], writes=[cols])
        P.emit("dve", lambda e, gg=gg: e.tensor_tensor(
            colg("sq2", gg), colg("sqa", gg), colg("sqb", gg), op=ALU.add),
            reads=[cols], writes=[cols])
    if debug:
        P.dma("act", dbg["ret"], ret_sb[:], None, src_bufs=[ret_sb], name="dbgret")

    if stop_after == "a1":
        P.wait_all(); P.play(); return nc
    P.emit("dve", lambda e: e.tensor_scalar(colr("mu2"), colr("sum2"), 1.0 / D, None, op0=ALU.mult),
           reads=[cols], writes=[cols])
    P.emit("dve", lambda e: e.tensor_scalar(colr("scrA"), colr("sq2"), 1.0 / (D * D), None, op0=ALU.mult),
           reads=[cols], writes=[cols])
    P.emit("dve", lambda e: e.tensor_tensor(colr("scrB"), colr("mu2"), colr("mu2"), op=ALU.mult),
           reads=[cols], writes=[cols])
    P.emit("dve", lambda e: e.tensor_scalar(colr("scrB"), colr("scrB"), 1.0 / D, None, op0=ALU.mult),
           reads=[cols], writes=[cols])
    P.emit("dve", lambda e: e.tensor_tensor(colr("scrA"), colr("scrA"), colr("scrB"), op=ALU.subtract),
           reads=[cols], writes=[cols])
    P.emit("dve", lambda e: e.tensor_scalar(colr("scrA"), colr("scrA"), 1.0, 1e-5, op0=ALU.mult, op1=ALU.add),
           reads=[cols], writes=[cols])
    P.emit("act", lambda e: e.activation(colr("scrB"), colr("scrA"), AF.Ln),
           reads=[cols], writes=[cols])
    P.emit("act", lambda e: e.activation(colr("scrA"), colr("scrB"), AF.Exp, scale=-0.5),
           reads=[cols], writes=[cols])
    P.emit("dve", lambda e: e.tensor_scalar(colr("rstd2"), colr("scrA"), 1.0 / 32.0, None, op0=ALU.mult),
           reads=[cols], writes=[cols])
    b2scr = sb("b2scr", [128, 1024], F32, O_W["k"])
    P.dma("sp", b2scr[0:1, 0:1024], bo_d, b2scr, name="bo")
    fns = []
    for nch in range(8):
        for kc in range(8):
            fns.append(lambda e, nch=nch, kc=kc: e.matmul(
                PS[1][0:1, nch * 128:(nch + 1) * 128],
                lhsT=lnbc[:, kc:kc + 1],
                rhs=wsb["o"][:, kc * 1024 + nch * 128:kc * 1024 + nch * 128 + 128],
                start=(kc == 0), stop=(kc == 7)))
    P.emit("pe", fns, reads=[wsb["o"], lnbc], writes=[PS[1]])
    P.emit("dve", lambda e: e.tensor_tensor(row("bwo"), PS[1][0:1, 0:1024], b2scr[0:1, 0:1024], op=ALU.add),
           reads=[PS[1], b2scr], writes=[rows])
    for kc in range(8):
        P.emit("dve", lambda e, kc=kc: e.tensor_scalar(
            wsb["o"][:, kc * 1024:(kc + 1) * 1024],
            wsb["o"][:, kc * 1024:(kc + 1) * 1024],
            cols[:, CP["lng"] + kc:CP["lng"] + kc + 1], None, op0=ALU.mult),
            reads=[wsb["o"], cols], writes=[wsb["o"]])

    if stop_after == "b2":
        P.wait_all(); P.play(); return nc
    x0 = ga("x0", [128, 8 * TA], F32, off=G0)
    x1 = ga("x1", [128, 8 * TA], F32, off=G0 + 8192)
    xh_tm = ga("xh_tm", [128, 1024], F32, off=G0 + 16384)
    xh_fm = ga("xh_fm", [128, 8 * TA], BF16, off=G0 + 20480)

    def a2b_xstage(ti):
        xb = (x0, x1)[ti % 2]
        t0 = ti * TA
        xbv = xb[:].rearrange("p (c t) -> p c t", c=8)
        P.dma("sp", xbv, x_fm_v[:, :, t0:t0 + TA], xb, name=f"xa{ti % 2}")
        for gl in range(TA // 128):
            gg = ti * (TA // 128) + gl
            P.emit("dve", lambda e, gg=gg: e.tensor_scalar(
                xh_tm[:], ret_v[:, RSLOT[gg], :], colg("mu2", gg), colg("rstd2", gg),
                op0=ALU.subtract, op1=ALU.mult),
                reads=[retr[RSLOT[gg]], cols], writes=[xh_tm])
            ps = PS[2 + gl % 2]
            P.emit("pe", [
                (lambda e, c=c, ps=ps: e.matmul(
                    ps[:, c * 128:(c + 1) * 128], lhsT=xh_tm[:, c * 128:(c + 1) * 128],
                    rhs=ident[:], is_transpose=True, start=True, stop=True))
                for c in range(8)],
                reads=[xh_tm, ident], writes=[ps])
            P.emit("dve", lambda e, gl=gl, ps=ps: e.tensor_copy(
                xh_fm[:].rearrange("p (c t) -> p c t", c=8)[:, :, gl * 128:(gl + 1) * 128],
                ps[:].rearrange("p (c t) -> p c t", c=8)),
                reads=[ps], writes=[xh_fm])

    def a2b_mms(ti):
        for half in range(2):
            ps = PS[half]
            fns = []
            for nco in range(4):
                nch = half * 4 + nco
                for kc in range(8):
                    fns.append(lambda e, nch=nch, nco=nco, kc=kc, ps=ps:
                               e.matmul(ps[:, nco * TA:(nco + 1) * TA],
                                        lhsT=wsb["o"][:, kc * 1024 + nch * 128:kc * 1024 + nch * 128 + 128],
                                        rhs=xh_fm[:, kc * TA:(kc + 1) * TA],
                                        start=(kc == 0), stop=False))
                pb, c = RP["bwo"]
                fns.append(lambda e, nch=nch, nco=nco, ps=ps, pb=pb, c=c:
                           e.matmul(ps[:, nco * TA:(nco + 1) * TA],
                                    lhsT=rows[pb:pb + 1, c + nch * 128:c + nch * 128 + 128],
                                    rhs=ones[pb:pb + 1, 0:TA],
                                    start=False, stop=True))
            P.emit("pe", fns, reads=[wsb["o"], xh_fm, rows, ones], writes=[ps])

    def a2b_acc(ti):
        xb = (x0, x1)[ti % 2]
        t0 = ti * TA
        for half in range(2):
            ps = PS[half]
            P.emit("dve", lambda e, half=half, xb=xb, t0=t0: e.tensor_tensor(
                xb[:, half * 4 * TA:(half + 1) * 4 * TA],
                xb[:, half * 4 * TA:(half + 1) * 4 * TA],
                xn_sb[:].rearrange("p (c t) -> p c t", c=8)[:, half * 4:half * 4 + 4, t0:t0 + TA],
                op=ALU.add), reads=[xb, xn_sb], writes=[xb])
            P.emit("dve", lambda e, half=half, ps=ps, xb=xb: e.tensor_tensor(
                xb[:, half * 4 * TA:(half + 1) * 4 * TA],
                xb[:, half * 4 * TA:(half + 1) * 4 * TA],
                ps[:], op=ALU.add), reads=[xb, ps], writes=[xb])
        P.dma("act", acc_v[:, :, t0:t0 + TA],
              xb[:].rearrange("p (c t) -> p c t", c=8), acc_h,
              src_bufs=[xb], name="accst")


    a2b_xstage(0)
    for ti in range(NTA):
        a2b_mms(ti)
        if ti + 1 < NTA:
            a2b_xstage(ti + 1)
        a2b_acc(ti)

    h1 = sb("h1_sb", [128, 32 * TF], BF16, 32768)
    h2 = sb("h2_sb", [128, 32 * TF], BF16, 65536)
    kst = [sb("kst0", [128, 8192], BF16, 98304),
           sb("kst1", [128, 8192], BF16, 114688),
           sb("kst2", [128, 8192], BF16, G0)]
    accin = sb("accin", [128, 8 * TF], F32, 131072)
    outb = sb("outb", [128, 8 * TF], F32, 147456)

    def srow_build(ps, src_h, hi_nm, lo_nm, f_nm):
        P.emit("pe", [
            (lambda e, kc=kc, ps=ps: e.matmul(
                ps[0:1, 0:TF], lhsT=ones[:, 0:1],
                rhs=src_h[:, kc * TF:(kc + 1) * TF],
                start=(kc == 0), stop=(kc == 31)))
            for kc in range(32)], reads=[ones, src_h], writes=[ps])
        P.emit("dve", lambda e, ps=ps: e.tensor_copy(rowf(f_nm), ps[0:1, 0:TF]),
               reads=[ps], writes=[rowsf])
        P.emit("dve", lambda e: e.tensor_copy(row(hi_nm, TF), rowf(f_nm)),
               reads=[rowsf], writes=[rows])
        P.emit("dve", lambda e: e.tensor_tensor(rowf("scr"), rowf(f_nm), row(hi_nm, TF), op=ALU.subtract),
               reads=[rowsf, rows], writes=[rowsf])
        P.emit("dve", lambda e: e.tensor_copy(row(lo_nm, TF), rowf("scr")),
               reads=[rowsf], writes=[rows])

    for tj in range(NTF):
        t0 = tj * TF
        abv = accin[:].rearrange("p (c t) -> p c t", c=8)
        P.dma("pool", abv, acc_v[:, :, t0:t0 + TF], accin, src_bufs=[acc_h],
              name="accin")
        for g in range(16):
            kb = kst[g % 3]
            P.dma("sp", kb[:, 0:2048].rearrange("p (k n) -> p k n", k=8),
                  k0dbf[g].rearrange("k p n -> p k n"), kb,
                  src_bufs=[k0dbf_h], name=f"k0s{g % 2}")
            ps = PS[g % 4]
            fns = []
            for nco in range(2):
                for kc in range(8):
                    fns.append(lambda e, nco=nco, kc=kc, ps=ps, kb=kb, t0=t0:
                               e.matmul(ps[:, nco * TF:(nco + 1) * TF],
                                        lhsT=kb[:, kc * 256 + nco * 128:kc * 256 + nco * 128 + 128],
                                        rhs=xn_v[:, kc, t0:t0 + TF],
                                        start=(kc == 0), stop=(kc == 7)))
            P.emit("pe", fns, reads=[kb, xn_sb], writes=[ps])
            P.emit("act", lambda e, g=g, ps=ps: e.activation(
                h1[:, g * 1024:(g + 1) * 1024], ps[:], AF.Relu, scale=sgl("two_s0")),
                reads=[ps, cols], writes=[h1])
        srow_build(PS[0], h1, "s1hi", "s1lo", "s1")
        for g in range(16):
            kb = kst[g % 3]
            P.dma("sp", kb[:].rearrange("p (k n) -> p k n", k=32),
                  k1bf[g].rearrange("k p n -> p k n"), kb,
                  src_bufs=[k1bf_h], name=f"k1s{g % 3}")
            ps = PS[g % 4]
            fns = []
            for nco in range(2):
                for kc in range(32):
                    fns.append(lambda e, nco=nco, kc=kc, ps=ps, kb=kb:
                               e.matmul(ps[:, nco * TF:(nco + 1) * TF],
                                        lhsT=kb[:, kc * 256 + nco * 128:kc * 256 + nco * 128 + 128],
                                        rhs=h1[:, kc * TF:(kc + 1) * TF],
                                        start=(kc == 0), stop=False))
            P.emit("pe", fns, reads=[kb, h1], writes=[ps])
            fns = []
            for nco in range(2):
                fns.append(lambda e, nco=nco, ps=ps: e.matmul(
                    ps[:, nco * TF:(nco + 1) * TF], lhsT=row("mh0", 128),
                    rhs=row("s1hi", TF), start=False, stop=False))
                fns.append(lambda e, nco=nco, ps=ps: e.matmul(
                    ps[:, nco * TF:(nco + 1) * TF], lhsT=row("mh0", 128),
                    rhs=row("s1lo", TF), start=False, stop=True))
            P.emit("pe", fns, reads=[rows], writes=[ps])
            P.emit("act", lambda e, g=g, ps=ps: e.activation(
                h2[:, g * 1024:(g + 1) * 1024], ps[:], AF.Relu, scale=sgl("four_s1")),
                reads=[ps, cols], writes=[h2])
        srow_build(PS[0], h2, "s2hi", "s2lo", "s2")
        for g in range(4):
            kb = kst[1 + g % 2]
            P.dma("sp", kb[:].rearrange("p (k n) -> p k n", k=32),
                  k2bf[g].rearrange("k p n -> p k n"), kb,
                  src_bufs=[k2bf_h], name=f"k2s{g % 3}")
            ps = PS[g % 4]
            fns = []
            for nco in range(2):
                for kc in range(32):
                    fns.append(lambda e, nco=nco, kc=kc, ps=ps, kb=kb:
                               e.matmul(ps[:, nco * TF:(nco + 1) * TF],
                                        lhsT=kb[:, kc * 256 + nco * 128:kc * 256 + nco * 128 + 128],
                                        rhs=h2[:, kc * TF:(kc + 1) * TF],
                                        start=(kc == 0), stop=False))
            P.emit("pe", fns, reads=[kb, h2], writes=[ps])
            fns = []
            for nco in range(2):
                fns.append(lambda e, nco=nco, ps=ps: e.matmul(
                    ps[:, nco * TF:(nco + 1) * TF], lhsT=row("mh0", 128),
                    rhs=row("s2hi", TF), start=False, stop=False))
                fns.append(lambda e, nco=nco, ps=ps: e.matmul(
                    ps[:, nco * TF:(nco + 1) * TF], lhsT=row("mh0", 128),
                    rhs=row("s2lo", TF), start=False, stop=True))
            P.emit("pe", fns, reads=[rows], writes=[ps])
            P.emit("act", lambda e, g=g, ps=ps: e.activation(
                outb[:, g * 1024:(g + 1) * 1024], ps[:], AF.Identity,
                scale=sgl("four_s2")), reads=[ps, cols], writes=[outb])
        P.emit("dve", lambda e: e.tensor_tensor(outb[:], outb[:], accin[:], op=ALU.add),
               reads=[outb, accin], writes=[outb])
        P.dma("act", out_v[:, :, t0:t0 + TF],
              outb[:].rearrange("p (c t) -> p c t", c=8), None,
              src_bufs=[outb], name="outst")
        if debug and tj == 0:
            P.dma("act", dbg["h1"], h1[:], None, src_bufs=[h1], name="dbgh1")
            P.dma("act", dbg["h2"], h2[:], None, src_bufs=[h2], name="dbgh2")

    P.wait_all()
    P.play()
    return nc


def _prep_shared(inputs):
    f = np.ascontiguousarray
    d = {}
    for k, nm in (("k", "Wk"), ("v", "Wv"), ("q", "Wq"), ("kp", "Wkp"),
                  ("qp", "Wqp"), ("o", "Wo")):
        w = np.asarray(inputs[f"L0_attn_{nm}"], dtype=np.float32)
        d[f"w_{k}"] = f(w.reshape(8, 128, 8, 128).transpose(1, 0, 2, 3)
                        .reshape(128, 8192))
    for k, nm in (("v", "bv"), ("kp", "bkp"), ("qp", "bqp"), ("o", "bo")):
        d[f"b_{k}"] = f(np.asarray(inputs[f"L0_attn_{nm}"],
                                   dtype=np.float32).reshape(1, D))
    for nm, key in (("Wk", "wkT"), ("Wq", "wqT")):
        w = np.asarray(inputs[f"L0_attn_{nm}"], dtype=np.float32).T
        d[key] = f(w.reshape(8, 128, 8, 128).transpose(1, 0, 2, 3)
                   .reshape(128, 8192))
    d["bkc"] = f(np.asarray(inputs["L0_attn_bk"], dtype=np.float32)
                 .reshape(8, 128).T)
    d["bqc"] = f(np.asarray(inputs["L0_attn_bq"], dtype=np.float32)
                 .reshape(8, 128).T)
    d["ps_r"] = f(np.asarray(inputs["L0_attn_phase_scale"],
                             dtype=np.float32).reshape(1, D))
    d["lng_c"] = f(np.asarray(inputs["L0_attn_ln_g"],
                              dtype=np.float32).reshape(8, 128).T)
    d["lnb_c"] = f(np.asarray(inputs["L0_attn_ln_b"],
                              dtype=np.float32).reshape(8, 128).T)
    k0 = np.asarray(inputs["L0_ffn_k0"]).view(np.uint8)
    d["k0t"] = f(k0[:D].reshape(8, 128, DF).transpose(1, 0, 2).reshape(128, 8 * DF))
    d["k0b"] = f(k0[D:].reshape(8, 128, DF).transpose(1, 0, 2).reshape(128, 8 * DF))
    k1 = np.asarray(inputs["L0_ffn_k1"]).view(np.uint8)
    d["k1t"] = f(k1.reshape(32, 128, 16, 256).transpose(2, 0, 1, 3))
    k2 = np.asarray(inputs["L0_ffn_k2"]).view(np.uint8)
    d["k2t"] = f(k2.reshape(32, 128, 4, 256).transpose(2, 0, 1, 3))
    for i in (0, 1, 2):
        d[f"s{i}_c"] = np.full((128, 1), np.asarray(inputs[f"L0_ffn_s{i}"],
                                                    dtype=np.float32))
    import ml_dtypes
    p = np.arange(128)[:, None]; q = np.arange(128)[None, :]
    d["u128_c"] = (((p <= q) & ((p < 64) == (q < 64)))
                   .astype(ml_dtypes.bfloat16))
    d["ident_c"] = np.eye(128, dtype=np.float32)
    return d


def kernel(**inputs):
    nc = build(debug=False)
    shared = _prep_shared(inputs)
    x = np.asarray(inputs["L0_attn_x"], dtype=np.float32).reshape(S, D)
    in_maps = []
    for c in range(NCORE):
        xs = x[c * TOK:(c + 1) * TOK]
        m = dict(shared)
        m["x_tm"] = np.ascontiguousarray(xs)
        m["x_fm"] = np.ascontiguousarray(xs.T)
        in_maps.append(m)
    res = run_bass_kernel_spmd(nc, in_maps, core_ids=list(range(NCORE)))
    out = np.concatenate([r["out_fm"].T for r in res.results], axis=0)
    return out.reshape(1, S, D).astype(np.float32)


# version 33
# speedup vs baseline: 1.0495x; 1.0220x over previous
import math

import numpy as np

import concourse.bass as bass
import concourse.mybir as mybir
from concourse.bass_utils import run_bass_kernel_spmd

F32 = mybir.dt.float32
BF16 = mybir.dt.bfloat16
U8 = mybir.dt.uint8
I16 = mybir.dt.int16
I32 = mybir.dt.int32
AF = mybir.ActivationFunctionType
ALU = mybir.AluOpType
PI = math.pi
INV2PI = 1.0 / (2.0 * PI)
TWOPI = 2.0 * PI
TP_HI = float(np.float32(2 * np.pi))
TP_LO = float(np.float32(2 * np.pi - np.float64(np.float32(2 * np.pi))))
HP_HI = float(np.float32(np.pi / 2))
HP_LO = float(np.float32(np.pi / 2 - np.float64(np.float32(np.pi / 2))))

NCORE = 8
S, D = 16384, 1024
TOK = S // NCORE
NG = TOK // 128
TA = 256
NTA = TOK // TA
TF = 512
NTF = TOK // TF
DF = 4096

SBUF_BASE = 16512
SBUF_TOP = 229344
ENGS = ("pe", "act", "dve", "pool", "sp")
DT_SZ = {F32: 4, BF16: 2, U8: 1, I16: 2, I32: 4}


class Prog:

    def __init__(self, nc):
        self.nc = nc
        self.ops = {e: [] for e in ENGS}
        self.esem = {e: nc.alloc_semaphore(f"sem_{e}") for e in ENGS}
        self.ecnt = {e: 0 for e in ENGS}
        self.water = {e: {} for e in ENGS}
        self.bufs = {}
        self.dsem = {}
        self.sems = dict(self.esem)

    def reg(self, h, lo=None, size=None):
        self.bufs[id(h)] = dict(h=h, iv=(lo, lo + size) if lo is not None else None,
                                w={}, r=[])
        return h

    def sballoc(self, name, shape, dtype, off):
        per = int(np.prod(shape[1:])) * DT_SZ[dtype]
        assert SBUF_BASE + off + per <= SBUF_TOP, (name, off, per)
        h = self.nc.alloc_sbuf_tensor_at(name, list(shape), dtype,
                                         offset=SBUF_BASE + off)
        return self.reg(h, off, per)

    def _overlaps(self, b):
        me = self.bufs[id(b)]
        if me["iv"] is None:
            return [me]
        lo, hi = me["iv"]
        return [m for m in self.bufs.values()
                if m["iv"] is not None and m["iv"][0] < hi and lo < m["iv"][1]]

    def _need(self, eng, mark):
        if mark is None:
            return None
        k, v = mark
        if k == eng == "pe":
            return None
        if self.water[eng].get(k, 0) >= v:
            return None
        return mark

    def _collect(self, eng, reads, writes):
        need = {}

        def add(mark):
            m = self._need(eng, mark)
            if m:
                need[m[0]] = max(need.get(m[0], 0), m[1])

        for b in reads:
            if id(b) in self.bufs:
                for m in self._overlaps(b):
                    for kv in m["w"].items():
                        add(kv)
        for b in writes:
            if id(b) in self.bufs:
                for m in self._overlaps(b):
                    for kv in m["w"].items():
                        add(kv)
                    for rr in m["r"]:
                        add(rr)
        return need

    def _emit_waits(self, eng, need):
        for k, v in sorted(need.items(), key=lambda kv: str(kv[0])):
            self.water[eng][k] = v
            sem = self.sems[k]
            self.ops[eng].append(lambda e, s=sem, vv=v: e.wait_ge(s, vv))

    def _mark(self, reads, writes, semkey, val):
        for b in writes:
            if id(b) not in self.bufs:
                self.reg(b)
            m = self.bufs[id(b)]
            m["w"][semkey] = val
            m["r"] = [r for r in m["r"] if r[0] != semkey or r[1] > val]
        for b in reads:
            if id(b) in self.bufs:
                self.bufs[id(b)]["r"].append((semkey, val))

    def emit(self, eng, fns, reads=(), writes=()):
        if not isinstance(fns, (list, tuple)):
            fns = [fns]
        need = self._collect(eng, reads, writes)
        self._emit_waits(eng, need)
        sem = self.esem[eng]
        for f in fns[:-1]:
            self.ops[eng].append(f)
        self.ecnt[eng] += 1
        val = self.ecnt[eng]
        self.ops[eng].append(lambda e, f=fns[-1], s=sem: f(e).then_inc(s, 1))
        self._mark(reads, writes, eng, val)
        return (eng, val)

    def dma(self, queue, dst, src, dst_buf, src_bufs=(), name="d", after=(),
            **kw):
        key = id(dst_buf) if dst_buf is not None else ("anon", len(self.dsem))
        if key not in self.dsem:
            h = self.nc.alloc_semaphore(f"sd_{name}_{len(self.dsem)}")
            self.dsem[key] = [h, 0]
            self.sems[("d", key)] = h
        ent = self.dsem[key]
        writes = [dst_buf] if dst_buf is not None else []
        need = self._collect(queue, list(src_bufs) + list(after), writes)
        m = self._need(queue, (("d", key), ent[1])) if ent[1] > 0 else None
        if m:
            need[m[0]] = max(need.get(m[0], 0), m[1])
        self._emit_waits(queue, need)
        ent[1] += 16
        total, h = ent[1], ent[0]
        self.ops[queue].append(
            lambda e, d=dst, s=src, hh=h, kws=kw: e.dma_start(d, s, **kws).then_inc(hh, 16))
        self._mark(list(src_bufs), writes, ("d", key), total)
        return (("d", key), total)

    def wait_all(self, engs=ENGS):
        marks = [(e, self.ecnt[e]) for e in ENGS if self.ecnt[e] > 0]
        marks += [(("d", k), v[1]) for k, v in self.dsem.items() if v[1] > 0]
        for eng in engs:
            need = {}
            for k, v in marks:
                m = self._need(eng, (k, v))
                if m:
                    need[m[0]] = max(need.get(m[0], 0), m[1])
            self._emit_waits(eng, need)

    def play(self):
        with self.nc.Block() as block:
            block.tensor(lambda e: [f(e) for f in self.ops["pe"]])
            block.scalar(lambda e: [f(e) for f in self.ops["act"]])
            block.vector(lambda e: [f(e) for f in self.ops["dve"]])
            block.gpsimd(lambda e: [f(e) for f in self.ops["pool"]])
            block.sync(lambda e: [f(e) for f in self.ops["sp"]])


def build(debug=False, stop_after=None):
    nc = bass.Bass("TRN2", target_bir_lowering=False)
    P = Prog(nc)

    def din(name, shape, dtype):
        return nc.dram_tensor(name, list(shape), dtype, kind="ExternalInput").ap()

    x_fm = din("x_fm", [D, TOK], F32)
    x_tm = din("x_tm", [TOK, D], F32)
    w_in = {k: din(f"w_{k}", [128, 8192], F32)
            for k in ("k", "kp", "v", "q", "qp", "o")}
    k0t_d = din("k0t", [128, 8 * DF], U8)
    k0b_d = din("k0b", [128, 8 * DF], U8)
    k1_d = din("k1t", [16, 32, 128, 256], U8)
    k2_d = din("k2t", [4, 32, 128, 256], U8)
    brow_in = {k: din(f"b_{k}", [1, D], F32)
               for k in ("v", "kp", "qp")}
    wkT_d = din("wkT", [128, 8192], F32)
    wqT_d = din("wqT", [128, 8192], F32)
    bkc_d = din("bkc", [128, 8], F32)
    bqc_d = din("bqc", [128, 8], F32)
    bo_d = din("b_o", [1, D], F32)
    ps_d = din("ps_r", [1, D], F32)
    lng_d = din("lng_c", [128, 8], F32)
    lnb_d = din("lnb_c", [128, 8], F32)
    s_d = {k: din(f"s{k}_c", [128, 1], F32) for k in (0, 1, 2)}
    u128_d = din("u128_c", [128, 128], BF16)
    ident_d = din("ident_c", [128, 128], F32)

    out_d = nc.dram_tensor("out_fm", [D, TOK], F32, kind="ExternalOutput").ap()
    dbg = {}
    if debug:
        for nm, shp in (("xn", [128, 8 * TOK]), ("ret", [128, NG * D]),
                        ("h1", [128, 32 * TF]), ("h2", [128, 32 * TF])):
            dbg[nm] = nc.dram_tensor("dbg_" + nm, shp, BF16,
                                     kind="ExternalOutput").ap()
        dbg["acc"] = nc.dram_tensor("dbg_acc", [D, TOK], F32,
                                    kind="ExternalOutput").ap()
        for nm in ("br", "bi", "sq", "cosq"):
            dbg[nm] = nc.dram_tensor("dbg_" + nm, [128, 1024], BF16,
                                     kind="ExternalOutput").ap()
        for nm in ("memr", "memi"):
            dbg[nm] = nc.dram_tensor("dbg_" + nm, [128, 1024], BF16,
                                     kind="ExternalOutput").ap()
        dbg["u128"] = nc.dram_tensor("dbg_u128", [128, 128], BF16,
                                     kind="ExternalOutput").ap()

    def dint(name, shape, dtype):
        return nc.dram_tensor(name, list(shape), dtype, kind="Internal").ap()

    k1bf = dint("k1bf", [16, 32, 128, 256], BF16)
    k2bf = dint("k2bf", [4, 32, 128, 256], BF16)
    k0dbf = dint("k0dbf", [16, 8, 128, 256], BF16)
    acc_d = dint("acc_d", [D, TOK], F32)
    k1bf_h = P.reg(object())
    k2bf_h = P.reg(object())
    k0dbf_h = P.reg(object())
    acc_h = P.reg(object())

    O_XN = 0
    O_W = {k: 32768 + i * 16384 for i, k in enumerate(("k", "kp", "v", "q", "qp"))}
    O_WO = 114688
    O_RET = 131072
    O_ROWS = 163840
    O_ROWSF = 176128
    E_END = 180224
    O_ONES = 180224
    O_PSD = 180736
    O_U128 = 184832
    O_IDENT = 185152
    O_COLS = 185664
    O_LNB = 186688
    G0 = 186816
    GTOP = SBUF_TOP - SBUF_BASE
    assert G0 + 24576 <= GTOP, (G0, GTOP)

    sb = P.sballoc
    xn_sb = sb("xn_sb", [128, 8 * TOK], BF16, O_XN)
    wsb = {k: sb(f"w{k}_sb", [128, 8192], BF16, O_W[k]) for k in O_W}
    wsb["o"] = sb("wo_sb", [128, 8192], BF16, O_WO)
    ret_sb = sb("ret_sb", [128, NG * D], BF16, O_RET)
    rows = sb("rows_sb", [128, 6144], BF16, O_ROWS)
    rowsf = sb("rowsf_sb", [128, 1024], F32, O_ROWSF)
    ones = sb("ones_sb", [128, 256], BF16, O_ONES)
    psd = sb("psd_sb", [128, 1024], F32, O_PSD)
    u128 = sb("u128_sb", [128, 128], BF16, O_U128)
    ident = sb("ident_sb", [128, 128], F32, O_IDENT)
    cols = sb("cols_sb", [128, 256], F32, O_COLS)
    lnbc = sb("lnbc_sb", [128, 8], BF16, O_LNB)
    bkcs = sb("bkc_sb", [128, 8], BF16, O_LNB + 32)
    bqcs = sb("bqc_sb", [128, 8], BF16, O_LNB + 64)

    RP = {"bkp": (0, 0), "bqp": (0, 1024), "mh0": (0, 2048),
          "bv": (0, 3072),
          "s1hi": (0, 3072), "s1lo": (0, 3584),
          "s2hi": (0, 4096), "s2lo": (0, 4608),
          "bwo": (0, 5120)}

    def row(nm, n=1024, c0=0):
        p, c = RP[nm]
        return rows[p:p + 1, c + c0:c + c0 + n]

    RF = {"s1": (0, 0, 512), "s2": (0, 512, 512), "scr": (32, 0, 512)}

    def rowf(nm):
        p, c, n = RF[nm]
        return rowsf[p:p + 1, c:c + n]

    CP = dict(sum1=0, sq1=16, mu1=32, rstd1=48, sum2=64, sq2=80, mu2=96,
              rstd2=112, scrA=128, scrB=144, lng=160, singles=176,
              sqa=192, sqb=208)
    SGL = dict(two_s0=176, four_s1=177, four_s2=178, hpi=179)

    def colg(nm, g):
        return cols[:, CP[nm] + g:CP[nm] + g + 1]

    def colr(nm, n=16):
        return cols[:, CP[nm]:CP[nm] + n]

    def sgl(nm):
        return cols[:, SGL[nm]:SGL[nm] + 1]

    PS = [P.reg(nc.alloc_psum_tensor(f"ps{i}", [128, 1024], F32))
          for i in range(4)]

    xn_v = xn_sb[:].rearrange("p (c t) -> p c t", c=8)
    xng = [P.reg(object(), lo=O_XN + g * 256, size=256) for g in range(NG)]
    x_tmf = sb("xtm_sb", [128, NG * D], F32, O_WO)
    xtm_v = x_tmf[:].rearrange("p (g f) -> p g f", g=NG)
    XSLOT = [NG - 1 - g for g in range(NG)]
    xtmr = [P.reg(object(), lo=O_WO + XSLOT[g] * 4096, size=4096)
            for g in range(NG)]
    x_fm_v = x_fm.rearrange("(c p) t -> p c t", p=128)
    acc_v = acc_d.rearrange("(c p) t -> p c t", p=128)
    out_v = out_d.rearrange("(c p) t -> p c t", p=128)
    ret_v = ret_sb[:].rearrange("p (g f) -> p g f", g=NG)
    RSLOT = [NG - 1 - g for g in range(NG)]
    retr = [P.reg(object(), lo=O_RET + sl * 2048, size=2048) for sl in range(NG)]

    g_off = G0

    def ga(name, shape, dtype, off=None):
        nonlocal g_off
        if off is None:
            off = g_off
            per = int(np.prod(shape[1:])) * DT_SZ[dtype]
            g_off += (per + 63) & ~63
        assert off + int(np.prod(shape[1:])) * DT_SZ[dtype] <= GTOP
        return sb(name, shape, dtype, off)

    scs = {k: ga(f"sc{k}", [128, 1], F32) for k in (0, 1, 2)}
    p0r = ga("p0r", [128, 1024], F32)
    st_a = sb("st_a", [128, DF], BF16, O_WO)
    st_b = sb("st_b", [128, DF], BF16, O_WO + 8192)
    st_dd = sb("st_dd", [128, DF], BF16, O_WO + 16384)
    psc_a = sb("psc_a", [128, 2048], BF16, O_WO + 24576)
    psc_f = sb("psc_f", [128, 2048], F32, O_WO + 28672)

    P.dma("sp", u128[:], u128_d, u128, name="u128")
    P.dma("sp", ident[:], ident_d, ident, name="ident")
    P.emit("dve", lambda e: e.memset(ones[:], 1.0), writes=[ones])
    P.emit("dve", lambda e: e.memset(sgl("hpi"), PI / 2.0), writes=[cols])

    P.dma("pool", lnbc[:], lnb_d, lnbc, name="lnb")
    P.dma("pool", bkcs[:], bkc_d, bkcs, name="bkc")
    P.dma("pool", bqcs[:], bqc_d, bqcs, name="bqc")
    stK = sb("stK", [128, 8192], BF16, O_WO)
    for wtd, wpname, dstname in ((wkT_d, "kp", "k"), (wqT_d, "qp", "q")):
        P.dma("pool", wsb[wpname][:], w_in[wpname], wsb[wpname],
              name=f"w{wpname}", max_dma_last_dim=8192)
        P.dma("pool", stK[:], wtd, stK, name="wT", max_dma_last_dim=8192)
        for dc in range(8):
            for h in range(2):
                ps = PS[(dc * 2 + h) % 4]
                fns = [
                    (lambda e, kfc=kfc, dc=dc, h=h, ps=ps, wp=wpname:
                     e.matmul(ps[:, 0:512],
                              lhsT=stK[:, kfc * 1024 + dc * 128:kfc * 1024 + dc * 128 + 128],
                              rhs=wsb[wp][:, kfc * 1024 + h * 512:kfc * 1024 + h * 512 + 512],
                              start=(kfc == 0), stop=(kfc == 7)))
                    for kfc in range(8)]
                P.emit("pe", fns, reads=[stK, wsb[wpname]], writes=[ps])
                P.emit("act", lambda e, dc=dc, h=h, ps=ps, dn=dstname: e.activation(
                    wsb[dn][:, dc * 1024 + h * 512:dc * 1024 + h * 512 + 512],
                    ps[:, 0:512], AF.Identity), reads=[ps], writes=[wsb[dstname]])
    for k in (0, 1, 2):
        P.dma("sp", scs[k][:], s_d[k], scs[k], name=f"s{k}")
    P.dma("sp", colr("lng", 8), lng_d, cols, name="lng")
    P.emit("dve", lambda e: e.tensor_scalar(sgl("two_s0"), scs[0][:], 2.0, None, op0=ALU.mult),
           reads=[scs[0]], writes=[cols])
    P.emit("dve", lambda e: e.tensor_scalar(sgl("four_s1"), scs[1][:], 4.0, None, op0=ALU.mult),
           reads=[scs[1]], writes=[cols])
    P.emit("dve", lambda e: e.tensor_scalar(sgl("four_s2"), scs[2][:], 4.0, None, op0=ALU.mult),
           reads=[scs[2]], writes=[cols])

    P.dma("pool", wsb["v"][:], w_in["v"], wsb["v"], name="wv",
          max_dma_last_dim=8192)
    P.dma("sp", psc_f[0:1, 0:1024], ps_d, psc_f, name="psr")
    P.emit("dve", lambda e: e.tensor_copy(psc_a[0:1, 0:1024], psc_f[0:1, 0:1024]),
           reads=[psc_f], writes=[psc_a])
    P.emit("dve", lambda e: e.tensor_tensor(psc_f[0:1, 1024:2048], psc_f[0:1, 0:1024], psc_a[0:1, 0:1024], op=ALU.subtract),
           reads=[psc_f, psc_a], writes=[psc_f])
    P.emit("dve", lambda e: e.tensor_copy(psc_a[0:1, 1024:2048], psc_f[0:1, 1024:2048]),
           reads=[psc_f], writes=[psc_a])
    P.emit("pe", [
        lambda e: e.matmul(PS[0][:, 0:512], lhsT=ones[0:1, 0:128], rhs=psc_a[0:1, 0:512], start=True, stop=False),
        lambda e: e.matmul(PS[0][:, 0:512], lhsT=ones[0:1, 0:128], rhs=psc_a[0:1, 1024:1536], start=False, stop=True),
        lambda e: e.matmul(PS[0][:, 512:1024], lhsT=ones[0:1, 0:128], rhs=psc_a[0:1, 512:1024], start=True, stop=False),
        lambda e: e.matmul(PS[0][:, 512:1024], lhsT=ones[0:1, 0:128], rhs=psc_a[0:1, 1536:2048], start=False, stop=True),
    ], reads=[ones, psc_a], writes=[PS[0]])
    P.emit("dve", lambda e: e.tensor_scalar(psd[:], PS[0][:], 0.5, None, op0=ALU.mult),
           reads=[PS[0]], writes=[psd])

    def bias_builds():
        P.dma("sp", p0r[0:1, 0:1024], brow_in["v"], p0r, name="brow")
        P.emit("dve", lambda e: e.tensor_copy(row("bv"), p0r[0:1, 0:1024]),
               reads=[p0r], writes=[rows])
        for bnm, bc, wpname, slot in (("kp", bkcs, "kp", "bkp"),
                                      ("qp", bqcs, "qp", "bqp")):
            P.dma("sp", p0r[0:1, 0:1024], brow_in[bnm], p0r, name="brow")
            fns = []
            for h in range(2):
                for kfc in range(8):
                    fns.append(lambda e, kfc=kfc, h=h, bc=bc, wp=wpname: e.matmul(
                        PS[3][0:1, h * 512:h * 512 + 512],
                        lhsT=bc[:, kfc:kfc + 1],
                        rhs=wsb[wp][:, kfc * 1024 + h * 512:kfc * 1024 + h * 512 + 512],
                        start=(kfc == 0), stop=(kfc == 7)))
            P.emit("pe", fns, reads=[bc, wsb[wpname]], writes=[PS[3]])
            P.emit("dve", lambda e, slot=slot: e.tensor_tensor(
                row(slot), PS[3][0:1, 0:1024], p0r[0:1, 0:1024], op=ALU.add),
                reads=[PS[3], p0r], writes=[rows])
        P.emit("dve", lambda e: e.memset(row("mh0", 128), -0.5), writes=[rows])

    a0_scr = [ga("a0s0", [128, 1024], BF16), ga("a0s1", [128, 1024], BF16)]
    a0_xn = ga("a0xn", [128, 1024], F32)
    for g in range(NG):
        P.dma("sp", xtm_v[:, XSLOT[g], :], x_tm[g * 128:(g + 1) * 128, :],
              xtmr[g], name=f"xt{g}", after=(stK,) if g == 0 else ())
    for g in range(NG):
        sc = a0_scr[g % 2]
        P.emit("dve", lambda e, g=g, s=sc: e.tensor_scalar(
            s[:], xtm_v[:, XSLOT[g], :], 1.0, 0.0, op0=ALU.mult, op1=ALU.add,
            accum_out=colg("sum1", g)), reads=[xtmr[g]], writes=[sc, cols])
        P.emit("dve", lambda e, g=g, s=sc: e.scalar_tensor_tensor(
            s[:], xtm_v[:, XSLOT[g], :], 1.0, xtm_v[:, XSLOT[g], :],
            op0=ALU.mult, op1=ALU.mult, accum_out=colg("sq1", g)),
            reads=[xtmr[g]], writes=[sc, cols])
    P.emit("dve", lambda e: e.tensor_scalar(colr("mu1"), colr("sum1"), 1.0 / D, None, op0=ALU.mult),
           reads=[cols], writes=[cols])
    P.emit("dve", lambda e: e.tensor_scalar(colr("scrA"), colr("sq1"), 1.0 / D, None, op0=ALU.mult),
           reads=[cols], writes=[cols])
    P.emit("dve", lambda e: e.tensor_tensor(colr("scrB"), colr("mu1"), colr("mu1"), op=ALU.mult),
           reads=[cols], writes=[cols])
    P.emit("dve", lambda e: e.tensor_tensor(colr("scrA"), colr("scrA"), colr("scrB"), op=ALU.subtract),
           reads=[cols], writes=[cols])
    P.emit("dve", lambda e: e.tensor_scalar(colr("scrA"), colr("scrA"), 1.0, 1e-5, op0=ALU.mult, op1=ALU.add),
           reads=[cols], writes=[cols])
    P.emit("act", lambda e: e.activation(colr("scrB"), colr("scrA"), AF.Ln),
           reads=[cols], writes=[cols])
    P.emit("act", lambda e: e.activation(colr("rstd1"), colr("scrB"), AF.Exp, scale=-0.5),
           reads=[cols], writes=[cols])
    for g in range(NG):
        P.emit("dve", lambda e, g=g: e.tensor_scalar(
            a0_xn[:], xtm_v[:, XSLOT[g], :], colg("mu1", g), colg("rstd1", g),
            op0=ALU.subtract, op1=ALU.mult), reads=[xtmr[g], cols], writes=[a0_xn])
        ps = PS[g % 2]
        P.emit("pe", [
            (lambda e, g=g, c=c, ps=ps: e.matmul(
                ps[:, c * 128:(c + 1) * 128],
                lhsT=a0_xn[:, c * 128:(c + 1) * 128], rhs=ident[:],
                is_transpose=True, start=True, stop=True,
                )) for c in range(8)],
            reads=[a0_xn, ident], writes=[ps])
        P.emit("act", lambda e, g=g, ps=ps: e.activation(
            xn_v[:, :, g * 128:(g + 1) * 128],
            ps[:].rearrange("p (c t) -> p c t", c=8), AF.Identity),
            reads=[ps], writes=[xng[g]])
        if g == 3:
            bias_builds()
    if debug:
        P.dma("act", dbg["xn"], xn_sb[:], None, src_bufs=[xn_sb], name="dbgxn")

    if stop_after == "a0":
        P.wait_all(); P.play(); return nc
    for kc in range(8):
        P.dma("pool", st_a[:], k0t_d[:, kc * DF:(kc + 1) * DF], st_a,
              name="k0a", max_dma_last_dim=2048)
        P.dma("pool", st_b[:], k0b_d[:, kc * DF:(kc + 1) * DF], st_b,
              name="k0b", max_dma_last_dim=2048)
        P.emit("pool", lambda e: e.tensor_tensor(st_dd[:], st_a[:], st_b[:], op=ALU.subtract),
               reads=[st_a, st_b], writes=[st_dd])
        P.dma("sp", k0dbf[:, kc, :, :].rearrange("g p n -> p g n"),
              st_dd[:].rearrange("p (g n) -> p g n", g=16), k0dbf_h,
              src_bufs=[st_dd], name="k0d")

    if stop_after == "p0":
        P.wait_all(); P.play(); return nc


    P.dma("pool", wsb["o"][:], w_in["o"], wsb["o"], name="wo",
          max_dma_last_dim=8192)
    P.dma("pool", k1bf[0:8], k1_d[0:8], k1bf_h, name="k1bf", max_dma_last_dim=2048)
    P.dma("pool", k1bf[8:16], k1_d[8:16], k1bf_h, name="k1bf", max_dma_last_dim=2048)
    P.dma("pool", k2bf, k2_d, k2bf_h, name="k2bf", max_dma_last_dim=2048)

    ph_ = [ga("ph0", [128, 1024], F32, off=G0),
           ga("ph1", [128, 1024], F32, off=G0 + 4096)]
    wA_ = [ga("wA0", [128, 1024], BF16, off=G0 + 8192),
           ga("wA1", [128, 1024], BF16, off=G0 + 10240)]
    wB_ = [ga("wB0", [128, 1024], BF16, off=G0 + 12288),
           ga("wB1", [128, 1024], BF16, off=G0 + 14336)]
    brb = ga("brb", [128, 1024], BF16, off=G0 + 16384)
    bib = ga("bib", [128, 1024], BF16, off=G0 + 18432)
    cb = ga("cb", [128, 1024], F32, off=G0 + 20480)

    def tmproj_xn(wname, bslot, gg, ps):
        fns = []
        pb, c = RP[bslot]
        for h2 in range(2):
            for kc in range(8):
                fns.append(lambda e, kc=kc, h2=h2, ps=ps, gg=gg, wn=wname:
                           e.matmul(ps[:, h2 * 512:(h2 + 1) * 512],
                                    lhsT=xn_v[:, kc, gg * 128:gg * 128 + 128],
                                    rhs=wsb[wn][:, kc * 1024 + h2 * 512:kc * 1024 + (h2 + 1) * 512],
                                    start=(kc == 0), stop=False))
            fns.append(lambda e, h2=h2, ps=ps, pb=pb, c=c:
                       e.matmul(ps[:, h2 * 512:(h2 + 1) * 512],
                                lhsT=ones[pb:pb + 1, 0:128],
                                rhs=rows[pb:pb + 1, c + h2 * 512:c + (h2 + 1) * 512],
                                start=False, stop=True))
        P.emit("pe", fns, reads=[xn_sb, wsb[wname], rows, ones, xng[gg]],
               writes=[ps])

    def trig_ops(ps_src, par):
        ph, wA, wB = ph_[par], wA_[par], wB_[par]
        return [
            ("act", (lambda e, p=ps_src, ph=ph: e.activation(ph[:], p[:], AF.Tanh)), [ps_src], [ph]),
            ("dve", (lambda e, ph=ph: e.tensor_tensor(ph[:], ph[:], psd[:], op=ALU.mult)), [ph, psd], [ph]),
            ("act", (lambda e, ph=ph, wA=wA: e.activation(wA[:], ph[:], AF.Sin)), [ph], [wA]),
            ("act", (lambda e, ph=ph: e.activation(ph[:], ph[:], AF.Abs)), [ph], [ph]),
            ("dve", (lambda e, ph=ph: e.tensor_scalar(ph[:], ph[:], HP_HI, HP_LO, op0=ALU.subtract, op1=ALU.subtract)), [ph], [ph]),
            ("act", (lambda e, ph=ph, wB=wB: e.activation(wB[:], ph[:], AF.Sin, scale=-1.0)), [ph], [wB]),
            ("act", (lambda e, ph=ph, wA=wA: e.activation(ph[:], wA[:], AF.Square)), [wA], [ph]),
            ("dve", (lambda e, wA=wA, wB=wB: e.scalar_tensor_tensor(wA[:], wA[:], 2.0, wB[:], op0=ALU.mult, op1=ALU.mult)), [wA, wB], [wA]),
            ("dve", (lambda e, ph=ph, wB=wB: e.tensor_scalar(wB[:], ph[:], -2.0, 1.0, op0=ALU.mult, op1=ALU.add)), [ph], [wB]),
        ]

    def a1_A(gg):
        tmproj_xn("v", "bv", gg, PS[0])
        tmproj_xn("k", "bkp", gg, PS[1])
        tmproj_xn("q", "bqp", gg, PS[2])

    def a1_emit(ops):
        for eng, fn, rd, wr in ops:
            P.emit(eng, fn, reads=rd, writes=wr)

    def a1_cumsum(gg, h):
        P.emit("pe", [
            (lambda e, h=h: e.matmul(PS[3][:, 0:512], lhsT=u128[:],
                                     rhs=brb[:, h * 512:(h + 1) * 512],
                                     start=True, stop=True)),
            (lambda e, h=h: e.matmul(PS[3][:, 512:1024], lhsT=u128[:],
                                     rhs=bib[:, h * 512:(h + 1) * 512],
                                     start=True, stop=True)),
        ], reads=[u128, brb, bib], writes=[PS[3]])

    def a1_combine(gg, h):
        P.emit("dve", lambda e, h=h: e.tensor_tensor(
            ph_[0][:, 0:512], PS[3][:, 0:512],
            wB_[1][:, h * 512:(h + 1) * 512], op=ALU.mult),
            reads=[PS[3], wB_[1]], writes=[ph_[0]])
        P.emit("dve", lambda e, h=h: e.tensor_tensor(
            cb[:, 0:512], PS[3][:, 512:1024],
            wA_[1][:, h * 512:(h + 1) * 512], op=ALU.mult),
            reads=[PS[3], wA_[1]], writes=[cb])
        acc_c = "scrA" if h == 0 else "scrB"
        P.emit("dve", lambda e, h=h, gg=gg, ac=acc_c: e.scalar_tensor_tensor(
            ret_v[:, RSLOT[gg], h * 512:(h + 1) * 512], ph_[0][:, 0:512], 1.0,
            cb[:, 0:512], op0=ALU.mult, op1=ALU.add, accum_out=colg(ac, gg)),
            reads=[ph_[0], cb], writes=[retr[RSLOT[gg]], cols])
        sq_c = "sqa" if h == 0 else "sqb"
        P.emit("act", lambda e, h=h, gg=gg, sc=sq_c: e.activation(
            cb[:, 512:1024], ret_v[:, RSLOT[gg], h * 512:(h + 1) * 512], AF.Square,
            accum_out=colg(sc, gg)),
            reads=[retr[RSLOT[gg]]], writes=[cb, cols])

    a1_A(0)
    for gg in range(NG):
        a1_emit(trig_ops(PS[1], 0))
        P.emit("dve", lambda e: e.tensor_tensor(bib[:], PS[0][:], wA_[0][:], op=ALU.mult),
               reads=[PS[0], wA_[0]], writes=[bib])
        P.emit("dve", lambda e: e.tensor_tensor(brb[:], PS[0][:], wB_[0][:], op=ALU.mult),
               reads=[PS[0], wB_[0]], writes=[brb])
        a1_cumsum(gg, 0)
        if gg + 1 < NG:
            tmproj_xn("v", "bv", gg + 1, PS[0])
            tmproj_xn("k", "bkp", gg + 1, PS[1])
        a1_emit(trig_ops(PS[2], 1))
        a1_combine(gg, 0)
        a1_cumsum(gg, 1)
        if gg + 1 < NG:
            tmproj_xn("q", "bqp", gg + 1, PS[2])
        a1_combine(gg, 1)
        P.emit("dve", lambda e, gg=gg: e.tensor_tensor(
            colg("sum2", gg), colg("scrA", gg), colg("scrB", gg), op=ALU.add),
            reads=[cols], writes=[cols])
        P.emit("dve", lambda e, gg=gg: e.tensor_tensor(
            colg("sq2", gg), colg("sqa", gg), colg("sqb", gg), op=ALU.add),
            reads=[cols], writes=[cols])
    if debug:
        P.dma("act", dbg["ret"], ret_sb[:], None, src_bufs=[ret_sb], name="dbgret")

    if stop_after == "a1":
        P.wait_all(); P.play(); return nc
    P.emit("dve", lambda e: e.tensor_scalar(colr("mu2"), colr("sum2"), 1.0 / D, None, op0=ALU.mult),
           reads=[cols], writes=[cols])
    P.emit("dve", lambda e: e.tensor_scalar(colr("scrA"), colr("sq2"), 1.0 / (D * D), None, op0=ALU.mult),
           reads=[cols], writes=[cols])
    P.emit("dve", lambda e: e.tensor_tensor(colr("scrB"), colr("mu2"), colr("mu2"), op=ALU.mult),
           reads=[cols], writes=[cols])
    P.emit("dve", lambda e: e.tensor_scalar(colr("scrB"), colr("scrB"), 1.0 / D, None, op0=ALU.mult),
           reads=[cols], writes=[cols])
    P.emit("dve", lambda e: e.tensor_tensor(colr("scrA"), colr("scrA"), colr("scrB"), op=ALU.subtract),
           reads=[cols], writes=[cols])
    P.emit("dve", lambda e: e.tensor_scalar(colr("scrA"), colr("scrA"), 1.0, 1e-5, op0=ALU.mult, op1=ALU.add),
           reads=[cols], writes=[cols])
    P.emit("act", lambda e: e.activation(colr("scrB"), colr("scrA"), AF.Ln),
           reads=[cols], writes=[cols])
    P.emit("act", lambda e: e.activation(colr("scrA"), colr("scrB"), AF.Exp, scale=-0.5),
           reads=[cols], writes=[cols])
    P.emit("dve", lambda e: e.tensor_scalar(colr("rstd2"), colr("scrA"), 1.0 / 32.0, None, op0=ALU.mult),
           reads=[cols], writes=[cols])
    b2scr = sb("b2scr", [128, 1024], F32, O_W["k"])
    P.dma("sp", b2scr[0:1, 0:1024], bo_d, b2scr, name="bo")
    fns = []
    for nch in range(8):
        for kc in range(8):
            fns.append(lambda e, nch=nch, kc=kc: e.matmul(
                PS[1][0:1, nch * 128:(nch + 1) * 128],
                lhsT=lnbc[:, kc:kc + 1],
                rhs=wsb["o"][:, kc * 1024 + nch * 128:kc * 1024 + nch * 128 + 128],
                start=(kc == 0), stop=(kc == 7)))
    P.emit("pe", fns, reads=[wsb["o"], lnbc], writes=[PS[1]])
    P.emit("dve", lambda e: e.tensor_tensor(row("bwo"), PS[1][0:1, 0:1024], b2scr[0:1, 0:1024], op=ALU.add),
           reads=[PS[1], b2scr], writes=[rows])
    for kc in range(8):
        P.emit("dve", lambda e, kc=kc: e.tensor_scalar(
            wsb["o"][:, kc * 1024:(kc + 1) * 1024],
            wsb["o"][:, kc * 1024:(kc + 1) * 1024],
            cols[:, CP["lng"] + kc:CP["lng"] + kc + 1], None, op0=ALU.mult),
            reads=[wsb["o"], cols], writes=[wsb["o"]])

    if stop_after == "b2":
        P.wait_all(); P.play(); return nc
    x0 = ga("x0", [128, 8 * TA], F32, off=G0)
    x1 = ga("x1", [128, 8 * TA], F32, off=G0 + 8192)
    xh_tm = ga("xh_tm", [128, 1024], F32, off=G0 + 16384)
    xh_fm = ga("xh_fm", [128, 8 * TA], BF16, off=G0 + 20480)

    def a2b_xstage(ti):
        xb = (x0, x1)[ti % 2]
        t0 = ti * TA
        xbv = xb[:].rearrange("p (c t) -> p c t", c=8)
        P.dma("sp", xbv, x_fm_v[:, :, t0:t0 + TA], xb, name=f"xa{ti % 2}")
        for gl in range(TA // 128):
            gg = ti * (TA // 128) + gl
            P.emit("dve", lambda e, gg=gg: e.tensor_scalar(
                xh_tm[:], ret_v[:, RSLOT[gg], :], colg("mu2", gg), colg("rstd2", gg),
                op0=ALU.subtract, op1=ALU.mult),
                reads=[retr[RSLOT[gg]], cols], writes=[xh_tm])
            ps = PS[2 + gl % 2]
            P.emit("pe", [
                (lambda e, c=c, ps=ps: e.matmul(
                    ps[:, c * 128:(c + 1) * 128], lhsT=xh_tm[:, c * 128:(c + 1) * 128],
                    rhs=ident[:], is_transpose=True, start=True, stop=True))
                for c in range(8)],
                reads=[xh_tm, ident], writes=[ps])
            P.emit("act", lambda e, gl=gl, ps=ps: e.activation(
                xh_fm[:].rearrange("p (c t) -> p c t", c=8)[:, :, gl * 128:(gl + 1) * 128],
                ps[:].rearrange("p (c t) -> p c t", c=8), AF.Identity),
                reads=[ps], writes=[xh_fm])

    def a2b_mms(ti):
        for half in range(2):
            ps = PS[half]
            fns = []
            for nco in range(4):
                nch = half * 4 + nco
                for kc in range(8):
                    fns.append(lambda e, nch=nch, nco=nco, kc=kc, ps=ps:
                               e.matmul(ps[:, nco * TA:(nco + 1) * TA],
                                        lhsT=wsb["o"][:, kc * 1024 + nch * 128:kc * 1024 + nch * 128 + 128],
                                        rhs=xh_fm[:, kc * TA:(kc + 1) * TA],
                                        start=(kc == 0), stop=False))
                pb, c = RP["bwo"]
                fns.append(lambda e, nch=nch, nco=nco, ps=ps, pb=pb, c=c:
                           e.matmul(ps[:, nco * TA:(nco + 1) * TA],
                                    lhsT=rows[pb:pb + 1, c + nch * 128:c + nch * 128 + 128],
                                    rhs=ones[pb:pb + 1, 0:TA],
                                    start=False, stop=True))
            P.emit("pe", fns, reads=[wsb["o"], xh_fm, rows, ones], writes=[ps])

    def a2b_acc(ti):
        xb = (x0, x1)[ti % 2]
        t0 = ti * TA
        for half in range(2):
            ps = PS[half]
            P.emit("dve", lambda e, half=half, xb=xb, t0=t0: e.tensor_tensor(
                xb[:, half * 4 * TA:(half + 1) * 4 * TA],
                xb[:, half * 4 * TA:(half + 1) * 4 * TA],
                xn_sb[:].rearrange("p (c t) -> p c t", c=8)[:, half * 4:half * 4 + 4, t0:t0 + TA],
                op=ALU.add), reads=[xb, xn_sb], writes=[xb])
            P.emit("dve", lambda e, half=half, ps=ps, xb=xb: e.tensor_tensor(
                xb[:, half * 4 * TA:(half + 1) * 4 * TA],
                xb[:, half * 4 * TA:(half + 1) * 4 * TA],
                ps[:], op=ALU.add), reads=[xb, ps], writes=[xb])
        P.dma("act", acc_v[:, :, t0:t0 + TA],
              xb[:].rearrange("p (c t) -> p c t", c=8), acc_h,
              src_bufs=[xb], name="accst")


    a2b_xstage(0)
    for ti in range(NTA):
        a2b_mms(ti)
        if ti + 1 < NTA:
            a2b_xstage(ti + 1)
        a2b_acc(ti)

    h1 = sb("h1_sb", [128, 32 * TF], BF16, 32768)
    h2 = sb("h2_sb", [128, 32 * TF], BF16, 65536)
    kst = [sb("kst0", [128, 8192], BF16, 98304),
           sb("kst1", [128, 8192], BF16, 114688),
           sb("kst2", [128, 8192], BF16, G0)]
    accin = sb("accin", [128, 8 * TF], F32, 131072)
    outb = sb("outb", [128, 8 * TF], F32, 147456)

    def srow_build(ps, src_h, hi_nm, lo_nm, f_nm):
        P.emit("pe", [
            (lambda e, kc=kc, ps=ps: e.matmul(
                ps[0:1, 0:TF], lhsT=ones[:, 0:1],
                rhs=src_h[:, kc * TF:(kc + 1) * TF],
                start=(kc == 0), stop=(kc == 31)))
            for kc in range(32)], reads=[ones, src_h], writes=[ps])
        P.emit("dve", lambda e, ps=ps: e.tensor_copy(rowf(f_nm), ps[0:1, 0:TF]),
               reads=[ps], writes=[rowsf])
        P.emit("dve", lambda e: e.tensor_copy(row(hi_nm, TF), rowf(f_nm)),
               reads=[rowsf], writes=[rows])
        P.emit("dve", lambda e: e.tensor_tensor(rowf("scr"), rowf(f_nm), row(hi_nm, TF), op=ALU.subtract),
               reads=[rowsf, rows], writes=[rowsf])
        P.emit("dve", lambda e: e.tensor_copy(row(lo_nm, TF), rowf("scr")),
               reads=[rowsf], writes=[rows])

    for tj in range(NTF):
        t0 = tj * TF
        abv = accin[:].rearrange("p (c t) -> p c t", c=8)
        P.dma("pool", abv, acc_v[:, :, t0:t0 + TF], accin, src_bufs=[acc_h],
              name="accin")
        for g in range(16):
            kb = kst[g % 3]
            P.dma("sp", kb[:, 0:2048].rearrange("p (k n) -> p k n", k=8),
                  k0dbf[g].rearrange("k p n -> p k n"), kb,
                  src_bufs=[k0dbf_h], name=f"k0s{g % 2}")
            ps = PS[g % 4]
            fns = []
            for nco in range(2):
                for kc in range(8):
                    fns.append(lambda e, nco=nco, kc=kc, ps=ps, kb=kb, t0=t0:
                               e.matmul(ps[:, nco * TF:(nco + 1) * TF],
                                        lhsT=kb[:, kc * 256 + nco * 128:kc * 256 + nco * 128 + 128],
                                        rhs=xn_v[:, kc, t0:t0 + TF],
                                        start=(kc == 0), stop=(kc == 7)))
            P.emit("pe", fns, reads=[kb, xn_sb], writes=[ps])
            P.emit("act", lambda e, g=g, ps=ps: e.activation(
                h1[:, g * 1024:(g + 1) * 1024], ps[:], AF.Relu, scale=sgl("two_s0")),
                reads=[ps, cols], writes=[h1])
        srow_build(PS[0], h1, "s1hi", "s1lo", "s1")
        for g in range(16):
            kb = kst[g % 3]
            P.dma("sp", kb[:].rearrange("p (k n) -> p k n", k=32),
                  k1bf[g].rearrange("k p n -> p k n"), kb,
                  src_bufs=[k1bf_h], name=f"k1s{g % 3}")
            ps = PS[g % 4]
            fns = []
            for nco in range(2):
                for kc in range(32):
                    fns.append(lambda e, nco=nco, kc=kc, ps=ps, kb=kb:
                               e.matmul(ps[:, nco * TF:(nco + 1) * TF],
                                        lhsT=kb[:, kc * 256 + nco * 128:kc * 256 + nco * 128 + 128],
                                        rhs=h1[:, kc * TF:(kc + 1) * TF],
                                        start=(kc == 0), stop=False))
            P.emit("pe", fns, reads=[kb, h1], writes=[ps])
            fns = []
            for nco in range(2):
                fns.append(lambda e, nco=nco, ps=ps: e.matmul(
                    ps[:, nco * TF:(nco + 1) * TF], lhsT=row("mh0", 128),
                    rhs=row("s1hi", TF), start=False, stop=False))
                fns.append(lambda e, nco=nco, ps=ps: e.matmul(
                    ps[:, nco * TF:(nco + 1) * TF], lhsT=row("mh0", 128),
                    rhs=row("s1lo", TF), start=False, stop=True))
            P.emit("pe", fns, reads=[rows], writes=[ps])
            P.emit("act", lambda e, g=g, ps=ps: e.activation(
                h2[:, g * 1024:(g + 1) * 1024], ps[:], AF.Relu, scale=sgl("four_s1")),
                reads=[ps, cols], writes=[h2])
        srow_build(PS[0], h2, "s2hi", "s2lo", "s2")
        for g in range(4):
            kb = kst[1 + g % 2]
            P.dma("sp", kb[:].rearrange("p (k n) -> p k n", k=32),
                  k2bf[g].rearrange("k p n -> p k n"), kb,
                  src_bufs=[k2bf_h], name=f"k2s{g % 3}")
            ps = PS[g % 4]
            fns = []
            for nco in range(2):
                for kc in range(32):
                    fns.append(lambda e, nco=nco, kc=kc, ps=ps, kb=kb:
                               e.matmul(ps[:, nco * TF:(nco + 1) * TF],
                                        lhsT=kb[:, kc * 256 + nco * 128:kc * 256 + nco * 128 + 128],
                                        rhs=h2[:, kc * TF:(kc + 1) * TF],
                                        start=(kc == 0), stop=False))
            P.emit("pe", fns, reads=[kb, h2], writes=[ps])
            fns = []
            for nco in range(2):
                fns.append(lambda e, nco=nco, ps=ps: e.matmul(
                    ps[:, nco * TF:(nco + 1) * TF], lhsT=row("mh0", 128),
                    rhs=row("s2hi", TF), start=False, stop=False))
                fns.append(lambda e, nco=nco, ps=ps: e.matmul(
                    ps[:, nco * TF:(nco + 1) * TF], lhsT=row("mh0", 128),
                    rhs=row("s2lo", TF), start=False, stop=True))
            P.emit("pe", fns, reads=[rows], writes=[ps])
            P.emit("act", lambda e, g=g, ps=ps: e.activation(
                outb[:, g * 1024:(g + 1) * 1024], ps[:], AF.Identity,
                scale=sgl("four_s2")), reads=[ps, cols], writes=[outb])
        P.emit("dve", lambda e: e.tensor_tensor(outb[:], outb[:], accin[:], op=ALU.add),
               reads=[outb, accin], writes=[outb])
        P.dma("act", out_v[:, :, t0:t0 + TF],
              outb[:].rearrange("p (c t) -> p c t", c=8), None,
              src_bufs=[outb], name="outst")
        if debug and tj == 0:
            P.dma("act", dbg["h1"], h1[:], None, src_bufs=[h1], name="dbgh1")
            P.dma("act", dbg["h2"], h2[:], None, src_bufs=[h2], name="dbgh2")

    P.wait_all()
    P.play()
    return nc


def _prep_shared(inputs):
    f = np.ascontiguousarray
    d = {}
    for k, nm in (("k", "Wk"), ("v", "Wv"), ("q", "Wq"), ("kp", "Wkp"),
                  ("qp", "Wqp"), ("o", "Wo")):
        w = np.asarray(inputs[f"L0_attn_{nm}"], dtype=np.float32)
        d[f"w_{k}"] = f(w.reshape(8, 128, 8, 128).transpose(1, 0, 2, 3)
                        .reshape(128, 8192))
    for k, nm in (("v", "bv"), ("kp", "bkp"), ("qp", "bqp"), ("o", "bo")):
        d[f"b_{k}"] = f(np.asarray(inputs[f"L0_attn_{nm}"],
                                   dtype=np.float32).reshape(1, D))
    for nm, key in (("Wk", "wkT"), ("Wq", "wqT")):
        w = np.asarray(inputs[f"L0_attn_{nm}"], dtype=np.float32).T
        d[key] = f(w.reshape(8, 128, 8, 128).transpose(1, 0, 2, 3)
                   .reshape(128, 8192))
    d["bkc"] = f(np.asarray(inputs["L0_attn_bk"], dtype=np.float32)
                 .reshape(8, 128).T)
    d["bqc"] = f(np.asarray(inputs["L0_attn_bq"], dtype=np.float32)
                 .reshape(8, 128).T)
    d["ps_r"] = f(np.asarray(inputs["L0_attn_phase_scale"],
                             dtype=np.float32).reshape(1, D))
    d["lng_c"] = f(np.asarray(inputs["L0_attn_ln_g"],
                              dtype=np.float32).reshape(8, 128).T)
    d["lnb_c"] = f(np.asarray(inputs["L0_attn_ln_b"],
                              dtype=np.float32).reshape(8, 128).T)
    k0 = np.asarray(inputs["L0_ffn_k0"]).view(np.uint8)
    d["k0t"] = f(k0[:D].reshape(8, 128, DF).transpose(1, 0, 2).reshape(128, 8 * DF))
    d["k0b"] = f(k0[D:].reshape(8, 128, DF).transpose(1, 0, 2).reshape(128, 8 * DF))
    k1 = np.asarray(inputs["L0_ffn_k1"]).view(np.uint8)
    d["k1t"] = f(k1.reshape(32, 128, 16, 256).transpose(2, 0, 1, 3))
    k2 = np.asarray(inputs["L0_ffn_k2"]).view(np.uint8)
    d["k2t"] = f(k2.reshape(32, 128, 4, 256).transpose(2, 0, 1, 3))
    for i in (0, 1, 2):
        d[f"s{i}_c"] = np.full((128, 1), np.asarray(inputs[f"L0_ffn_s{i}"],
                                                    dtype=np.float32))
    import ml_dtypes
    p = np.arange(128)[:, None]; q = np.arange(128)[None, :]
    d["u128_c"] = (((p <= q) & ((p < 64) == (q < 64)))
                   .astype(ml_dtypes.bfloat16))
    d["ident_c"] = np.eye(128, dtype=np.float32)
    return d


def kernel(**inputs):
    nc = build(debug=False)
    shared = _prep_shared(inputs)
    x = np.asarray(inputs["L0_attn_x"], dtype=np.float32).reshape(S, D)
    in_maps = []
    for c in range(NCORE):
        xs = x[c * TOK:(c + 1) * TOK]
        m = dict(shared)
        m["x_tm"] = np.ascontiguousarray(xs)
        m["x_fm"] = np.ascontiguousarray(xs.T)
        in_maps.append(m)
    res = run_bass_kernel_spmd(nc, in_maps, core_ids=list(range(NCORE)))
    out = np.concatenate([r["out_fm"].T for r in res.results], axis=0)
    return out.reshape(1, S, D).astype(np.float32)


# version 34 (final)
# speedup vs baseline: 1.0548x; 1.0051x over previous
import math

import numpy as np

import concourse.bass as bass
import concourse.mybir as mybir
from concourse.bass_utils import run_bass_kernel_spmd

F32 = mybir.dt.float32
BF16 = mybir.dt.bfloat16
U8 = mybir.dt.uint8
I16 = mybir.dt.int16
I32 = mybir.dt.int32
AF = mybir.ActivationFunctionType
ALU = mybir.AluOpType
PI = math.pi
INV2PI = 1.0 / (2.0 * PI)
TWOPI = 2.0 * PI
TP_HI = float(np.float32(2 * np.pi))
TP_LO = float(np.float32(2 * np.pi - np.float64(np.float32(2 * np.pi))))
HP_HI = float(np.float32(np.pi / 2))
HP_LO = float(np.float32(np.pi / 2 - np.float64(np.float32(np.pi / 2))))

NCORE = 8
S, D = 16384, 1024
TOK = S // NCORE
NG = TOK // 128
TA = 256
NTA = TOK // TA
TF = 512
NTF = TOK // TF
DF = 4096

SBUF_BASE = 16512
SBUF_TOP = 229344
ENGS = ("pe", "act", "dve", "pool", "sp")
DT_SZ = {F32: 4, BF16: 2, U8: 1, I16: 2, I32: 4}


class Prog:

    def __init__(self, nc):
        self.nc = nc
        self.ops = {e: [] for e in ENGS}
        self.esem = {e: nc.alloc_semaphore(f"sem_{e}") for e in ENGS}
        self.ecnt = {e: 0 for e in ENGS}
        self.water = {e: {} for e in ENGS}
        self.bufs = {}
        self.dsem = {}
        self.sems = dict(self.esem)

    def reg(self, h, lo=None, size=None):
        self.bufs[id(h)] = dict(h=h, iv=(lo, lo + size) if lo is not None else None,
                                w={}, r=[])
        return h

    def sballoc(self, name, shape, dtype, off):
        per = int(np.prod(shape[1:])) * DT_SZ[dtype]
        assert SBUF_BASE + off + per <= SBUF_TOP, (name, off, per)
        h = self.nc.alloc_sbuf_tensor_at(name, list(shape), dtype,
                                         offset=SBUF_BASE + off)
        return self.reg(h, off, per)

    def _overlaps(self, b):
        me = self.bufs[id(b)]
        if me["iv"] is None:
            return [me]
        lo, hi = me["iv"]
        return [m for m in self.bufs.values()
                if m["iv"] is not None and m["iv"][0] < hi and lo < m["iv"][1]]

    def _need(self, eng, mark):
        if mark is None:
            return None
        k, v = mark
        if k == eng == "pe":
            return None
        if self.water[eng].get(k, 0) >= v:
            return None
        return mark

    def _collect(self, eng, reads, writes):
        need = {}

        def add(mark):
            m = self._need(eng, mark)
            if m:
                need[m[0]] = max(need.get(m[0], 0), m[1])

        for b in reads:
            if id(b) in self.bufs:
                for m in self._overlaps(b):
                    for kv in m["w"].items():
                        add(kv)
        for b in writes:
            if id(b) in self.bufs:
                for m in self._overlaps(b):
                    for kv in m["w"].items():
                        add(kv)
                    for rr in m["r"]:
                        add(rr)
        return need

    def _emit_waits(self, eng, need):
        for k, v in sorted(need.items(), key=lambda kv: str(kv[0])):
            self.water[eng][k] = v
            sem = self.sems[k]
            self.ops[eng].append(lambda e, s=sem, vv=v: e.wait_ge(s, vv))

    def _mark(self, reads, writes, semkey, val):
        for b in writes:
            if id(b) not in self.bufs:
                self.reg(b)
            m = self.bufs[id(b)]
            m["w"][semkey] = val
            m["r"] = [r for r in m["r"] if r[0] != semkey or r[1] > val]
        for b in reads:
            if id(b) in self.bufs:
                self.bufs[id(b)]["r"].append((semkey, val))

    def emit(self, eng, fns, reads=(), writes=()):
        if not isinstance(fns, (list, tuple)):
            fns = [fns]
        need = self._collect(eng, reads, writes)
        self._emit_waits(eng, need)
        sem = self.esem[eng]
        for f in fns[:-1]:
            self.ops[eng].append(f)
        self.ecnt[eng] += 1
        val = self.ecnt[eng]
        self.ops[eng].append(lambda e, f=fns[-1], s=sem: f(e).then_inc(s, 1))
        self._mark(reads, writes, eng, val)
        return (eng, val)

    def dma(self, queue, dst, src, dst_buf, src_bufs=(), name="d", after=(),
            **kw):
        key = id(dst_buf) if dst_buf is not None else ("anon", len(self.dsem))
        if key not in self.dsem:
            h = self.nc.alloc_semaphore(f"sd_{name}_{len(self.dsem)}")
            self.dsem[key] = [h, 0]
            self.sems[("d", key)] = h
        ent = self.dsem[key]
        writes = [dst_buf] if dst_buf is not None else []
        need = self._collect(queue, list(src_bufs) + list(after), writes)
        m = self._need(queue, (("d", key), ent[1])) if ent[1] > 0 else None
        if m:
            need[m[0]] = max(need.get(m[0], 0), m[1])
        self._emit_waits(queue, need)
        ent[1] += 16
        total, h = ent[1], ent[0]
        self.ops[queue].append(
            lambda e, d=dst, s=src, hh=h, kws=kw: e.dma_start(d, s, **kws).then_inc(hh, 16))
        self._mark(list(src_bufs), writes, ("d", key), total)
        return (("d", key), total)

    def wait_all(self, engs=ENGS):
        marks = [(e, self.ecnt[e]) for e in ENGS if self.ecnt[e] > 0]
        marks += [(("d", k), v[1]) for k, v in self.dsem.items() if v[1] > 0]
        for eng in engs:
            need = {}
            for k, v in marks:
                m = self._need(eng, (k, v))
                if m:
                    need[m[0]] = max(need.get(m[0], 0), m[1])
            self._emit_waits(eng, need)

    def play(self):
        with self.nc.Block() as block:
            block.tensor(lambda e: [f(e) for f in self.ops["pe"]])
            block.scalar(lambda e: [f(e) for f in self.ops["act"]])
            block.vector(lambda e: [f(e) for f in self.ops["dve"]])
            block.gpsimd(lambda e: [f(e) for f in self.ops["pool"]])
            block.sync(lambda e: [f(e) for f in self.ops["sp"]])


def build(debug=False, stop_after=None):
    nc = bass.Bass("TRN2", target_bir_lowering=False)
    P = Prog(nc)

    def din(name, shape, dtype):
        return nc.dram_tensor(name, list(shape), dtype, kind="ExternalInput").ap()

    x_fm = din("x_fm", [D, TOK], F32)
    x_tm = din("x_tm", [TOK, D], F32)
    w_in = {k: din(f"w_{k}", [128, 8192], F32)
            for k in ("k", "kp", "v", "q", "qp", "o")}
    k0t_d = din("k0t", [128, 8 * DF], U8)
    k0b_d = din("k0b", [128, 8 * DF], U8)
    k1_d = din("k1t", [16, 32, 128, 256], U8)
    k2_d = din("k2t", [4, 32, 128, 256], U8)
    brow_in = {k: din(f"b_{k}", [1, D], F32)
               for k in ("v", "kp", "qp")}
    wkT_d = din("wkT", [128, 8192], F32)
    wqT_d = din("wqT", [128, 8192], F32)
    bkc_d = din("bkc", [128, 8], F32)
    bqc_d = din("bqc", [128, 8], F32)
    bo_d = din("b_o", [1, D], F32)
    ps_d = din("ps_r", [1, D], F32)
    lng_d = din("lng_c", [128, 8], F32)
    lnb_d = din("lnb_c", [128, 8], F32)
    s_d = {k: din(f"s{k}_c", [128, 1], F32) for k in (0, 1, 2)}
    u128_d = din("u128_c", [128, 128], BF16)
    ident_d = din("ident_c", [128, 128], F32)

    out_d = nc.dram_tensor("out_fm", [D, TOK], F32, kind="ExternalOutput").ap()
    dbg = {}
    if debug:
        for nm, shp in (("xn", [128, 8 * TOK]), ("ret", [128, NG * D]),
                        ("h1", [128, 32 * TF]), ("h2", [128, 32 * TF])):
            dbg[nm] = nc.dram_tensor("dbg_" + nm, shp, BF16,
                                     kind="ExternalOutput").ap()
        dbg["acc"] = nc.dram_tensor("dbg_acc", [D, TOK], F32,
                                    kind="ExternalOutput").ap()
        for nm in ("br", "bi", "sq", "cosq"):
            dbg[nm] = nc.dram_tensor("dbg_" + nm, [128, 1024], BF16,
                                     kind="ExternalOutput").ap()
        for nm in ("memr", "memi"):
            dbg[nm] = nc.dram_tensor("dbg_" + nm, [128, 1024], BF16,
                                     kind="ExternalOutput").ap()
        dbg["u128"] = nc.dram_tensor("dbg_u128", [128, 128], BF16,
                                     kind="ExternalOutput").ap()

    def dint(name, shape, dtype):
        return nc.dram_tensor(name, list(shape), dtype, kind="Internal").ap()

    k1bf = dint("k1bf", [16, 32, 128, 256], BF16)
    k2bf = dint("k2bf", [4, 32, 128, 256], BF16)
    k0dbf = dint("k0dbf", [16, 8, 128, 256], BF16)
    acc_d = dint("acc_d", [D, TOK], F32)
    k1bf_h = P.reg(object())
    k2bf_h = P.reg(object())
    k0dbf_h = P.reg(object())
    acc_h = P.reg(object())

    O_XN = 0
    O_W = {k: 32768 + i * 16384 for i, k in enumerate(("k", "kp", "v", "q", "qp"))}
    O_WO = 114688
    O_RET = 131072
    O_ROWS = 163840
    O_ROWSF = 176128
    E_END = 180224
    O_ONES = 180224
    O_PSD = 180736
    O_U128 = 184832
    O_IDENT = 185152
    O_COLS = 185664
    O_LNB = 186688
    G0 = 186816
    GTOP = SBUF_TOP - SBUF_BASE
    assert G0 + 24576 <= GTOP, (G0, GTOP)

    sb = P.sballoc
    xn_sb = sb("xn_sb", [128, 8 * TOK], BF16, O_XN)
    wsb = {k: sb(f"w{k}_sb", [128, 8192], BF16, O_W[k]) for k in O_W}
    wsb["o"] = sb("wo_sb", [128, 8192], BF16, O_WO)
    ret_sb = sb("ret_sb", [128, NG * D], BF16, O_RET)
    rows = sb("rows_sb", [128, 6144], BF16, O_ROWS)
    rowsf = sb("rowsf_sb", [128, 1024], F32, O_ROWSF)
    ones = sb("ones_sb", [128, 256], BF16, O_ONES)
    psd = sb("psd_sb", [128, 1024], F32, O_PSD)
    u128 = sb("u128_sb", [128, 128], BF16, O_U128)
    ident = sb("ident_sb", [128, 128], F32, O_IDENT)
    cols = sb("cols_sb", [128, 256], F32, O_COLS)
    lnbc = sb("lnbc_sb", [128, 8], BF16, O_LNB)
    bkcs = sb("bkc_sb", [128, 8], BF16, O_LNB + 32)
    bqcs = sb("bqc_sb", [128, 8], BF16, O_LNB + 64)

    RP = {"bkp": (0, 0), "bqp": (0, 1024), "mh0": (0, 2048),
          "bv": (0, 3072),
          "s1hi": (0, 3072), "s1lo": (0, 3584),
          "s2hi": (0, 4096), "s2lo": (0, 4608),
          "bwo": (0, 5120)}

    def row(nm, n=1024, c0=0):
        p, c = RP[nm]
        return rows[p:p + 1, c + c0:c + c0 + n]

    RF = {"s1": (0, 0, 512), "s2": (0, 512, 512), "scr": (32, 0, 512)}

    def rowf(nm):
        p, c, n = RF[nm]
        return rowsf[p:p + 1, c:c + n]

    CP = dict(sum1=0, sq1=16, mu1=32, rstd1=48, sum2=64, sq2=80, mu2=96,
              rstd2=112, scrA=128, scrB=144, lng=160, singles=176,
              sqa=192, sqb=208)
    SGL = dict(two_s0=176, four_s1=177, four_s2=178, hpi=179)

    def colg(nm, g):
        return cols[:, CP[nm] + g:CP[nm] + g + 1]

    def colr(nm, n=16):
        return cols[:, CP[nm]:CP[nm] + n]

    def sgl(nm):
        return cols[:, SGL[nm]:SGL[nm] + 1]

    PS = [P.reg(nc.alloc_psum_tensor(f"ps{i}", [128, 1024], F32))
          for i in range(4)]

    xn_v = xn_sb[:].rearrange("p (c t) -> p c t", c=8)
    xng = [P.reg(object(), lo=O_XN + g * 256, size=256) for g in range(NG)]
    x_tmf = sb("xtm_sb", [128, NG * D], F32, O_WO)
    xtm_v = x_tmf[:].rearrange("p (g f) -> p g f", g=NG)
    XSLOT = [NG - 1 - g for g in range(NG)]
    xtmr = [P.reg(object(), lo=O_WO + XSLOT[g] * 4096, size=4096)
            for g in range(NG)]
    x_fm_v = x_fm.rearrange("(c p) t -> p c t", p=128)
    acc_v = acc_d.rearrange("(c p) t -> p c t", p=128)
    out_v = out_d.rearrange("(c p) t -> p c t", p=128)
    ret_v = ret_sb[:].rearrange("p (g f) -> p g f", g=NG)
    RSLOT = [NG - 1 - g for g in range(NG)]
    retr = [P.reg(object(), lo=O_RET + sl * 2048, size=2048) for sl in range(NG)]

    g_off = G0

    def ga(name, shape, dtype, off=None):
        nonlocal g_off
        if off is None:
            off = g_off
            per = int(np.prod(shape[1:])) * DT_SZ[dtype]
            g_off += (per + 63) & ~63
        assert off + int(np.prod(shape[1:])) * DT_SZ[dtype] <= GTOP
        return sb(name, shape, dtype, off)

    scs = {k: ga(f"sc{k}", [128, 1], F32) for k in (0, 1, 2)}
    p0r = ga("p0r", [128, 1024], F32)
    st_a = sb("st_a", [128, DF], BF16, O_WO)
    st_b = sb("st_b", [128, DF], BF16, O_WO + 8192)
    st_dd = sb("st_dd", [128, DF], BF16, O_WO + 16384)
    psc_a = sb("psc_a", [128, 2048], BF16, O_WO + 24576)
    psc_f = sb("psc_f", [128, 2048], F32, O_WO + 28672)

    P.dma("sp", u128[:], u128_d, u128, name="u128")
    P.dma("sp", ident[:], ident_d, ident, name="ident")
    P.emit("dve", lambda e: e.memset(ones[:], 1.0), writes=[ones])
    P.emit("dve", lambda e: e.memset(sgl("hpi"), PI / 2.0), writes=[cols])

    P.dma("pool", lnbc[:], lnb_d, lnbc, name="lnb")
    P.dma("pool", bkcs[:], bkc_d, bkcs, name="bkc")
    P.dma("pool", bqcs[:], bqc_d, bqcs, name="bqc")
    stK = sb("stK", [128, 8192], BF16, O_WO)
    for wtd, wpname, dstname in ((wkT_d, "kp", "k"), (wqT_d, "qp", "q")):
        P.dma("pool", wsb[wpname][:], w_in[wpname], wsb[wpname],
              name=f"w{wpname}", max_dma_last_dim=8192)
        P.dma("pool", stK[:], wtd, stK, name="wT", max_dma_last_dim=8192)
        for dc in range(8):
            for h in range(2):
                ps = PS[(dc * 2 + h) % 4]
                fns = [
                    (lambda e, kfc=kfc, dc=dc, h=h, ps=ps, wp=wpname:
                     e.matmul(ps[:, 0:512],
                              lhsT=stK[:, kfc * 1024 + dc * 128:kfc * 1024 + dc * 128 + 128],
                              rhs=wsb[wp][:, kfc * 1024 + h * 512:kfc * 1024 + h * 512 + 512],
                              start=(kfc == 0), stop=(kfc == 7)))
                    for kfc in range(8)]
                P.emit("pe", fns, reads=[stK, wsb[wpname]], writes=[ps])
                P.emit("act", lambda e, dc=dc, h=h, ps=ps, dn=dstname: e.activation(
                    wsb[dn][:, dc * 1024 + h * 512:dc * 1024 + h * 512 + 512],
                    ps[:, 0:512], AF.Identity), reads=[ps], writes=[wsb[dstname]])
    for k in (0, 1, 2):
        P.dma("sp", scs[k][:], s_d[k], scs[k], name=f"s{k}")
    P.dma("sp", colr("lng", 8), lng_d, cols, name="lng")
    P.emit("dve", lambda e: e.tensor_scalar(sgl("two_s0"), scs[0][:], 2.0, None, op0=ALU.mult),
           reads=[scs[0]], writes=[cols])
    P.emit("dve", lambda e: e.tensor_scalar(sgl("four_s1"), scs[1][:], 4.0, None, op0=ALU.mult),
           reads=[scs[1]], writes=[cols])
    P.emit("dve", lambda e: e.tensor_scalar(sgl("four_s2"), scs[2][:], 4.0, None, op0=ALU.mult),
           reads=[scs[2]], writes=[cols])

    P.dma("pool", wsb["v"][:], w_in["v"], wsb["v"], name="wv",
          max_dma_last_dim=8192)
    P.dma("sp", psc_f[0:1, 0:1024], ps_d, psc_f, name="psr")
    P.emit("dve", lambda e: e.tensor_copy(psc_a[0:1, 0:1024], psc_f[0:1, 0:1024]),
           reads=[psc_f], writes=[psc_a])
    P.emit("dve", lambda e: e.tensor_tensor(psc_f[0:1, 1024:2048], psc_f[0:1, 0:1024], psc_a[0:1, 0:1024], op=ALU.subtract),
           reads=[psc_f, psc_a], writes=[psc_f])
    P.emit("dve", lambda e: e.tensor_copy(psc_a[0:1, 1024:2048], psc_f[0:1, 1024:2048]),
           reads=[psc_f], writes=[psc_a])
    P.emit("pe", [
        lambda e: e.matmul(PS[0][:, 0:512], lhsT=ones[0:1, 0:128], rhs=psc_a[0:1, 0:512], start=True, stop=False),
        lambda e: e.matmul(PS[0][:, 0:512], lhsT=ones[0:1, 0:128], rhs=psc_a[0:1, 1024:1536], start=False, stop=True),
        lambda e: e.matmul(PS[0][:, 512:1024], lhsT=ones[0:1, 0:128], rhs=psc_a[0:1, 512:1024], start=True, stop=False),
        lambda e: e.matmul(PS[0][:, 512:1024], lhsT=ones[0:1, 0:128], rhs=psc_a[0:1, 1536:2048], start=False, stop=True),
    ], reads=[ones, psc_a], writes=[PS[0]])
    P.emit("dve", lambda e: e.tensor_scalar(psd[:], PS[0][:], 0.5, None, op0=ALU.mult),
           reads=[PS[0]], writes=[psd])

    def bias_builds():
        P.dma("sp", p0r[0:1, 0:1024], brow_in["v"], p0r, name="brow")
        P.emit("dve", lambda e: e.tensor_copy(row("bv"), p0r[0:1, 0:1024]),
               reads=[p0r], writes=[rows])
        for bnm, bc, wpname, slot in (("kp", bkcs, "kp", "bkp"),
                                      ("qp", bqcs, "qp", "bqp")):
            P.dma("sp", p0r[0:1, 0:1024], brow_in[bnm], p0r, name="brow")
            fns = []
            for h in range(2):
                for kfc in range(8):
                    fns.append(lambda e, kfc=kfc, h=h, bc=bc, wp=wpname: e.matmul(
                        PS[3][0:1, h * 512:h * 512 + 512],
                        lhsT=bc[:, kfc:kfc + 1],
                        rhs=wsb[wp][:, kfc * 1024 + h * 512:kfc * 1024 + h * 512 + 512],
                        start=(kfc == 0), stop=(kfc == 7)))
            P.emit("pe", fns, reads=[bc, wsb[wpname]], writes=[PS[3]])
            P.emit("dve", lambda e, slot=slot: e.tensor_tensor(
                row(slot), PS[3][0:1, 0:1024], p0r[0:1, 0:1024], op=ALU.add),
                reads=[PS[3], p0r], writes=[rows])
        P.emit("dve", lambda e: e.memset(row("mh0", 128), -0.5), writes=[rows])

    a0_scr = [ga("a0s0", [128, 1024], BF16), ga("a0s1", [128, 1024], BF16)]
    a0_xn = ga("a0xn", [128, 1024], F32)
    for g in range(NG):
        P.dma("sp", xtm_v[:, XSLOT[g], :], x_tm[g * 128:(g + 1) * 128, :],
              xtmr[g], name=f"xt{g}", after=(stK,) if g == 0 else ())
    for g in range(NG):
        sc = a0_scr[g % 2]
        P.emit("dve", lambda e, g=g, s=sc: e.tensor_scalar(
            s[:], xtm_v[:, XSLOT[g], :], 1.0, 0.0, op0=ALU.mult, op1=ALU.add,
            accum_out=colg("sum1", g)), reads=[xtmr[g]], writes=[sc, cols])
        P.emit("dve", lambda e, g=g, s=sc: e.scalar_tensor_tensor(
            s[:], xtm_v[:, XSLOT[g], :], 1.0, xtm_v[:, XSLOT[g], :],
            op0=ALU.mult, op1=ALU.mult, accum_out=colg("sq1", g)),
            reads=[xtmr[g]], writes=[sc, cols])
    P.emit("dve", lambda e: e.tensor_scalar(colr("mu1"), colr("sum1"), 1.0 / D, None, op0=ALU.mult),
           reads=[cols], writes=[cols])
    P.emit("dve", lambda e: e.tensor_scalar(colr("scrA"), colr("sq1"), 1.0 / D, None, op0=ALU.mult),
           reads=[cols], writes=[cols])
    P.emit("dve", lambda e: e.tensor_tensor(colr("scrB"), colr("mu1"), colr("mu1"), op=ALU.mult),
           reads=[cols], writes=[cols])
    P.emit("dve", lambda e: e.tensor_tensor(colr("scrA"), colr("scrA"), colr("scrB"), op=ALU.subtract),
           reads=[cols], writes=[cols])
    P.emit("dve", lambda e: e.tensor_scalar(colr("scrA"), colr("scrA"), 1.0, 1e-5, op0=ALU.mult, op1=ALU.add),
           reads=[cols], writes=[cols])
    P.emit("act", lambda e: e.activation(colr("scrB"), colr("scrA"), AF.Ln),
           reads=[cols], writes=[cols])
    P.emit("act", lambda e: e.activation(colr("rstd1"), colr("scrB"), AF.Exp, scale=-0.5),
           reads=[cols], writes=[cols])
    for g in range(NG):
        P.emit("dve", lambda e, g=g: e.tensor_scalar(
            a0_xn[:], xtm_v[:, XSLOT[g], :], colg("mu1", g), colg("rstd1", g),
            op0=ALU.subtract, op1=ALU.mult), reads=[xtmr[g], cols], writes=[a0_xn])
        ps = PS[g % 2]
        P.emit("pe", [
            (lambda e, g=g, c=c, ps=ps: e.matmul(
                ps[:, c * 128:(c + 1) * 128],
                lhsT=a0_xn[:, c * 128:(c + 1) * 128], rhs=ident[:],
                is_transpose=True, start=True, stop=True,
                )) for c in range(8)],
            reads=[a0_xn, ident], writes=[ps])
        P.emit("act", lambda e, g=g, ps=ps: e.activation(
            xn_v[:, :, g * 128:(g + 1) * 128],
            ps[:].rearrange("p (c t) -> p c t", c=8), AF.Identity),
            reads=[ps], writes=[xng[g]])
        if g == 3:
            bias_builds()
    if debug:
        P.dma("act", dbg["xn"], xn_sb[:], None, src_bufs=[xn_sb], name="dbgxn")

    if stop_after == "a0":
        P.wait_all(); P.play(); return nc
    for kc in range(8):
        P.dma("pool", st_a[:], k0t_d[:, kc * DF:(kc + 1) * DF], st_a,
              name="k0a", max_dma_last_dim=2048)
        P.dma("pool", st_b[:], k0b_d[:, kc * DF:(kc + 1) * DF], st_b,
              name="k0b", max_dma_last_dim=2048)
        P.emit("pool", lambda e: e.tensor_tensor(st_dd[:], st_a[:], st_b[:], op=ALU.subtract),
               reads=[st_a, st_b], writes=[st_dd])
        P.dma("sp", k0dbf[:, kc, :, :].rearrange("g p n -> p g n"),
              st_dd[:].rearrange("p (g n) -> p g n", g=16), k0dbf_h,
              src_bufs=[st_dd], name="k0d")

    if stop_after == "p0":
        P.wait_all(); P.play(); return nc


    P.dma("pool", wsb["o"][:], w_in["o"], wsb["o"], name="wo",
          max_dma_last_dim=8192)
    P.dma("pool", k1bf[0:8], k1_d[0:8], k1bf_h, name="k1bf", max_dma_last_dim=2048)
    P.dma("pool", k1bf[8:16], k1_d[8:16], k1bf_h, name="k1bf", max_dma_last_dim=2048)
    P.dma("pool", k2bf, k2_d, k2bf_h, name="k2bf", max_dma_last_dim=2048)

    ph_ = [ga("ph0", [128, 1024], F32, off=G0),
           ga("ph1", [128, 1024], F32, off=G0 + 4096)]
    wA_ = [ga("wA0", [128, 1024], BF16, off=G0 + 8192),
           ga("wA1", [128, 1024], BF16, off=G0 + 10240)]
    wB_ = [ga("wB0", [128, 1024], BF16, off=G0 + 12288),
           ga("wB1", [128, 1024], BF16, off=G0 + 14336)]
    brb = ga("brb", [128, 1024], BF16, off=G0 + 16384)
    bib = ga("bib", [128, 1024], BF16, off=G0 + 18432)
    cb = ga("cb", [128, 1024], F32, off=G0 + 20480)

    def tmproj_xn(wname, bslot, gg, ps):
        fns = []
        pb, c = RP[bslot]
        for h2 in range(2):
            for kc in range(8):
                fns.append(lambda e, kc=kc, h2=h2, ps=ps, gg=gg, wn=wname:
                           e.matmul(ps[:, h2 * 512:(h2 + 1) * 512],
                                    lhsT=xn_v[:, kc, gg * 128:gg * 128 + 128],
                                    rhs=wsb[wn][:, kc * 1024 + h2 * 512:kc * 1024 + (h2 + 1) * 512],
                                    start=(kc == 0), stop=False))
            fns.append(lambda e, h2=h2, ps=ps, pb=pb, c=c:
                       e.matmul(ps[:, h2 * 512:(h2 + 1) * 512],
                                lhsT=ones[pb:pb + 1, 0:128],
                                rhs=rows[pb:pb + 1, c + h2 * 512:c + (h2 + 1) * 512],
                                start=False, stop=True))
        P.emit("pe", fns, reads=[xn_sb, wsb[wname], rows, ones, xng[gg]],
               writes=[ps])

    def trig_ops(ps_src, par):
        ph, wA, wB = ph_[par], wA_[par], wB_[par]
        return [
            ("act", (lambda e, p=ps_src, ph=ph: e.activation(ph[:], p[:], AF.Tanh)), [ps_src], [ph]),
            ("dve", (lambda e, ph=ph: e.tensor_tensor(ph[:], ph[:], psd[:], op=ALU.mult)), [ph, psd], [ph]),
            ("act", (lambda e, ph=ph, wA=wA: e.activation(wA[:], ph[:], AF.Sin)), [ph], [wA]),
            ("act", (lambda e, ph=ph: e.activation(ph[:], ph[:], AF.Abs)), [ph], [ph]),
            ("dve", (lambda e, ph=ph: e.tensor_scalar(ph[:], ph[:], HP_HI, HP_LO, op0=ALU.subtract, op1=ALU.subtract)), [ph], [ph]),
            ("act", (lambda e, ph=ph, wB=wB: e.activation(wB[:], ph[:], AF.Sin, scale=-1.0)), [ph], [wB]),
            ("act", (lambda e, ph=ph, wA=wA: e.activation(ph[:], wA[:], AF.Square)), [wA], [ph]),
            ("dve", (lambda e, wA=wA, wB=wB: e.scalar_tensor_tensor(wA[:], wA[:], 2.0, wB[:], op0=ALU.mult, op1=ALU.mult)), [wA, wB], [wA]),
            ("dve", (lambda e, ph=ph, wB=wB: e.tensor_scalar(wB[:], ph[:], -2.0, 1.0, op0=ALU.mult, op1=ALU.add)), [ph], [wB]),
        ]

    def a1_A(gg):
        tmproj_xn("v", "bv", gg, PS[0])
        tmproj_xn("k", "bkp", gg, PS[1])
        tmproj_xn("q", "bqp", gg, PS[2])

    def a1_emit(ops):
        for eng, fn, rd, wr in ops:
            P.emit(eng, fn, reads=rd, writes=wr)

    def a1_cumsum(gg, h):
        P.emit("pe", [
            (lambda e, h=h: e.matmul(PS[3][:, 0:512], lhsT=u128[:],
                                     rhs=brb[:, h * 512:(h + 1) * 512],
                                     start=True, stop=True)),
            (lambda e, h=h: e.matmul(PS[3][:, 512:1024], lhsT=u128[:],
                                     rhs=bib[:, h * 512:(h + 1) * 512],
                                     start=True, stop=True)),
        ], reads=[u128, brb, bib], writes=[PS[3]])

    def a1_combine(gg, h):
        P.emit("dve", lambda e, h=h: e.tensor_tensor(
            rowsf[:, 0:512], PS[3][:, 0:512],
            wB_[1][:, h * 512:(h + 1) * 512], op=ALU.mult),
            reads=[PS[3], wB_[1]], writes=[rowsf])
        P.emit("dve", lambda e, h=h: e.tensor_tensor(
            cb[:, 0:512], PS[3][:, 512:1024],
            wA_[1][:, h * 512:(h + 1) * 512], op=ALU.mult),
            reads=[PS[3], wA_[1]], writes=[cb])
        acc_c = "scrA" if h == 0 else "scrB"
        P.emit("dve", lambda e, h=h, gg=gg, ac=acc_c: e.scalar_tensor_tensor(
            ret_v[:, RSLOT[gg], h * 512:(h + 1) * 512], rowsf[:, 0:512], 1.0,
            cb[:, 0:512], op0=ALU.mult, op1=ALU.add, accum_out=colg(ac, gg)),
            reads=[rowsf, cb], writes=[retr[RSLOT[gg]], cols])
        sq_c = "sqa" if h == 0 else "sqb"
        P.emit("act", lambda e, h=h, gg=gg, sc=sq_c: e.activation(
            cb[:, 512:1024], ret_v[:, RSLOT[gg], h * 512:(h + 1) * 512], AF.Square,
            accum_out=colg(sc, gg)),
            reads=[retr[RSLOT[gg]]], writes=[cb, cols])

    a1_A(0)
    for gg in range(NG):
        a1_emit(trig_ops(PS[1], 0))
        P.emit("dve", lambda e: e.tensor_tensor(bib[:], PS[0][:], wA_[0][:], op=ALU.mult),
               reads=[PS[0], wA_[0]], writes=[bib])
        P.emit("dve", lambda e: e.tensor_tensor(brb[:], PS[0][:], wB_[0][:], op=ALU.mult),
               reads=[PS[0], wB_[0]], writes=[brb])
        a1_cumsum(gg, 0)
        if gg + 1 < NG:
            tmproj_xn("v", "bv", gg + 1, PS[0])
            tmproj_xn("k", "bkp", gg + 1, PS[1])
        a1_emit(trig_ops(PS[2], 1))
        a1_combine(gg, 0)
        a1_cumsum(gg, 1)
        if gg + 1 < NG:
            tmproj_xn("q", "bqp", gg + 1, PS[2])
        a1_combine(gg, 1)
        P.emit("dve", lambda e, gg=gg: e.tensor_tensor(
            colg("sum2", gg), colg("scrA", gg), colg("scrB", gg), op=ALU.add),
            reads=[cols], writes=[cols])
        P.emit("dve", lambda e, gg=gg: e.tensor_tensor(
            colg("sq2", gg), colg("sqa", gg), colg("sqb", gg), op=ALU.add),
            reads=[cols], writes=[cols])
    if debug:
        P.dma("act", dbg["ret"], ret_sb[:], None, src_bufs=[ret_sb], name="dbgret")

    if stop_after == "a1":
        P.wait_all(); P.play(); return nc
    P.emit("dve", lambda e: e.tensor_scalar(colr("mu2"), colr("sum2"), 1.0 / D, None, op0=ALU.mult),
           reads=[cols], writes=[cols])
    P.emit("dve", lambda e: e.tensor_scalar(colr("scrA"), colr("sq2"), 1.0 / (D * D), None, op0=ALU.mult),
           reads=[cols], writes=[cols])
    P.emit("dve", lambda e: e.tensor_tensor(colr("scrB"), colr("mu2"), colr("mu2"), op=ALU.mult),
           reads=[cols], writes=[cols])
    P.emit("dve", lambda e: e.tensor_scalar(colr("scrB"), colr("scrB"), 1.0 / D, None, op0=ALU.mult),
           reads=[cols], writes=[cols])
    P.emit("dve", lambda e: e.tensor_tensor(colr("scrA"), colr("scrA"), colr("scrB"), op=ALU.subtract),
           reads=[cols], writes=[cols])
    P.emit("dve", lambda e: e.tensor_scalar(colr("scrA"), colr("scrA"), 1.0, 1e-5, op0=ALU.mult, op1=ALU.add),
           reads=[cols], writes=[cols])
    P.emit("act", lambda e: e.activation(colr("scrB"), colr("scrA"), AF.Ln),
           reads=[cols], writes=[cols])
    P.emit("act", lambda e: e.activation(colr("scrA"), colr("scrB"), AF.Exp, scale=-0.5),
           reads=[cols], writes=[cols])
    P.emit("dve", lambda e: e.tensor_scalar(colr("rstd2"), colr("scrA"), 1.0 / 32.0, None, op0=ALU.mult),
           reads=[cols], writes=[cols])
    b2scr = sb("b2scr", [128, 1024], F32, O_W["k"])
    P.dma("sp", b2scr[0:1, 0:1024], bo_d, b2scr, name="bo")
    fns = []
    for nch in range(8):
        for kc in range(8):
            fns.append(lambda e, nch=nch, kc=kc: e.matmul(
                PS[1][0:1, nch * 128:(nch + 1) * 128],
                lhsT=lnbc[:, kc:kc + 1],
                rhs=wsb["o"][:, kc * 1024 + nch * 128:kc * 1024 + nch * 128 + 128],
                start=(kc == 0), stop=(kc == 7)))
    P.emit("pe", fns, reads=[wsb["o"], lnbc], writes=[PS[1]])
    P.emit("dve", lambda e: e.tensor_tensor(row("bwo"), PS[1][0:1, 0:1024], b2scr[0:1, 0:1024], op=ALU.add),
           reads=[PS[1], b2scr], writes=[rows])
    for kc in range(8):
        P.emit("dve", lambda e, kc=kc: e.tensor_scalar(
            wsb["o"][:, kc * 1024:(kc + 1) * 1024],
            wsb["o"][:, kc * 1024:(kc + 1) * 1024],
            cols[:, CP["lng"] + kc:CP["lng"] + kc + 1], None, op0=ALU.mult),
            reads=[wsb["o"], cols], writes=[wsb["o"]])

    if stop_after == "b2":
        P.wait_all(); P.play(); return nc
    x0 = ga("x0", [128, 8 * TA], F32, off=G0)
    x1 = ga("x1", [128, 8 * TA], F32, off=G0 + 8192)
    xh_tm = ga("xh_tm", [128, 1024], F32, off=G0 + 16384)
    xh_fm = ga("xh_fm", [128, 8 * TA], BF16, off=G0 + 20480)

    def a2b_xstage(ti):
        xb = (x0, x1)[ti % 2]
        t0 = ti * TA
        xbv = xb[:].rearrange("p (c t) -> p c t", c=8)
        P.dma("sp", xbv, x_fm_v[:, :, t0:t0 + TA], xb, name=f"xa{ti % 2}")
        for gl in range(TA // 128):
            gg = ti * (TA // 128) + gl
            P.emit("dve", lambda e, gg=gg: e.tensor_scalar(
                xh_tm[:], ret_v[:, RSLOT[gg], :], colg("mu2", gg), colg("rstd2", gg),
                op0=ALU.subtract, op1=ALU.mult),
                reads=[retr[RSLOT[gg]], cols], writes=[xh_tm])
            ps = PS[2 + gl % 2]
            P.emit("pe", [
                (lambda e, c=c, ps=ps: e.matmul(
                    ps[:, c * 128:(c + 1) * 128], lhsT=xh_tm[:, c * 128:(c + 1) * 128],
                    rhs=ident[:], is_transpose=True, start=True, stop=True))
                for c in range(8)],
                reads=[xh_tm, ident], writes=[ps])
            P.emit("act", lambda e, gl=gl, ps=ps: e.activation(
                xh_fm[:].rearrange("p (c t) -> p c t", c=8)[:, :, gl * 128:(gl + 1) * 128],
                ps[:].rearrange("p (c t) -> p c t", c=8), AF.Identity),
                reads=[ps], writes=[xh_fm])

    def a2b_mms(ti):
        for half in range(2):
            ps = PS[half]
            fns = []
            for nco in range(4):
                nch = half * 4 + nco
                for kc in range(8):
                    fns.append(lambda e, nch=nch, nco=nco, kc=kc, ps=ps:
                               e.matmul(ps[:, nco * TA:(nco + 1) * TA],
                                        lhsT=wsb["o"][:, kc * 1024 + nch * 128:kc * 1024 + nch * 128 + 128],
                                        rhs=xh_fm[:, kc * TA:(kc + 1) * TA],
                                        start=(kc == 0), stop=False))
                pb, c = RP["bwo"]
                fns.append(lambda e, nch=nch, nco=nco, ps=ps, pb=pb, c=c:
                           e.matmul(ps[:, nco * TA:(nco + 1) * TA],
                                    lhsT=rows[pb:pb + 1, c + nch * 128:c + nch * 128 + 128],
                                    rhs=ones[pb:pb + 1, 0:TA],
                                    start=False, stop=True))
            P.emit("pe", fns, reads=[wsb["o"], xh_fm, rows, ones], writes=[ps])

    def a2b_acc(ti):
        xb = (x0, x1)[ti % 2]
        t0 = ti * TA
        for half in range(2):
            ps = PS[half]
            P.emit("dve", lambda e, half=half, xb=xb, t0=t0: e.tensor_tensor(
                xb[:, half * 4 * TA:(half + 1) * 4 * TA],
                xb[:, half * 4 * TA:(half + 1) * 4 * TA],
                xn_sb[:].rearrange("p (c t) -> p c t", c=8)[:, half * 4:half * 4 + 4, t0:t0 + TA],
                op=ALU.add), reads=[xb, xn_sb], writes=[xb])
            P.emit("dve", lambda e, half=half, ps=ps, xb=xb: e.tensor_tensor(
                xb[:, half * 4 * TA:(half + 1) * 4 * TA],
                xb[:, half * 4 * TA:(half + 1) * 4 * TA],
                ps[:], op=ALU.add), reads=[xb, ps], writes=[xb])
        P.dma("act", acc_v[:, :, t0:t0 + TA],
              xb[:].rearrange("p (c t) -> p c t", c=8), acc_h,
              src_bufs=[xb], name="accst")


    a2b_xstage(0)
    for ti in range(NTA):
        a2b_mms(ti)
        if ti + 1 < NTA:
            a2b_xstage(ti + 1)
        a2b_acc(ti)

    h1 = sb("h1_sb", [128, 32 * TF], BF16, 32768)
    h2 = sb("h2_sb", [128, 32 * TF], BF16, 65536)
    kst = [sb("kst0", [128, 8192], BF16, 98304),
           sb("kst1", [128, 8192], BF16, 114688),
           sb("kst2", [128, 8192], BF16, G0)]
    accin = sb("accin", [128, 8 * TF], F32, 131072)
    outb = sb("outb", [128, 8 * TF], F32, 147456)

    def srow_build(ps, src_h, hi_nm, lo_nm, f_nm):
        P.emit("pe", [
            (lambda e, kc=kc, ps=ps: e.matmul(
                ps[0:1, 0:TF], lhsT=ones[:, 0:1],
                rhs=src_h[:, kc * TF:(kc + 1) * TF],
                start=(kc == 0), stop=(kc == 31)))
            for kc in range(32)], reads=[ones, src_h], writes=[ps])
        P.emit("dve", lambda e, ps=ps: e.tensor_copy(rowf(f_nm), ps[0:1, 0:TF]),
               reads=[ps], writes=[rowsf])
        P.emit("dve", lambda e: e.tensor_copy(row(hi_nm, TF), rowf(f_nm)),
               reads=[rowsf], writes=[rows])
        P.emit("dve", lambda e: e.tensor_tensor(rowf("scr"), rowf(f_nm), row(hi_nm, TF), op=ALU.subtract),
               reads=[rowsf, rows], writes=[rowsf])
        P.emit("dve", lambda e: e.tensor_copy(row(lo_nm, TF), rowf("scr")),
               reads=[rowsf], writes=[rows])

    for tj in range(NTF):
        t0 = tj * TF
        abv = accin[:].rearrange("p (c t) -> p c t", c=8)
        P.dma("pool", abv, acc_v[:, :, t0:t0 + TF], accin, src_bufs=[acc_h],
              name="accin")
        for g in range(16):
            kb = kst[g % 3]
            P.dma("sp", kb[:, 0:2048].rearrange("p (k n) -> p k n", k=8),
                  k0dbf[g].rearrange("k p n -> p k n"), kb,
                  src_bufs=[k0dbf_h], name=f"k0s{g % 2}")
            ps = PS[g % 4]
            fns = []
            for nco in range(2):
                for kc in range(8):
                    fns.append(lambda e, nco=nco, kc=kc, ps=ps, kb=kb, t0=t0:
                               e.matmul(ps[:, nco * TF:(nco + 1) * TF],
                                        lhsT=kb[:, kc * 256 + nco * 128:kc * 256 + nco * 128 + 128],
                                        rhs=xn_v[:, kc, t0:t0 + TF],
                                        start=(kc == 0), stop=(kc == 7)))
            P.emit("pe", fns, reads=[kb, xn_sb], writes=[ps])
            P.emit("act", lambda e, g=g, ps=ps: e.activation(
                h1[:, g * 1024:(g + 1) * 1024], ps[:], AF.Relu, scale=sgl("two_s0")),
                reads=[ps, cols], writes=[h1])
        srow_build(PS[0], h1, "s1hi", "s1lo", "s1")
        for g in range(16):
            kb = kst[g % 3]
            P.dma("sp", kb[:].rearrange("p (k n) -> p k n", k=32),
                  k1bf[g].rearrange("k p n -> p k n"), kb,
                  src_bufs=[k1bf_h], name=f"k1s{g % 3}")
            ps = PS[g % 4]
            fns = []
            for nco in range(2):
                for kc in range(32):
                    fns.append(lambda e, nco=nco, kc=kc, ps=ps, kb=kb:
                               e.matmul(ps[:, nco * TF:(nco + 1) * TF],
                                        lhsT=kb[:, kc * 256 + nco * 128:kc * 256 + nco * 128 + 128],
                                        rhs=h1[:, kc * TF:(kc + 1) * TF],
                                        start=(kc == 0), stop=False))
            P.emit("pe", fns, reads=[kb, h1], writes=[ps])
            fns = []
            for nco in range(2):
                fns.append(lambda e, nco=nco, ps=ps: e.matmul(
                    ps[:, nco * TF:(nco + 1) * TF], lhsT=row("mh0", 128),
                    rhs=row("s1hi", TF), start=False, stop=False))
                fns.append(lambda e, nco=nco, ps=ps: e.matmul(
                    ps[:, nco * TF:(nco + 1) * TF], lhsT=row("mh0", 128),
                    rhs=row("s1lo", TF), start=False, stop=True))
            P.emit("pe", fns, reads=[rows], writes=[ps])
            P.emit("act", lambda e, g=g, ps=ps: e.activation(
                h2[:, g * 1024:(g + 1) * 1024], ps[:], AF.Relu, scale=sgl("four_s1")),
                reads=[ps, cols], writes=[h2])
        srow_build(PS[0], h2, "s2hi", "s2lo", "s2")
        for g in range(4):
            kb = kst[1 + g % 2]
            P.dma("sp", kb[:].rearrange("p (k n) -> p k n", k=32),
                  k2bf[g].rearrange("k p n -> p k n"), kb,
                  src_bufs=[k2bf_h], name=f"k2s{g % 3}")
            ps = PS[g % 4]
            fns = []
            for nco in range(2):
                for kc in range(32):
                    fns.append(lambda e, nco=nco, kc=kc, ps=ps, kb=kb:
                               e.matmul(ps[:, nco * TF:(nco + 1) * TF],
                                        lhsT=kb[:, kc * 256 + nco * 128:kc * 256 + nco * 128 + 128],
                                        rhs=h2[:, kc * TF:(kc + 1) * TF],
                                        start=(kc == 0), stop=False))
            P.emit("pe", fns, reads=[kb, h2], writes=[ps])
            fns = []
            for nco in range(2):
                fns.append(lambda e, nco=nco, ps=ps: e.matmul(
                    ps[:, nco * TF:(nco + 1) * TF], lhsT=row("mh0", 128),
                    rhs=row("s2hi", TF), start=False, stop=False))
                fns.append(lambda e, nco=nco, ps=ps: e.matmul(
                    ps[:, nco * TF:(nco + 1) * TF], lhsT=row("mh0", 128),
                    rhs=row("s2lo", TF), start=False, stop=True))
            P.emit("pe", fns, reads=[rows], writes=[ps])
            P.emit("act", lambda e, g=g, ps=ps: e.activation(
                outb[:, g * 1024:(g + 1) * 1024], ps[:], AF.Identity,
                scale=sgl("four_s2")), reads=[ps, cols], writes=[outb])
        P.emit("dve", lambda e: e.tensor_tensor(outb[:], outb[:], accin[:], op=ALU.add),
               reads=[outb, accin], writes=[outb])
        P.dma("act", out_v[:, :, t0:t0 + TF],
              outb[:].rearrange("p (c t) -> p c t", c=8), None,
              src_bufs=[outb], name="outst")
        if debug and tj == 0:
            P.dma("act", dbg["h1"], h1[:], None, src_bufs=[h1], name="dbgh1")
            P.dma("act", dbg["h2"], h2[:], None, src_bufs=[h2], name="dbgh2")

    P.wait_all()
    P.play()
    return nc


def _prep_shared(inputs):
    f = np.ascontiguousarray
    d = {}
    for k, nm in (("k", "Wk"), ("v", "Wv"), ("q", "Wq"), ("kp", "Wkp"),
                  ("qp", "Wqp"), ("o", "Wo")):
        w = np.asarray(inputs[f"L0_attn_{nm}"], dtype=np.float32)
        d[f"w_{k}"] = f(w.reshape(8, 128, 8, 128).transpose(1, 0, 2, 3)
                        .reshape(128, 8192))
    for k, nm in (("v", "bv"), ("kp", "bkp"), ("qp", "bqp"), ("o", "bo")):
        d[f"b_{k}"] = f(np.asarray(inputs[f"L0_attn_{nm}"],
                                   dtype=np.float32).reshape(1, D))
    for nm, key in (("Wk", "wkT"), ("Wq", "wqT")):
        w = np.asarray(inputs[f"L0_attn_{nm}"], dtype=np.float32).T
        d[key] = f(w.reshape(8, 128, 8, 128).transpose(1, 0, 2, 3)
                   .reshape(128, 8192))
    d["bkc"] = f(np.asarray(inputs["L0_attn_bk"], dtype=np.float32)
                 .reshape(8, 128).T)
    d["bqc"] = f(np.asarray(inputs["L0_attn_bq"], dtype=np.float32)
                 .reshape(8, 128).T)
    d["ps_r"] = f(np.asarray(inputs["L0_attn_phase_scale"],
                             dtype=np.float32).reshape(1, D))
    d["lng_c"] = f(np.asarray(inputs["L0_attn_ln_g"],
                              dtype=np.float32).reshape(8, 128).T)
    d["lnb_c"] = f(np.asarray(inputs["L0_attn_ln_b"],
                              dtype=np.float32).reshape(8, 128).T)
    k0 = np.asarray(inputs["L0_ffn_k0"]).view(np.uint8)
    d["k0t"] = f(k0[:D].reshape(8, 128, DF).transpose(1, 0, 2).reshape(128, 8 * DF))
    d["k0b"] = f(k0[D:].reshape(8, 128, DF).transpose(1, 0, 2).reshape(128, 8 * DF))
    k1 = np.asarray(inputs["L0_ffn_k1"]).view(np.uint8)
    d["k1t"] = f(k1.reshape(32, 128, 16, 256).transpose(2, 0, 1, 3))
    k2 = np.asarray(inputs["L0_ffn_k2"]).view(np.uint8)
    d["k2t"] = f(k2.reshape(32, 128, 4, 256).transpose(2, 0, 1, 3))
    for i in (0, 1, 2):
        d[f"s{i}_c"] = np.full((128, 1), np.asarray(inputs[f"L0_ffn_s{i}"],
                                                    dtype=np.float32))
    import ml_dtypes
    p = np.arange(128)[:, None]; q = np.arange(128)[None, :]
    d["u128_c"] = (((p <= q) & ((p < 64) == (q < 64)))
                   .astype(ml_dtypes.bfloat16))
    d["ident_c"] = np.eye(128, dtype=np.float32)
    return d


def kernel(**inputs):
    nc = build(debug=False)
    shared = _prep_shared(inputs)
    x = np.asarray(inputs["L0_attn_x"], dtype=np.float32).reshape(S, D)
    in_maps = []
    for c in range(NCORE):
        xs = x[c * TOK:(c + 1) * TOK]
        m = dict(shared)
        m["x_tm"] = np.ascontiguousarray(xs)
        m["x_fm"] = np.ascontiguousarray(xs.T)
        in_maps.append(m)
    res = run_bass_kernel_spmd(nc, in_maps, core_ids=list(range(NCORE)))
    out = np.concatenate([r["out_fm"].T for r in res.results], axis=0)
    return out.reshape(1, S, D).astype(np.float32)
